# version 10
# baseline (speedup 1.0000x reference)
from contextlib import ExitStack

import numpy as np
import ml_dtypes
import concourse.bass as bass
import concourse.mybir as mybir
from concourse.bass_utils import run_bass_kernel_spmd

F32 = mybir.dt.float32
BF16 = mybir.dt.bfloat16
AF = mybir.ActivationFunctionType
ALU = mybir.AluOpType
BF = ml_dtypes.bfloat16

NCORES = 8
T_OWN = 2048
OVL = 128
T_IN = T_OWN + OVL
TG = 512
D = 1024
NHEAD = 16
HK = 64
CH = 128
NCHUNK = T_OWN // CH
EPS = 1e-5
HG_SLOTS = 63

SAFE_CHAIN = True
L2_CHUNKS = NCHUNK
L2_FFN = True
L2_STAGE = 9


class Sem:
    __slots__ = ("h", "n")

    def __init__(self, h):
        self.h = h
        self.n = 0


class Plan:
    ENGINES = ("sync", "tensor", "vector", "scalar", "gpsimd")

    def __init__(self, nc):
        self.nc = nc
        self.steps = {e: [] for e in self.ENGINES}

    def emit(self, eng, fn, waits=(), inc=None):
        ws = tuple((s, v) for (s, v) in waits if v > 0)
        incs = ()
        if inc is not None:
            s, k = inc
            s.n += k
            incs = ((s, k),)
        self.steps[eng].append((fn, ws, incs))
        return incs[0][0].n if incs else None

    def wait(self, eng, sem, val=None):
        v = sem.n if val is None else val
        if v > 0:
            self.steps[eng].append((None, ((sem, v),), ()))

    def build(self, block):
        for eng in self.ENGINES:
            steps = self.steps[eng]
            if not steps:
                continue

            def body(e, steps=steps):
                hw = {}
                for fn, ws, incs in steps:
                    for (s, v) in ws:
                        if v > hw.get(s.h.num, 0):
                            e.wait_ge(s.h, v)
                            hw[s.h.num] = v
                    if fn is None:
                        continue
                    inst = fn(e)
                    for (s, k) in incs:
                        inst = inst.then_inc(s.h, k)

            getattr(block, eng)(body)


def emit_ln(P, sem, pb, st, xf_ap, out_ap, n, scr, rows, consts,
            pre_dv=0, pre_ac=0):
    s_dv, s_ac, s_pe = sem["dv"], sem["ac"], sem["pe"]
    onesb, onesf, epst = consts["ones_bf"], consts["ones_f32"], consts["epst"]
    xb_t, sq_t, d_t = scr["xb"], scr["sq"], scr["d"]
    st0, st1, bc0, bc1 = pb["st0"], pb["st1"], pb["bc0"], pb["bc1"]
    mrow, vrow = rows["m"], rows["v"]

    if pre_dv:
        P.wait("vector", s_dv, pre_dv)
        P.wait("scalar", s_dv, pre_dv)
    if pre_ac:
        P.wait("vector", s_ac, pre_ac)
        P.wait("scalar", s_ac, pre_ac)

    mm_done = []
    for c in range(8):
        xa = xf_ap(c)
        xb = xb_t[:, 0:n]
        sq = sq_t[:, 0:n]
        w_pe = [(s_pe, mm_done[c - 1])] if c >= 1 else []
        P.emit("vector", lambda e, xb=xb, xa=xa: e.tensor_copy(out=xb, in_=xa),
               waits=w_pe, inc=(s_dv, 1))
        dvn = s_dv.n
        P.emit("scalar", lambda e, sq=sq, xa=xa: e.activation(out=sq, in_=xa, func=AF.Square),
               waits=w_pe, inc=(s_ac, 1))
        acn = s_ac.n
        w0 = [(s_dv, dvn)]
        if c == 0:
            w0.append((s_ac, st["stats_free_ac"]))
            w0.append((s_dv, st["stats_free_dv"]))
        P.emit("tensor", lambda e, xb=xb, c=c, n=n: e.matmul(
            st0[:, 0:n], onesb[:], xb, start=(c == 0), stop=(c == 7)), waits=w0)
        mmn = P.emit("tensor", lambda e, sq=sq, c=c, n=n: e.matmul(
            st1[:, 0:n], onesb[:], sq, start=(c == 0), stop=(c == 7)),
            waits=[(s_ac, acn)], inc=(s_pe, 1))
        mm_done.append(mmn)

    pe_n = s_pe.n
    P.emit("scalar", lambda e, n=n: e.activation(out=mrow[:, 0:n], in_=st0[:, 0:n],
                                                 func=AF.Copy, scale=1.0 / D),
           waits=[(s_pe, pe_n)], inc=(s_ac, 1))
    P.emit("scalar", lambda e, n=n: e.activation(out=vrow[:, 0:n], in_=st1[:, 0:n],
                                                 func=AF.Copy, scale=1.0 / D),
           inc=(s_ac, 1))
    acn = s_ac.n
    P.emit("scalar", lambda e, n=n: e.activation(out=st0[:, 0:n], in_=mrow[:, 0:n],
                                                 func=AF.Square),
           waits=([(s_ac, acn)] if SAFE_CHAIN else ()), inc=(s_ac, 1))
    st["stats_free_ac"] = s_ac.n
    acn = s_ac.n
    P.emit("vector", lambda e, n=n: e.tensor_sub(out=vrow[:, 0:n], in0=vrow[:, 0:n],
                                                 in1=st0[:, 0:n]),
           waits=[(s_ac, acn)], inc=(s_dv, 1))
    st["stats_free_dv"] = s_dv.n
    dvn = s_dv.n
    P.emit("scalar", lambda e, n=n: e.activation(out=vrow[:, 0:n], in_=vrow[:, 0:n],
                                                 func=AF.Ln, bias=epst[0:1, :]),
           waits=[(s_dv, dvn)], inc=(s_ac, 1))
    acn = s_ac.n
    P.emit("scalar", lambda e, n=n: e.activation(out=vrow[:, 0:n], in_=vrow[:, 0:n],
                                                 func=AF.Exp, scale=-0.5),
           waits=([(s_ac, acn)] if SAFE_CHAIN else ()), inc=(s_ac, 1))
    acn = s_ac.n
    P.emit("tensor", lambda e, n=n: e.matmul(bc0[:, 0:n], onesf[:], mrow[:, 0:n],
                                             start=True, stop=True),
           waits=[(s_ac, acn), (s_dv, st["bc_free_dv"])])
    P.emit("tensor", lambda e, n=n: e.matmul(bc1[:, 0:n], onesf[:], vrow[:, 0:n],
                                             start=True, stop=True), inc=(s_pe, 1))
    pe_n = s_pe.n
    mul_done = []
    for c in range(8):
        xa = xf_ap(c)
        dd = d_t[:, 0:n]
        w = [(s_pe, pe_n)]
        if c >= 1:
            w.append((s_dv, mul_done[c - 1]))
        P.emit("vector", lambda e, xa=xa, dd=dd, n=n: e.tensor_sub(out=dd, in0=xa,
                                                                   in1=bc0[:, 0:n]),
               waits=w, inc=(s_dv, 1))
        dvn = s_dv.n
        mn = P.emit("vector", lambda e, dd=dd, c=c, n=n: e.tensor_mul(
            out=out_ap(c), in0=dd, in1=bc1[:, 0:n]),
            waits=([(s_dv, dvn)] if SAFE_CHAIN else ()), inc=(s_dv, 1))
        mul_done.append(mn)
    st["bc_free_dv"] = s_dv.n
    return s_dv.n


def emit_ln_single(P, sem, pb, st, xf_ap, out_ap, scr, rows, consts, pre_dv=0):
    s_dv, s_ac, s_pe = sem["dv"], sem["ac"], sem["pe"]
    onesb, onesf, epst = consts["ones_bf"], consts["ones_f32"], consts["epst"]
    st0, st1, bc0, bc1 = pb["st0"], pb["st1"], pb["bc0"], pb["bc1"]
    xb_t, sq_t = scr["xb"], scr["sq"]
    mrow, vrow = rows["m"], rows["v"]
    if pre_dv:
        P.wait("vector", s_dv, pre_dv)
        P.wait("scalar", s_dv, pre_dv)
    for c in range(8):
        xa = xf_ap(c)
        P.emit("vector", lambda e, c=c, xa=xa: e.tensor_copy(out=xb_t[:, c:c + 1], in_=xa),
               inc=(s_dv, 1))
        P.emit("scalar", lambda e, c=c, xa=xa: e.activation(out=sq_t[:, c:c + 1], in_=xa,
                                                            func=AF.Square), inc=(s_ac, 1))
    dvn, acn = s_dv.n, s_ac.n
    for c in range(8):
        w0 = [(s_dv, dvn)]
        if c == 0:
            w0.append((s_ac, st["stats_free_ac"]))
            w0.append((s_dv, st["stats_free_dv"]))
        P.emit("tensor", lambda e, c=c: e.matmul(st0[:, 0:1], onesb[:], xb_t[:, c:c + 1],
                                                 start=(c == 0), stop=(c == 7)), waits=w0)
        P.emit("tensor", lambda e, c=c: e.matmul(st1[:, 0:1], onesb[:], sq_t[:, c:c + 1],
                                                 start=(c == 0), stop=(c == 7)),
               waits=[(s_ac, acn)], inc=((s_pe, 1) if c == 7 else None))
    pe_n = s_pe.n
    P.emit("scalar", lambda e: e.activation(out=mrow[:, 0:1], in_=st0[:, 0:1], func=AF.Copy,
                                            scale=1.0 / D), waits=[(s_pe, pe_n)], inc=(s_ac, 1))
    P.emit("scalar", lambda e: e.activation(out=vrow[:, 0:1], in_=st1[:, 0:1], func=AF.Copy,
                                            scale=1.0 / D), inc=(s_ac, 1))
    st["stats_free_ac"] = s_ac.n
    acn = s_ac.n
    P.emit("scalar", lambda e: e.activation(out=st0[:, 0:1], in_=mrow[:, 0:1], func=AF.Square),
           waits=([(s_ac, acn)] if SAFE_CHAIN else ()), inc=(s_ac, 1))
    acn = s_ac.n
    P.emit("vector", lambda e: e.tensor_sub(out=vrow[:, 0:1], in0=vrow[:, 0:1],
                                            in1=st0[:, 0:1]),
           waits=[(s_ac, acn)], inc=(s_dv, 1))
    dvn = s_dv.n
    P.emit("scalar", lambda e: e.activation(out=vrow[:, 0:1], in_=vrow[:, 0:1], func=AF.Ln,
                                            bias=epst[0:1, :]),
           waits=[(s_dv, dvn)], inc=(s_ac, 1))
    acn = s_ac.n
    P.emit("scalar", lambda e: e.activation(out=vrow[:, 0:1], in_=vrow[:, 0:1], func=AF.Exp,
                                            scale=-0.5),
           waits=([(s_ac, acn)] if SAFE_CHAIN else ()), inc=(s_ac, 1))
    acn = s_ac.n
    P.emit("tensor", lambda e: e.matmul(bc0[:, 0:1], onesf[:], mrow[:, 0:1],
                                        start=True, stop=True),
           waits=[(s_ac, acn), (s_dv, st["bc_free_dv"])])
    P.emit("tensor", lambda e: e.matmul(bc1[:, 0:1], onesf[:], vrow[:, 0:1],
                                        start=True, stop=True), inc=(s_pe, 1))
    pe_n = s_pe.n
    dd = scr["d"][:, 0:1]
    for c in range(8):
        xa = xf_ap(c)
        P.emit("vector", lambda e, xa=xa, dd=dd: e.tensor_sub(out=dd, in0=xa, in1=bc0[:, 0:1]),
               waits=[(s_pe, pe_n), (s_dv, s_dv.n)], inc=(s_dv, 1))
        dvn = s_dv.n
        P.emit("vector", lambda e, c=c, dd=dd: e.tensor_mul(out=out_ap(c), in0=dd,
                                                            in1=bc1[:, 0:1]),
               waits=[(s_dv, dvn)], inc=(s_dv, 1))
    st["bc_free_dv"] = s_dv.n
    return s_dv.n


def emit_ffn(P, sem, pb, st, bufs, tg_list, ap_x_load, post_tg, consts):
    s_ld, s_cp, s_pe = sem["ld"], sem["cp"], sem["pe"]
    s_dv, s_ac, s_kp = sem["dv"], sem["ac"], sem["kp"]
    s_xf = sem["xfree"]
    xf, xn, hg, cpw, acc, C8 = (bufs["xf"], bufs["xn"], bufs["hg"], bufs["cpw"],
                                bufs["acc"], bufs["C8"])
    cpst = bufs["cpstream"]
    pbr = [pb["rail0"], pb["rail1"]]
    pbc = [pb["cp0"], pb["cp1"]]

    kp_base = s_kp.n
    ntg = len(tg_list)

    x_loaded = {}

    def emit_x_load(t_idx):
        if t_idx >= ntg:
            return
        tg_i, col0, n = tg_list[t_idx]
        b = t_idx % 2
        for c in range(8):
            dst = xf[:, (b * 8 + c) * TG:(b * 8 + c) * TG + n]
            w = ([(s_xf, st["xf_base"])] if (t_idx < 2 and st["xf_base"]) else [])
            P.emit("sync", lambda e, dst=dst, c=c, col0=col0, n=n: e.dma_start(
                out=dst, in_=ap_x_load(c, col0, n)), waits=w, inc=(s_ld, 16))
        x_loaded[t_idx] = s_ld.n

    emit_x_load(0)
    emit_x_load(1)

    slab_loaded = []
    for gstep in range(32 * ntg):
        g, o = (gstep % 32) // 8, gstep % 8
        dst = cpw[:, (gstep % 2) * 4096:(gstep % 2) * 4096 + 4096]
        src = cpst[:, (g * 8 + o) * 4096:(g * 8 + o) * 4096 + 4096]
        w = [(s_kp, kp_base + gstep - 1)] if gstep >= 2 else []
        P.emit("gpsimd", lambda e, dst=dst, src=src: e.dma_start(out=dst, in_=src),
               waits=w, inc=(s_cp, 16))
        slab_loaded.append(s_cp.n)

    for t_idx, (tg_i, col0, n) in enumerate(tg_list):
        b = t_idx % 2

        def xap(c, b=b, n=n):
            return xf[:, (b * 8 + c) * TG:(b * 8 + c) * TG + n]

        ldw = x_loaded[t_idx]
        P.wait("vector", s_ld, ldw)
        P.wait("scalar", s_ld, ldw)
        ln_done = emit_ln(P, sem, pb, st, xap,
                          lambda c, n=n: xn[:, c * TG:c * TG + n],
                          n, bufs["scr"], bufs["rows"], consts)
        if t_idx == ntg - 1 and "dbg_ln" in bufs:
            dbg = bufs["dbg_ln"]
            rows_ = bufs["rows"]
            P.emit("sync", lambda e: e.dma_start(out=dbg[0:1, 0:TG], in_=rows_["m"][:, 0:TG]),
                   waits=[(s_dv, ln_done)], inc=(sem["st"], 16))
            P.emit("sync", lambda e: e.dma_start(out=dbg[1:2, 0:TG], in_=rows_["v"][:, 0:TG]),
                   inc=(sem["st"], 16))
            P.emit("sync", lambda e, b=b: e.dma_start(out=dbg[2:130, 0:TG][0:128, :],
                                                      in_=xf[:, b * 8 * TG:b * 8 * TG + TG]),
                   inc=(sem["st"], 16))
            P.emit("gpsimd", lambda e: e.dma_start(out=dbg[130:258, 0:TG][0:128, :],
                                                   in_=xn[:, 0:TG]),
                   waits=[(s_dv, ln_done)], inc=(sem["st"], 16))
            P.emit("sync", lambda e: e.dma_start(out=dbg[0:1, 0:128],
                                                 in_=consts["ones_f32"][:]),
                   inc=(sem["st"], 16))

        rail_j = [0]
        tg_slot0 = st["slot_ctr"]

        def emit_rails(kcnt, t_idx=t_idx, tg_slot0=tg_slot0, n=n, ln_done=ln_done):
            for _ in range(kcnt):
                j = rail_j[0]
                if j >= 128:
                    return
                rail_j[0] += 1
                h, rp = j // 32, j % 32
                c, s = rp // 4, rp % 4
                slot = (tg_slot0 + j) % HG_SLOTS
                bank = pbr[j % 2]
                lhs = C8[:, (h * 8 + c) * TG:(h * 8 + c) * TG + TG]
                lhsT = lhs[32 * s:32 * s + 32, 128 * s:128 * s + 128]
                rhs = xn[32 * s:32 * s + 32, c * TG:c * TG + n]
                w = [(s_dv, ln_done), (s_ac, st["rail_gelu"][j % 2])]
                mmn = P.emit("tensor", lambda e, bank=bank, lhsT=lhsT, rhs=rhs, s=s, n=n:
                             e.matmul(bank[:, 0:n], lhsT, rhs, start=True, stop=True,
                                      tile_position=(32 * s, 0)), waits=w, inc=(s_pe, 1))
                glob_slot_idx = tg_slot0 + j
                wg = [(s_pe, mmn)]
                prev_reader = st["slot_reader_step"][slot]
                if prev_reader > 0:
                    wg.append((s_kp, prev_reader))
                dst = hg[:, slot * TG:slot * TG + n]
                gn = P.emit("scalar", lambda e, dst=dst, bank=bank, n=n: e.activation(
                    out=dst, in_=bank[:, 0:n], func=AF.Gelu), waits=wg, inc=(s_ac, 1))
                st["gelu_done"][slot] = gn
                st["rail_gelu"][j % 2] = gn
            return

        emit_rails(32)
        st["slot_ctr"] += 128

        for g in range(4):
            for o in range(8):
                gstep = t_idx * 32 + g * 8 + o
                bank = pbc[gstep % 2]
                slab = cpw[:, (gstep % 2) * 4096:(gstep % 2) * 4096 + 4096]
                w = [(s_cp, slab_loaded[gstep]), (s_dv, st["cp_bank_dv"][gstep % 2])]
                gmax = 0
                for jj in range(32):
                    slot = (tg_slot0 + g * 32 + jj) % HG_SLOTS
                    gmax = max(gmax, st["gelu_done"][slot])
                w.append((s_ac, gmax))
                for jj in range(32):
                    slot = (tg_slot0 + g * 32 + jj) % HG_SLOTS
                    lhsT = slab[:, jj * 128:jj * 128 + 128]
                    rhs = hg[:, slot * TG:slot * TG + n]
                    P.emit("tensor", lambda e, bank=bank, lhsT=lhsT, rhs=rhs, jj=jj, n=n:
                           e.matmul(bank[:, 0:n], lhsT, rhs, start=(jj == 0),
                                    stop=(jj == 31)),
                           waits=(w if jj == 0 else ()),
                           inc=((s_kp, 1) if jj == 31 else None))
                for jj in range(32):
                    slot = (tg_slot0 + g * 32 + jj) % HG_SLOTS
                    st["slot_reader_step"][slot] = s_kp.n
                emit_rails(4)
                accsl = acc[:, o * TG:o * TG + n]
                if g == 0:
                    fn = lambda e, accsl=accsl, bank=bank, n=n: e.tensor_copy(
                        out=accsl, in_=bank[:, 0:n])
                else:
                    fn = lambda e, accsl=accsl, bank=bank, n=n: e.tensor_add(
                        out=accsl, in0=accsl, in1=bank[:, 0:n])
                wdv = [(s_kp, s_kp.n)]
                if SAFE_CHAIN:
                    wdv.append((s_dv, s_dv.n))
                dn = P.emit("vector", fn, waits=wdv, inc=(s_dv, 1))
                st["cp_bank_dv"][gstep % 2] = dn

        res_done = []
        for c in range(8):
            xa = xap(c)
            accsl = acc[:, c * TG:c * TG + n]
            dn = P.emit("vector", lambda e, xa=xa, accsl=accsl: e.scalar_tensor_tensor(
                out=xa, in0=accsl, scalar=0.5, in1=xa, op0=ALU.mult, op1=ALU.add),
                waits=[(s_dv, s_dv.n)], inc=(s_dv, 1))
            res_done.append(dn)

        post_tg(t_idx, b, n, res_done)
        P.emit("vector", lambda e: e.engine_nop(),
               waits=[(s_dv, st["post_dv"]), (sem["st"], st["post_st"])],
               inc=(s_xf, 1))
        xfn = s_xf.n
        if t_idx + 2 < ntg:
            P.wait("sync", s_xf, xfn)
            emit_x_load(t_idx + 2)

    st["xf_base"] += ntg


def build_l1():
    nc = bass.Bass("TRN2", target_bir_lowering=False)

    x0T = nc.dram_tensor("x0T", [D, T_IN], F32, kind="ExternalInput").ap()
    cpstream_d = nc.dram_tensor("cpstream", [128, 131072], F32, kind="ExternalInput").ap()
    rmbd = nc.dram_tensor("rmbd", [32, 128, 128], F32, kind="ExternalInput").ap()
    ubd = nc.dram_tensor("ubd", [32, 128, 512], F32, kind="ExternalInput").ap()
    wrt = nc.dram_tensor("wrt", [D, D], F32, kind="ExternalInput").ap()
    wkt = nc.dram_tensor("wkt", [D, D], F32, kind="ExternalInput").ap()
    wvt = nc.dram_tensor("wvt", [D, D], F32, kind="ExternalInput").ap()
    wgt = nc.dram_tensor("wgt", [D, D], F32, kind="ExternalInput").ap()
    tmix = nc.dram_tensor("tmix", [128, 32], F32, kind="ExternalInput").ap()
    td = nc.dram_tensor("td", [1, 16], F32, kind="ExternalInput").ap()

    x1o = nc.dram_tensor("x1o", [D, T_OWN], F32, kind="ExternalOutput").ap()
    Ro = nc.dram_tensor("Ro", [D, T_OWN], BF16, kind="ExternalOutput").ap()
    Ko = nc.dram_tensor("Ko", [D, T_OWN], BF16, kind="ExternalOutput").ap()
    Go = nc.dram_tensor("Go", [T_OWN, D], BF16, kind="ExternalOutput").ap()
    Vo = nc.dram_tensor("Vo", [T_OWN, D], BF16, kind="ExternalOutput").ap()
    Kwo = nc.dram_tensor("Kwo", [T_OWN, D], BF16, kind="ExternalOutput").ap()
    Bo = nc.dram_tensor("Bo", [8, 128, 64], F32, kind="ExternalOutput").ap()
    dbgo = nc.dram_tensor("dbgo", [128, 4 * TG], F32, kind="ExternalOutput").ap()
    dbgln = nc.dram_tensor("dbgln", [258, TG], F32, kind="ExternalOutput").ap()

    ones_bf_h = nc.inline_tensor(np.ones((128, 1), dtype=BF), "c_ones_bf")
    ones_f32_h = nc.inline_tensor(np.ones((1, 128), dtype=np.float32), "c_ones_f32")
    epst_h = nc.inline_tensor(np.full((128, 1), EPS, dtype=np.float32), "c_eps")
    riota_h = nc.inline_tensor((127.0 - np.arange(128, dtype=np.float32)).reshape(128, 1),
                               "c_riota")

    with ExitStack() as ctx:
        ec = ctx.enter_context
        ones_bf = ec(nc.sbuf_tensor("ones_bf", [128, 1], BF16))
        ones_f32 = ec(nc.sbuf_tensor("ones_f32", [1, 128], F32))
        epst = ec(nc.sbuf_tensor("epst", [128, 1], F32))
        riota = ec(nc.sbuf_tensor("riota", [128, 1], F32))
        H2 = ec(nc.sbuf_tensor("H2", [128, 8 * 2050], BF16))
        tmixs = ec(nc.sbuf_tensor("tmixs", [128, 32], F32))
        xb_s = ec(nc.sbuf_tensor("xb_s", [128, TG], BF16))
        sq_s = ec(nc.sbuf_tensor("sq_s", [128, TG], BF16))
        d_s = ec(nc.sbuf_tensor("d_s", [128, TG], F32))
        mrow = ec(nc.sbuf_tensor("mrow", [1, TG], F32))
        vrow = ec(nc.sbuf_tensor("vrow", [1, TG], F32))
        st0 = ec(nc.psum_tensor("st0", [1, TG], F32))
        st1 = ec(nc.psum_tensor("st1", [1, TG], F32))
        bc0 = ec(nc.psum_tensor("bc0", [128, TG], F32))
        bc1 = ec(nc.psum_tensor("bc1", [128, TG], F32))
        h_ld = ec(nc.semaphore("s_ld"))
        h_cp = ec(nc.semaphore("s_cp"))
        h_pe = ec(nc.semaphore("s_pe"))
        h_dv = ec(nc.semaphore("s_dv"))
        h_ac = ec(nc.semaphore("s_ac"))
        h_st = ec(nc.semaphore("s_st"))
        h_tr = ec(nc.semaphore("s_tr"))
        h_xf = ec(nc.semaphore("s_xf"))
        h_kp = ec(nc.semaphore("s_kp"))
        P = Plan(nc)
        sem = {"ld": Sem(h_ld), "cp": Sem(h_cp), "pe": Sem(h_pe), "dv": Sem(h_dv),
               "ac": Sem(h_ac), "st": Sem(h_st), "tr": Sem(h_tr), "xfree": Sem(h_xf),
               "kp": Sem(h_kp)}
        s_ld, s_cp, s_pe = sem["ld"], sem["cp"], sem["pe"]
        s_dv, s_ac, s_st, s_tr, s_kp = (sem["dv"], sem["ac"], sem["st"], sem["tr"],
                                        sem["kp"])

        consts = {"ones_bf": ones_bf, "ones_f32": ones_f32, "epst": epst}
        pb = {"st0": st0, "st1": st1, "bc0": bc0, "bc1": bc1}
        st = {"stats_free_ac": 0, "stats_free_dv": 0, "bc_free_dv": 0,
              "xf_base": 0, "slot_ctr": 0,
              "slot_reader_step": [0] * HG_SLOTS, "gelu_done": [0] * HG_SLOTS,
              "rail_gelu": [0, 0], "cp_bank_dv": [0, 0], "post_dv": 0,
              "post_st": 0}
        rows = {"m": mrow, "v": vrow}

        for dst, src in [(ones_bf, ones_bf_h.ap()), (ones_f32, ones_f32_h.ap()),
                         (epst, epst_h.ap()), (riota, riota_h.ap()),
                         (tmixs, tmix)]:
            P.emit("sync", lambda e, dst=dst, src=src: e.dma_start(out=dst[:], in_=src),
                   inc=(s_ld, 16))
        const_ld = s_ld.n

        with ExitStack() as ctx1:
            ec1 = ctx1.enter_context
            C8 = ec1(nc.sbuf_tensor("C8", [128, 32 * TG], BF16))
            acc = ec1(nc.sbuf_tensor("acc", [128, 8 * TG], F32))
            hg = ec1(nc.sbuf_tensor("hg", [128, HG_SLOTS * TG], BF16))
            cpw = ec1(nc.sbuf_tensor("cpw", [128, 2 * 4096], BF16))
            xf = ec1(nc.sbuf_tensor("xf", [128, 2 * 8 * TG], F32))
            xnb = ec1(nc.sbuf_tensor("xnb", [128, 8 * TG], BF16))
            rail0 = ec1(nc.psum_tensor("rail0", [128, TG], F32))
            rail1 = ec1(nc.psum_tensor("rail1", [128, TG], F32))
            cp0 = ec1(nc.psum_tensor("cp0", [128, TG], F32))
            cp1 = ec1(nc.psum_tensor("cp1", [128, TG], F32))
            pb.update({"rail0": rail0, "rail1": rail1, "cp0": cp0, "cp1": cp1})

            rmw = hg[:, 0:128]
            ubw = hg[:, 4096:4096 + 512]
            for m in range(32):
                w = [(s_pe, s_pe.n)] if m >= 1 else []
                P.emit("gpsimd", lambda e, m=m: e.dma_start(out=rmw, in_=rmbd[m]),
                       waits=w, inc=(s_cp, 16))
                P.emit("gpsimd", lambda e, m=m: e.dma_start(out=ubw, in_=ubd[m]),
                       inc=(s_cp, 16))
                cpn = s_cp.n
                bank = [rail0, rail1][m % 2]
                mmn = P.emit("tensor", lambda e, bank=bank: e.matmul(
                    bank[:], rmw, ubw, start=True, stop=True),
                    waits=[(s_cp, cpn), (s_dv, s_dv.n)], inc=(s_pe, 1))
                P.emit("vector", lambda e, m=m, bank=bank: e.tensor_copy(
                    out=C8[:, m * TG:m * TG + TG], in_=bank[:]),
                    waits=[(s_pe, mmn)], inc=(s_dv, 1))

            bufs = {"xf": xf, "xn": xnb, "hg": hg, "cpw": cpw, "acc": acc, "C8": C8,
                    "cpstream": cpstream_d, "dbg_ln": dbgln,
                    "scr": {"xb": xb_s, "sq": sq_s, "d": d_s}, "rows": rows}

            def dump_debug():
                P.emit("gpsimd", lambda e: e.dma_start(out=dbgo[:, 0:TG],
                                                       in_=C8[:, 0:TG]), inc=(s_st, 16))
                P.emit("gpsimd", lambda e: e.dma_start(out=dbgo[:, TG:2*TG],
                                                       in_=xnb[:, 0:TG]), inc=(s_st, 16))
                P.emit("gpsimd", lambda e: e.dma_start(out=dbgo[:, 2*TG:3*TG],
                                                       in_=hg[:, 0:TG]), inc=(s_st, 16))
                P.emit("sync", lambda e: e.dma_start(out=dbgo[:, 3*TG:4*TG],
                                                     in_=acc[:, 0:TG]), inc=(s_st, 16))
                P.emit("sync", lambda e: e.dma_start(out=dbgo[0:1, 2*TG:3*TG],
                                                     in_=mrow[:, 0:TG]), inc=(s_st, 16))
                P.emit("sync", lambda e: e.dma_start(out=dbgo[1:2, 2*TG:3*TG],
                                                     in_=vrow[:, 0:TG]), inc=(s_st, 16))

            tg_list = [(0, 0, OVL)] + [(1 + k, OVL + k * TG, TG) for k in range(4)]

            def ap_x_load(c, col0, n):
                return x0T[128 * c:128 * c + 128, col0:col0 + n]

            def post_tg(t_idx, b, n, res_done):
                def xap1(c, b=b, n=n):
                    return xf[:, (b * 8 + c) * TG:(b * 8 + c) * TG + n]

                st["post_st"] = sem["st"].n
                if t_idx == 0:
                    dn = emit_ln_single(
                        P, sem, pb, st,
                        lambda c, b=b: xf[:, (b * 8 + c) * TG + OVL - 1:
                                          (b * 8 + c) * TG + OVL],
                        lambda c: H2[:, c * 2050:c * 2050 + 1],
                        bufs["scr"], rows, consts, pre_dv=max(res_done))
                    st["post_dv"] = dn
                else:
                    k = t_idx - 1
                    for c in range(8):
                        P.emit("sync", lambda e, c=c, k=k, b=b: e.dma_start(
                            out=x1o[128 * c:128 * c + 128, k * TG:k * TG + TG],
                            in_=xf[:, (b * 8 + c) * TG:(b * 8 + c) * TG + TG]),
                            waits=[(s_dv, res_done[c])], inc=(s_st, 16))
                    st["post_st"] = sem["st"].n
                    dn = emit_ln(
                        P, sem, pb, st, xap1,
                        lambda c, k=k: H2[:, c * 2050 + 1 + k * TG:
                                          c * 2050 + 1 + k * TG + TG],
                        TG, bufs["scr"], rows, consts,
                        pre_dv=max(res_done))
                    st["post_dv"] = dn

            emit_ffn(P, sem, pb, st, bufs, tg_list, ap_x_load, post_tg, consts)
            dump_debug()

            ph_dv, ph_ac, ph_kp, ph_st, ph_pe = s_dv.n, s_ac.n, s_kp.n, s_st.n, s_pe.n

        with ExitStack() as ctx2:
            ec2 = ctx2.enter_context
            Kt = ec2(nc.sbuf_tensor("Kt", [128, 8 * T_OWN], BF16))
            Vt = ec2(nc.sbuf_tensor("Vt", [128, 16 * D], BF16))
            wb8 = ec2(nc.sbuf_tensor("wb8", [128, 2 * 8 * D], BF16))
            dmix = ec2(nc.sbuf_tensor("dmix", [128, 8 * T_OWN], BF16))
            xm = ec2(nc.sbuf_tensor("xm", [128, 2 * 8 * TG], BF16))
            stg = ec2(nc.sbuf_tensor("stg", [128, 4 * TG], BF16))
            ktr = ec2(nc.sbuf_tensor("ktr", [128, 2 * D], BF16))
            kwb = ec2(nc.sbuf_tensor("kwb", [128, 2 * D], BF16))
            pp0 = ec2(nc.psum_tensor("pp0", [128, TG], F32))
            pp1 = ec2(nc.psum_tensor("pp1", [128, TG], F32))
            pc0 = ec2(nc.psum_tensor("pc0", [64, 64], F32))
            pc1 = ec2(nc.psum_tensor("pc1", [64, 64], F32))
            ldrow = ec2(nc.sbuf_tensor("ldrow", [1, 16], F32))
            ldb = ec2(nc.sbuf_tensor("ldb", [128, 16], F32))
            wkp = ec2(nc.sbuf_tensor("wkp", [128, 16], F32))
            wst = ec2(nc.sbuf_tensor("wst", [128, 8], F32))
            ldk = ec2(nc.sbuf_tensor("ldk", [128, 8], F32))
            Sst = ec2(nc.sbuf_tensor("Sst", [128, 8 * 64], F32))

            for eng in ("sync", "gpsimd", "tensor", "vector", "scalar"):
                P.wait(eng, s_dv, ph_dv)
                P.wait(eng, s_ac, ph_ac)
                P.wait(eng, s_kp, ph_kp)
                P.wait(eng, s_st, ph_st)

            P.emit("sync", lambda e: e.dma_start(out=ldrow[:], in_=td), inc=(s_ld, 16))
            ldn = s_ld.n
            P.emit("scalar", lambda e: e.activation(out=ldrow[:], in_=ldrow[:], func=AF.Exp),
                   waits=[(s_ld, ldn)], inc=(s_ac, 1))
            P.emit("vector", lambda e: e.tensor_scalar_mul(out=ldrow[:], in0=ldrow[:],
                                                           scalar1=-1.0),
                   waits=[(s_ac, s_ac.n)], inc=(s_dv, 1))
            P.emit("tensor", lambda e: e.matmul(bc0[:, 0:16], ones_f32[:], ldrow[:],
                                                start=True, stop=True),
                   waits=[(s_dv, s_dv.n), (s_dv, st["bc_free_dv"])], inc=(s_pe, 1))
            P.emit("vector", lambda e: e.tensor_copy(out=ldb[:], in_=bc0[:, 0:16]),
                   waits=[(s_pe, s_pe.n)], inc=(s_dv, 1))
            st["bc_free_dv"] = s_dv.n
            P.emit("vector", lambda e: e.tensor_scalar_mul(out=wkp[:], in0=ldb[:],
                                                           scalar1=riota[:]),
                   waits=([(s_dv, s_dv.n)] if SAFE_CHAIN else ()), inc=(s_dv, 1))
            P.emit("scalar", lambda e: e.activation(out=wkp[:], in_=wkp[:], func=AF.Exp),
                   waits=[(s_dv, s_dv.n)], inc=(s_ac, 1))
            for c in range(8):
                for hh in range(2):
                    sl = slice(64 * hh, 64 * hh + 64)
                    P.emit("vector", lambda e, c=c, hh=hh, sl=sl: e.tensor_copy(
                        out=ldk[sl, c:c + 1], in_=ldb[sl, 2 * c + hh:2 * c + hh + 1]),
                        waits=([(s_dv, s_dv.n)] if SAFE_CHAIN else ()), inc=(s_dv, 1))
            P.emit("scalar", lambda e: e.activation(out=wst[:], in_=ldk[:], func=AF.Exp,
                                                    scale=128.0),
                   waits=[(s_dv, s_dv.n), (s_ac, s_ac.n)], inc=(s_ac, 1))
            P.emit("vector", lambda e: e.memset(Sst[:], 0.0), inc=(s_dv, 1))

            for k in range(4):
                for c in range(8):
                    x1ap = H2[:, c * 2050 + 1 + k * TG: c * 2050 + 1 + k * TG + TG]
                    xxap = H2[:, c * 2050 + 0 + k * TG: c * 2050 + 0 + k * TG + TG]
                    dap = dmix[:, (c * 4 + k) * TG:(c * 4 + k) * TG + TG]
                    P.emit("vector", lambda e, dap=dap, x1ap=x1ap, xxap=xxap:
                           e.tensor_sub(out=dap, in0=x1ap, in1=xxap), inc=(s_dv, 1))
            dmix_done = s_dv.n

            def emit_mix(tgt_i, k, c, guard_pe):
                xxap = H2[:, c * 2050 + 0 + k * TG: c * 2050 + 0 + k * TG + TG]
                dap = dmix[:, (c * 4 + k) * TG:(c * 4 + k) * TG + TG]
                xb0 = (k % 2) * 8 * TG
                out = xm[:, xb0 + c * TG:xb0 + c * TG + TG]
                tm = tmixs[:, tgt_i * 8 + c:tgt_i * 8 + c + 1]
                return P.emit("vector", lambda e, out=out, dap=dap, tm=tm, xxap=xxap:
                              e.scalar_tensor_tensor(out=out, in0=dap, scalar=tm, in1=xxap,
                                                     op0=ALU.mult, op1=ALU.add),
                              waits=[(s_dv, dmix_done), (s_pe, guard_pe)], inc=(s_dv, 1))

            wld_map = {}

            def load_weights(tgt_i, wsrc, guard):
                wb0 = (tgt_i % 2) * 8 * D
                for ci in range(8):
                    P.emit("gpsimd", lambda e, ci=ci, wsrc=wsrc, wb0=wb0: e.dma_start(
                        out=wb8[:, wb0 + ci * D:wb0 + ci * D + D],
                        in_=wsrc[128 * ci:128 * ci + 128, :]),
                        waits=([(s_pe, guard)] if ci == 0 else ()), inc=(s_cp, 16))
                wld_map[tgt_i] = s_cp.n

            pp = [pp0, pp1]
            pp_free = [[], []]
            stg_free_st = [0, 0, 0, 0]

            load_weights(0, wrt, 0)
            load_weights(1, wkt, 0)
            xm_free = [0, 0]
            for tgt_i, (wsrc, resident, dram_out) in enumerate(
                    [(wrt, None, Ro), (wkt, Kt, Ko)]):
                wld = wld_map[tgt_i]
                wb0 = (tgt_i % 2) * 8 * D
                for k in range(4):
                    for c in range(8):
                        emit_mix(tgt_i, k, c, xm_free[k % 2])
                    mixn = s_dv.n
                    for o in range(8):
                        stp = tgt_i * 32 + k * 8 + o
                        bank = pp[stp % 2]
                        for ci in range(8):
                            lhsT = wb8[:, wb0 + ci * D + o * 128: wb0 + ci * D + o * 128 + 128]
                            rhs = xm[:, (k % 2) * 8 * TG + ci * TG:(k % 2) * 8 * TG + ci * TG + TG]
                            w0 = ([(s_cp, wld), (s_dv, mixn)] + pp_free[stp % 2]
                                  if ci == 0 else ())
                            P.emit("tensor", lambda e, bank=bank, lhsT=lhsT, rhs=rhs,
                                   ci=ci: e.matmul(bank[:], lhsT, rhs, start=(ci == 0),
                                                   stop=(ci == 7)),
                                   waits=w0, inc=((s_pe, 1) if ci == 7 else None))
                        pen = s_pe.n
                        if o == 7:
                            xm_free[k % 2] = pen
                        if resident is not None:
                            dst = resident[:, o * T_OWN + k * TG: o * T_OWN + k * TG + TG]
                            dn = P.emit("vector", lambda e, dst=dst, bank=bank:
                                        e.tensor_copy(out=dst, in_=bank[:]),
                                        waits=[(s_pe, pen)], inc=(s_dv, 1))
                            P.emit("sync", lambda e, dst=dst, o=o, k=k,
                                   dram_out=dram_out: e.dma_start(
                                       out=dram_out[128 * o:128 * o + 128,
                                                    k * TG:k * TG + TG], in_=dst),
                                   waits=[(s_dv, dn)], inc=(s_st, 16))
                            pp_free[stp % 2] = [(s_dv, dn)]
                        else:
                            sslot = stp % 4
                            dst = stg[:, sslot * TG:sslot * TG + TG]
                            dn = P.emit("vector", lambda e, dst=dst, bank=bank:
                                        e.tensor_copy(out=dst, in_=bank[:]),
                                        waits=[(s_pe, pen),
                                               (s_st, stg_free_st[sslot])],
                                        inc=(s_dv, 1))
                            stn = P.emit("sync", lambda e, dst=dst, o=o, k=k,
                                         dram_out=dram_out: e.dma_start(
                                             out=dram_out[128 * o:128 * o + 128,
                                                          k * TG:k * TG + TG], in_=dst),
                                         waits=[(s_dv, dn)], inc=(s_st, 16))
                            stg_free_st[sslot] = stn
                            pp_free[stp % 2] = [(s_dv, dn)]

            load_weights(2, wvt, wld_map[0] * 0 + s_pe.n)
            for tgt_i, (wsrc, resident, dram_out, act) in enumerate(
                    [(wvt, Vt, Vo, None), (wgt, None, Go, AF.Silu)], start=2):
                if tgt_i == 3:
                    load_weights(3, wgt, s_pe.n)
                wld = wld_map[tgt_i]
                wb0 = (tgt_i % 2) * 8 * D
                for k in range(4):
                    for c in range(8):
                        emit_mix(tgt_i, k, c, xm_free[k % 2])
                    mixn = s_dv.n
                    for tt in range(4):
                        t_glob = k * 4 + tt
                        for oh in range(2):
                            stp = tt * 2 + oh
                            bank = pp[stp % 2]
                            for ci in range(8):
                                lhsT = xm[:, (k % 2) * 8 * TG + ci * TG + tt * 128:
                                          (k % 2) * 8 * TG + ci * TG + tt * 128 + 128]
                                rhs = wb8[:, wb0 + ci * D + oh * 512: wb0 + ci * D + oh * 512 + 512]
                                w0 = ([(s_cp, wld), (s_dv, mixn)] + pp_free[stp % 2]
                                      if ci == 0 else ())
                                P.emit("tensor", lambda e, bank=bank, lhsT=lhsT, rhs=rhs,
                                       ci=ci: e.matmul(bank[:], lhsT, rhs,
                                                       start=(ci == 0), stop=(ci == 7)),
                                       waits=w0, inc=((s_pe, 1) if ci == 7 else None))
                            pen = s_pe.n
                            if tt == 3 and oh == 1:
                                xm_free[k % 2] = pen
                            if resident is not None:
                                dst = resident[:, t_glob * D + oh * 512:
                                               t_glob * D + oh * 512 + 512]
                                dn = P.emit("vector", lambda e, dst=dst, bank=bank:
                                            e.tensor_copy(out=dst, in_=bank[:]),
                                            waits=[(s_pe, pen)], inc=(s_dv, 1))
                                P.emit("sync", lambda e, dst=dst, t_glob=t_glob, oh=oh,
                                       dram_out=dram_out: e.dma_start(
                                           out=dram_out[t_glob * 128:t_glob * 128 + 128,
                                                        oh * 512:oh * 512 + 512],
                                           in_=dst),
                                       waits=[(s_dv, dn)], inc=(s_st, 16))
                                pp_free[stp % 2] = [(s_dv, dn)]
                            else:
                                sslot = stp % 4
                                dst = stg[:, sslot * TG:sslot * TG + TG]
                                an = P.emit("scalar", lambda e, dst=dst, bank=bank:
                                            e.activation(out=dst, in_=bank[:],
                                                         func=AF.Silu),
                                            waits=[(s_pe, pen),
                                                   (s_st, stg_free_st[sslot])],
                                            inc=(s_ac, 1))
                                stn = P.emit("sync", lambda e, dst=dst, t_glob=t_glob,
                                             oh=oh, dram_out=dram_out: e.dma_start(
                                                 out=dram_out[
                                                     t_glob * 128:t_glob * 128 + 128,
                                                     oh * 512:oh * 512 + 512], in_=dst),
                                             waits=[(s_ac, an)], inc=(s_st, 16))
                                stg_free_st[sslot] = stn
                                pp_free[stp % 2] = [(s_ac, an)]

            kv_done_dv = s_dv.n

            kw_ready = []
            chunk_pe = []
            chunk_st = []
            pc_free = [[], []]
            for n_ch in range(NCHUNK):
                nb = n_ch % 2
                for c in range(8):
                    src = Kt[:, c * T_OWN + n_ch * 128: c * T_OWN + n_ch * 128 + 128]
                    dst = ktr[:, nb * D + c * 128: nb * D + c * 128 + 128]
                    w = ([(s_dv, kw_ready[n_ch - 2])] if n_ch >= 2
                         else [(s_dv, kv_done_dv)])
                    P.emit("sync", lambda e, dst=dst, src=src: e.dma_start(
                        out=dst, in_=src, transpose=True), waits=w, inc=(s_tr, 16))
                trn = s_tr.n
                for c in range(8):
                    for hh in range(2):
                        src = ktr[:, nb * D + c * 128 + 64 * hh:
                                  nb * D + c * 128 + 64 * hh + 64]
                        dst = kwb[:, nb * D + c * 128 + 64 * hh:
                                  nb * D + c * 128 + 64 * hh + 64]
                        sc = wkp[:, 2 * c + hh:2 * c + hh + 1]
                        w2 = [(s_tr, trn)]
                        if n_ch >= 2:
                            w2 += [(s_pe, chunk_pe[n_ch - 2]), (s_st, chunk_st[n_ch - 2])]
                        P.emit("vector", lambda e, dst=dst, src=src, sc=sc:
                               e.tensor_scalar_mul(out=dst, in0=src, scalar1=sc),
                               waits=w2, inc=(s_dv, 1))
                kw_ready.append(s_dv.n)
                P.emit("sync", lambda e, nb=nb, n_ch=n_ch: e.dma_start(
                    out=Kwo[n_ch * 128:n_ch * 128 + 128, :],
                    in_=kwb[:, nb * D:nb * D + D]),
                    waits=[(s_dv, kw_ready[-1])], inc=(s_st, 16))
                chunk_st.append(s_st.n)
                for c in range(8):
                    pens = []
                    for hh, bank in ((0, pc0), (1, pc1)):
                        lhsT = kwb[:, nb * D + c * 128 + 64 * hh:
                                   nb * D + c * 128 + 64 * hh + 64]
                        rhs = Vt[:, n_ch * D + (2 * c + hh) * 64:
                                 n_ch * D + (2 * c + hh) * 64 + 64]
                        pn = P.emit("tensor", lambda e, bank=bank, lhsT=lhsT, rhs=rhs:
                                    e.matmul(bank[:], lhsT, rhs, start=True, stop=True),
                                    waits=[(s_dv, kw_ready[-1])] + pc_free[hh],
                                    inc=(s_pe, 1))
                        pens.append(pn)
                    for hh, bank in ((0, pc0), (1, pc1)):
                        ssl = Sst[64 * hh:64 * hh + 64, c * 64:c * 64 + 64]
                        wsl = wst[64 * hh:64 * hh + 64, c:c + 1]
                        dn = P.emit("vector", lambda e, ssl=ssl, wsl=wsl, bank=bank:
                                    e.scalar_tensor_tensor(out=ssl, in0=ssl, scalar=wsl,
                                                           in1=bank[:], op0=ALU.mult,
                                                           op1=ALU.add),
                                    waits=[(s_pe, pens[hh]), (s_dv, s_dv.n)],
                                    inc=(s_dv, 1))
                        pc_free[hh] = [(s_dv, dn)]
                chunk_pe.append(s_pe.n)

            bdone = s_dv.n
            for c in range(8):
                P.emit("sync", lambda e, c=c: e.dma_start(
                    out=Bo[c], in_=Sst[:, c * 64:c * 64 + 64]),
                    waits=[(s_dv, bdone)], inc=(s_st, 16))

            P.wait("gpsimd", s_st, s_st.n)
            P.wait("gpsimd", s_tr, s_tr.n)

        with nc.Block() as block:
            P.build(block)

    return nc



def build_l2():
    nc = bass.Bass("TRN2", target_bir_lowering=False)

    x1i = nc.dram_tensor("x1i", [D, T_OWN], F32, kind="ExternalInput").ap()
    Ri = nc.dram_tensor("Ri", [D, T_OWN], BF16, kind="ExternalInput").ap()
    Ki = nc.dram_tensor("Ki", [D, T_OWN], BF16, kind="ExternalInput").ap()
    Gi = nc.dram_tensor("Gi", [T_OWN, D], BF16, kind="ExternalInput").ap()
    Vi = nc.dram_tensor("Vi", [T_OWN, D], BF16, kind="ExternalInput").ap()
    Kwi = nc.dram_tensor("Kwi", [T_OWN, D], BF16, kind="ExternalInput").ap()
    Ball = nc.dram_tensor("Ball", [128, 4096], F32, kind="ExternalInput").ap()
    cpstream_d = nc.dram_tensor("cpstream", [128, 131072], F32, kind="ExternalInput").ap()
    rmbd = nc.dram_tensor("rmbd", [32, 128, 128], F32, kind="ExternalInput").ap()
    ubd = nc.dram_tensor("ubd", [32, 128, 512], F32, kind="ExternalInput").ap()
    wot = nc.dram_tensor("wot", [D, D], F32, kind="ExternalInput").ap()
    td = nc.dram_tensor("td", [1, 16], F32, kind="ExternalInput").ap()
    tf = nc.dram_tensor("tf", [1, 16], F32, kind="ExternalInput").ap()
    lnw_d = nc.dram_tensor("lnw", [1, D], F32, kind="ExternalInput").ap()
    lnb_d = nc.dram_tensor("lnb", [1, D], F32, kind="ExternalInput").ap()
    expo_d = nc.dram_tensor("expo", [1, 8], F32, kind="ExternalInput").ap()
    mask8_d = nc.dram_tensor("mask8", [1, 8], F32, kind="ExternalInput").ap()

    x3o = nc.dram_tensor("x3o", [D, T_OWN], F32, kind="ExternalOutput").ap()
    x2o = nc.dram_tensor("x2o", [D, T_OWN], F32, kind="ExternalOutput").ap()

    x2d = nc.dram_tensor("x2d", [D, T_OWN], F32, kind="Internal")

    ones_bf_h = nc.inline_tensor(np.ones((128, 1), dtype=BF), "c_ones_bf")
    ones_f32_h = nc.inline_tensor(np.ones((1, 128), dtype=np.float32), "c_ones_f32")
    epst_h = nc.inline_tensor(np.full((128, 1), EPS, dtype=np.float32), "c_eps")
    ii = np.arange(128, dtype=np.float32)
    d1t_h = nc.inline_tensor(np.maximum(ii[None, :] - ii[:, None] - 1.0, 0.0), "c_d1t")
    maskt_h = nc.inline_tensor((ii[None, :] > ii[:, None]).astype(np.float32), "c_maskt")
    iotaf_h = nc.inline_tensor(np.broadcast_to(ii[None, :], (128, 128)).copy(), "c_iotaf")
    eye_h = nc.inline_tensor(np.eye(128, dtype=np.float32), "c_eye")

    with ExitStack() as ctx:
        ec = ctx.enter_context
        ones_bf = ec(nc.sbuf_tensor("ones_bf", [128, 1], BF16))
        ones_f32 = ec(nc.sbuf_tensor("ones_f32", [1, 128], F32))
        epst = ec(nc.sbuf_tensor("epst", [128, 1], F32))
        xb_s = ec(nc.sbuf_tensor("xb_s", [128, TG], BF16))
        sq_s = ec(nc.sbuf_tensor("sq_s", [128, TG], BF16))
        d_s = ec(nc.sbuf_tensor("d_s", [128, TG], F32))
        mrow = ec(nc.sbuf_tensor("mrow", [1, TG], F32))
        vrow = ec(nc.sbuf_tensor("vrow", [1, TG], F32))
        h_ld = ec(nc.semaphore("s_ld"))
        h_cp = ec(nc.semaphore("s_cp"))
        h_pe = ec(nc.semaphore("s_pe"))
        h_dv = ec(nc.semaphore("s_dv"))
        h_ac = ec(nc.semaphore("s_ac"))
        h_st = ec(nc.semaphore("s_st"))
        h_tr = ec(nc.semaphore("s_tr"))
        h_xf = ec(nc.semaphore("s_xf"))
        h_kp = ec(nc.semaphore("s_kp"))

        P = Plan(nc)
        sem = {"ld": Sem(h_ld), "cp": Sem(h_cp), "pe": Sem(h_pe), "dv": Sem(h_dv),
               "ac": Sem(h_ac), "st": Sem(h_st), "tr": Sem(h_tr), "xfree": Sem(h_xf),
               "kp": Sem(h_kp)}
        s_ld, s_cp, s_pe = sem["ld"], sem["cp"], sem["pe"]
        s_dv, s_ac, s_st, s_tr, s_kp = (sem["dv"], sem["ac"], sem["st"], sem["tr"],
                                        sem["kp"])
        consts = {"ones_bf": ones_bf, "ones_f32": ones_f32, "epst": epst}
        pb = {}
        st = {"stats_free_ac": 0, "stats_free_dv": 0, "bc_free_dv": 0,
              "xf_base": 0, "slot_ctr": 0,
              "slot_reader_step": [0] * HG_SLOTS, "gelu_done": [0] * HG_SLOTS,
              "rail_gelu": [0, 0], "cp_bank_dv": [0, 0], "post_dv": 0,
              "post_st": 0, "prev_g1": 0, "ygtm_tr": 0}
        rows = {"m": mrow, "v": vrow}

        for dst, src in [(ones_bf, ones_bf_h.ap()), (ones_f32, ones_f32_h.ap()),
                         (epst, epst_h.ap())]:
            P.emit("sync", lambda e, dst=dst, src=src: e.dma_start(out=dst[:], in_=src),
                   inc=(s_ld, 16))

        with ExitStack() as ctxa:
            eca = ctxa.enter_context
            Rt = eca(nc.sbuf_tensor("Rt", [128, 8 * T_OWN], BF16))
            Kt = eca(nc.sbuf_tensor("Kt", [128, 8 * T_OWN], BF16))
            Vt = eca(nc.sbuf_tensor("Vt", [128, 16 * D], BF16))
            gt2 = eca(nc.sbuf_tensor("gt2", [128, 2 * D], BF16))
            kwt = eca(nc.sbuf_tensor("kwt", [128, 2 * D], BF16))
            wmt = eca(nc.sbuf_tensor("wmt", [128, 16 * 128], BF16))
            wbt = eca(nc.sbuf_tensor("wbt", [128, 8 * 128], BF16))
            lnwb = eca(nc.sbuf_tensor("lnwb", [128, D], BF16))
            lnbb = eca(nc.sbuf_tensor("lnbb", [128, D], BF16))
            lnwrow = eca(nc.sbuf_tensor("lnwrow", [1, D], F32))
            lnbrow = eca(nc.sbuf_tensor("lnbrow", [1, D], F32))
            d1t = eca(nc.sbuf_tensor("d1t", [128, 128], F32))
            maskt = eca(nc.sbuf_tensor("maskt", [128, 128], F32))
            iotaf = eca(nc.sbuf_tensor("iotaf", [128, 128], F32))
            eyet = eca(nc.sbuf_tensor("eyet", [128, 128], F32))
            ldrow = eca(nc.sbuf_tensor("ldrow", [1, 16], F32))
            ufrow = eca(nc.sbuf_tensor("ufrow", [1, 16], F32))
            ldb = eca(nc.sbuf_tensor("ldb", [128, 16], F32))
            ufb = eca(nc.sbuf_tensor("ufb", [128, 16], F32))
            ldk = eca(nc.sbuf_tensor("ldk", [128, 8], F32))
            wst = eca(nc.sbuf_tensor("wst", [128, 8], F32))
            Sst = eca(nc.sbuf_tensor("Sst", [128, 8 * 64], F32))
            Sbf = eca(nc.sbuf_tensor("Sbf", [128, 8 * 64], BF16))
            exb = eca(nc.sbuf_tensor("exb", [128, 8], F32))
            mkb = eca(nc.sbuf_tensor("mkb", [128, 8], F32))
            cofb = eca(nc.sbuf_tensor("cofb", [128, 8 * 8], F32))
            wo8 = eca(nc.sbuf_tensor("wo8", [128, 8 * D], BF16))
            attwb = eca(nc.sbuf_tensor("attwb", [128, 2 * 2048], BF16))
            rwbb = eca(nc.sbuf_tensor("rwbb", [128, 2 * D], BF16))
            xof = eca(nc.sbuf_tensor("xof", [128, 2 * D], F32))
            gnscr = eca(nc.sbuf_tensor("gnscr", [128, D], F32))
            ygtm2 = eca(nc.sbuf_tensor("ygtm2", [128, D], BF16))
            srow = eca(nc.sbuf_tensor("srow", [128, 96], F32))
            ygtm = eca(nc.sbuf_tensor("ygtm", [128, D], BF16))
            ygfm = eca(nc.sbuf_tensor("ygfm", [128, 8 * 512], BF16))
            x1t = eca(nc.sbuf_tensor("x1t", [128, 8 * 512], F32))
            x2t = eca(nc.sbuf_tensor("x2t", [128, 512], F32))
            setup_ctx = ExitStack()
            bc0 = setup_ctx.enter_context(nc.psum_tensor("bcA0", [128, TG], F32))
            bc1 = setup_ctx.enter_context(nc.psum_tensor("bcA1", [128, TG], F32))

            P.emit("sync", lambda e: e.dma_start(out=d1t[:], in_=d1t_h.ap()), inc=(s_ld, 16))
            P.emit("sync", lambda e: e.dma_start(out=maskt[:], in_=maskt_h.ap()), inc=(s_ld, 16))
            P.emit("sync", lambda e: e.dma_start(out=iotaf[:], in_=iotaf_h.ap()), inc=(s_ld, 16))
            P.emit("sync", lambda e: e.dma_start(out=eyet[:], in_=eye_h.ap()), inc=(s_ld, 16))
            P.emit("sync", lambda e: e.dma_start(out=ldrow[:], in_=td), inc=(s_ld, 16))
            P.emit("sync", lambda e: e.dma_start(out=ufrow[:], in_=tf), inc=(s_ld, 16))
            P.emit("sync", lambda e: e.dma_start(out=lnwrow[:], in_=lnw_d), inc=(s_ld, 16))
            P.emit("sync", lambda e: e.dma_start(out=lnbrow[:], in_=lnb_d), inc=(s_ld, 16))
            P.emit("sync", lambda e: e.dma_start(out=exb[0:1, :], in_=expo_d), inc=(s_ld, 16))
            P.emit("sync", lambda e: e.dma_start(out=mkb[0:1, :], in_=mask8_d), inc=(s_ld, 16))
            for c in range(8):
                P.emit("sync", lambda e, c=c: e.dma_start(
                    out=Rt[:, c * T_OWN:(c + 1) * T_OWN], in_=Ri[128 * c:128 * c + 128, :]),
                    inc=(s_ld, 16))
                P.emit("sync", lambda e, c=c: e.dma_start(
                    out=Kt[:, c * T_OWN:(c + 1) * T_OWN], in_=Ki[128 * c:128 * c + 128, :]),
                    inc=(s_ld, 16))
            for tt in range(16):
                P.emit("sync", lambda e, tt=tt: e.dma_start(
                    out=Vt[:, tt * D:(tt + 1) * D], in_=Vi[tt * 128:tt * 128 + 128, :]),
                    inc=(s_ld, 16))
            for c in range(8):
                P.emit("gpsimd", lambda e, c=c: e.dma_start(
                    out=wo8[:, c * D:(c + 1) * D], in_=wot[128 * c:128 * c + 128, :]),
                    inc=(s_cp, 16))
            const_ld = s_ld.n

            P.emit("scalar", lambda e: e.activation(out=ldrow[:], in_=ldrow[:], func=AF.Exp),
                   waits=[(s_ld, const_ld)], inc=(s_ac, 1))
            P.emit("vector", lambda e: e.tensor_scalar_mul(out=ldrow[:], in0=ldrow[:],
                                                           scalar1=-1.0),
                   waits=[(s_ac, s_ac.n)], inc=(s_dv, 1))
            P.emit("tensor", lambda e: e.matmul(bc0[:, 0:16], ones_f32[:], ldrow[:],
                                                start=True, stop=True),
                   waits=[(s_dv, s_dv.n)], inc=(s_pe, 1))
            P.emit("tensor", lambda e: e.matmul(bc1[:, 0:16], ones_f32[:], ufrow[:],
                                                start=True, stop=True), inc=(s_pe, 1))
            P.emit("vector", lambda e: e.tensor_copy(out=ldb[:], in_=bc0[:, 0:16]),
                   waits=[(s_pe, s_pe.n)], inc=(s_dv, 1))
            P.emit("vector", lambda e: e.tensor_copy(out=ufb[:], in_=bc1[:, 0:16]),
                   waits=([(s_dv, s_dv.n)] if SAFE_CHAIN else ()), inc=(s_dv, 1))
            st["bc_free_dv"] = s_dv.n
            for c in range(8):
                for hh in range(2):
                    sl = slice(64 * hh, 64 * hh + 64)
                    P.emit("vector", lambda e, c=c, hh=hh, sl=sl: e.tensor_copy(
                        out=ldk[sl, c:c + 1], in_=ldb[sl, 2 * c + hh:2 * c + hh + 1]),
                        waits=([(s_dv, s_dv.n)] if SAFE_CHAIN else ()), inc=(s_dv, 1))
            P.emit("scalar", lambda e: e.activation(out=wst[:], in_=ldk[:], func=AF.Exp,
                                                    scale=128.0),
                   waits=[(s_dv, s_dv.n), (s_ac, s_ac.n)], inc=(s_ac, 1))
            for h in range(16):
                dst = wmt[:, h * 128:(h + 1) * 128]
                P.emit("vector", lambda e, dst=dst, h=h: e.tensor_scalar_mul(
                    out=dst, in0=d1t[:], scalar1=ldb[:, h:h + 1]),
                    waits=[(s_dv, s_dv.n)], inc=(s_dv, 1))
                P.emit("scalar", lambda e, dst=dst: e.activation(out=dst, in_=dst,
                                                                 func=AF.Exp),
                       waits=[(s_dv, s_dv.n), (s_ac, s_ac.n)], inc=(s_ac, 1))
                P.emit("vector", lambda e, dst=dst: e.tensor_mul(out=dst, in0=dst,
                                                                 in1=maskt[:]),
                       waits=[(s_ac, s_ac.n)], inc=(s_dv, 1))
                P.emit("vector", lambda e, h=h: e.tensor_scalar_mul(
                    out=gnscr[:, 0:128], in0=eyet[:], scalar1=ufb[:, h:h + 1]),
                    waits=([(s_dv, s_dv.n)] if SAFE_CHAIN else ()), inc=(s_dv, 1))
                P.emit("vector", lambda e, dst=dst: e.tensor_add(out=dst, in0=dst,
                                                                 in1=gnscr[:, 0:128]),
                       waits=[(s_dv, s_dv.n)], inc=(s_dv, 1))
            for c in range(8):
                dst = wbt[:, c * 128:(c + 1) * 128]
                P.emit("vector", lambda e, dst=dst, c=c: e.tensor_scalar_mul(
                    out=dst, in0=iotaf[:], scalar1=ldk[:, c:c + 1]),
                    waits=[(s_dv, s_dv.n)], inc=(s_dv, 1))
                P.emit("scalar", lambda e, dst=dst: e.activation(out=dst, in_=dst,
                                                                 func=AF.Exp),
                       waits=[(s_dv, s_dv.n), (s_ac, s_ac.n)], inc=(s_ac, 1))
            wmt_done_ac = s_ac.n
            for half in range(2):
                P.emit("tensor", lambda e, half=half: e.matmul(
                    bc0[:, 0:512], ones_f32[:], lnwrow[0:1, half * 512:half * 512 + 512],
                    start=True, stop=True),
                    waits=[(s_ld, const_ld), (s_dv, st["bc_free_dv"])], inc=(s_pe, 1))
                P.emit("tensor", lambda e, half=half: e.matmul(
                    bc1[:, 0:512], ones_f32[:], lnbrow[0:1, half * 512:half * 512 + 512],
                    start=True, stop=True), inc=(s_pe, 1))
                pen = s_pe.n
                P.emit("vector", lambda e, half=half: e.tensor_copy(
                    out=lnwb[:, half * 512:half * 512 + 512], in_=bc0[:, 0:512]),
                    waits=[(s_pe, pen)], inc=(s_dv, 1))
                P.emit("vector", lambda e, half=half: e.tensor_copy(
                    out=lnbb[:, half * 512:half * 512 + 512], in_=bc1[:, 0:512]),
                    waits=([(s_dv, s_dv.n)] if SAFE_CHAIN else ()), inc=(s_dv, 1))
                st["bc_free_dv"] = s_dv.n
            P.emit("tensor", lambda e: e.matmul(bc0[:, 0:8], ones_f32[:], exb[0:1, :],
                                                start=True, stop=True),
                   waits=[(s_dv, st["bc_free_dv"])], inc=(s_pe, 1))
            P.emit("tensor", lambda e: e.matmul(bc1[:, 0:8], ones_f32[:], mkb[0:1, :],
                                                start=True, stop=True), inc=(s_pe, 1))
            pen = s_pe.n
            P.emit("vector", lambda e: e.tensor_copy(out=exb[:], in_=bc0[:, 0:8]),
                   waits=[(s_pe, pen)], inc=(s_dv, 1))
            P.emit("vector", lambda e: e.tensor_copy(out=mkb[:], in_=bc1[:, 0:8]),
                   waits=([(s_dv, s_dv.n)] if SAFE_CHAIN else ()), inc=(s_dv, 1))
            st["bc_free_dv"] = s_dv.n
            for c in range(8):
                dst = cofb[:, c * 8:(c + 1) * 8]
                P.emit("vector", lambda e, dst=dst, c=c: e.tensor_scalar_mul(
                    out=dst, in0=exb[:], scalar1=ldk[:, c:c + 1]),
                    waits=[(s_dv, s_dv.n)], inc=(s_dv, 1))
                P.emit("scalar", lambda e, dst=dst: e.activation(out=dst, in_=dst,
                                                                 func=AF.Exp),
                       waits=[(s_dv, s_dv.n), (s_ac, s_ac.n)], inc=(s_ac, 1))
                P.emit("vector", lambda e, dst=dst: e.tensor_mul(out=dst, in0=dst,
                                                                 in1=mkb[:]),
                       waits=[(s_ac, s_ac.n)], inc=(s_dv, 1))
            bts = x1t
            P.emit("sync", lambda e: e.dma_start(out=bts[:], in_=Ball), inc=(s_ld, 16))
            ldn = s_ld.n
            for c in range(8):
                for r in range(8):
                    bsl = bts[:, (r * 8 + c) * 64:(r * 8 + c) * 64 + 64]
                    ssl = Sst[:, c * 64:c * 64 + 64]
                    cf = cofb[:, c * 8 + r:c * 8 + r + 1]
                    if r == 0:
                        P.emit("vector", lambda e, ssl=ssl, cf=cf, bsl=bsl:
                               e.tensor_scalar_mul(out=ssl, in0=bsl, scalar1=cf),
                               waits=[(s_ld, ldn), (s_dv, s_dv.n)], inc=(s_dv, 1))
                    else:
                        P.emit("vector", lambda e, ssl=ssl, cf=cf, bsl=bsl:
                               e.scalar_tensor_tensor(out=ssl, in0=bsl, scalar=cf,
                                                      in1=ssl, op0=ALU.mult, op1=ALU.add),
                               waits=[(s_ld, ldn), (s_dv, s_dv.n)], inc=(s_dv, 1))
            for c in range(8):
                P.emit("vector", lambda e, c=c: e.tensor_copy(
                    out=Sbf[:, c * 64:c * 64 + 64], in_=Sst[:, c * 64:c * 64 + 64]),
                    waits=[(s_dv, s_dv.n)], inc=(s_dv, 1))
            sin_done = s_dv.n
            setup_ctx.close()
            psA = eca(nc.psum_tensor("psA", [128, 128], F32))
            psB = eca(nc.psum_tensor("psB", [128, 128], F32))
            psoA = eca(nc.psum_tensor("psoA", [128, 64], F32))
            psoB = eca(nc.psum_tensor("psoB", [128, 64], F32))
            pc0 = eca(nc.psum_tensor("pc0", [64, 64], F32))
            pc1 = eca(nc.psum_tensor("pc1", [64, 64], F32))
            ppo0 = eca(nc.psum_tensor("ppo0", [128, TG], F32))
            ppo1 = eca(nc.psum_tensor("ppo1", [128, TG], F32))

            x1_loaded = {}

            def load_x1(kk):
                dvn = s_dv.n
                for c in range(8):
                    P.emit("sync", lambda e, c=c, kk=kk: e.dma_start(
                        out=x1t[:, c * 512:c * 512 + 512],
                        in_=x1i[128 * c:128 * c + 128, kk * 512:kk * 512 + 512]),
                        waits=[(s_dv, dvn if kk > 0 else sin_done)], inc=(s_ld, 16))
                x1_loaded[kk] = s_ld.n

            res_ld = s_ld.n
            kw_loaded = {}

            def load_kw(n_ch):
                P.emit("sync", lambda e, n_ch=n_ch: e.dma_start(
                    out=kwt[:, (n_ch % 2) * D:(n_ch % 2) * D + D],
                    in_=Kwi[n_ch * 128:n_ch * 128 + 128, :]),
                    waits=[(s_pe, st.get("kw_pe_%d" % ((n_ch - 2) % 2), 0))],
                    inc=(s_ld, 16))
                kw_loaded[n_ch] = s_ld.n

            g_loaded = {}

            def load_g(n_ch):
                P.emit("sync", lambda e, n_ch=n_ch: e.dma_start(
                    out=gt2[:, (n_ch % 2) * D:(n_ch % 2) * D + D],
                    in_=Gi[n_ch * 128:n_ch * 128 + 128, :]),
                    waits=[(s_dv, st.get("g_dv_%d" % ((n_ch - 2) % 2), 0))],
                    inc=(s_ld, 16))
                g_loaded[n_ch] = s_ld.n

            load_kw(0)
            load_kw(1)
            load_g(0)
            load_g(1)
            load_x1(0)

            pc_free = [[], []]
            pso_free = [[], []]
            psAB_free = [[], []]
            attw_done = {}
            yg_done = []

            def emit_scores(n_ch):
                ab0 = (n_ch % 2) * 2048
                rb0 = (n_ch % 2) * D
                for c in range(8):
                    kcol = c * T_OWN + n_ch * 128
                    w0 = [(s_ld, res_ld), (s_ac, wmt_done_ac), (s_dv, sin_done)]
                    mmA = P.emit("tensor", lambda e, kcol=kcol: e.matmul(
                        psA[:], Kt[0:64, kcol:kcol + 128], Rt[0:64, kcol:kcol + 128],
                        start=True, stop=True, tile_position=(0, 0)),
                        waits=w0 + psAB_free[0], inc=(s_pe, 1))
                    mmB = P.emit("tensor", lambda e, kcol=kcol: e.matmul(
                        psB[:], Kt[64:128, kcol:kcol + 128], Rt[64:128, kcol:kcol + 128],
                        start=True, stop=True, tile_position=(64, 0)),
                        waits=psAB_free[1], inc=(s_pe, 1))
                    a0 = P.emit("vector", lambda e, c=c, ab0=ab0: e.tensor_mul(
                        out=attwb[:, ab0 + c * 256:ab0 + c * 256 + 128], in0=psA[:],
                        in1=wmt[:, (2 * c) * 128:(2 * c) * 128 + 128]),
                        waits=[(s_pe, mmA), (s_dv, s_dv.n)], inc=(s_dv, 1))
                    a1 = P.emit("vector", lambda e, c=c, ab0=ab0: e.tensor_mul(
                        out=attwb[:, ab0 + c * 256 + 128:ab0 + c * 256 + 256], in0=psB[:],
                        in1=wmt[:, (2 * c + 1) * 128:(2 * c + 1) * 128 + 128]),
                        waits=[(s_pe, mmB)], inc=(s_dv, 1))
                    psAB_free[0] = [(s_dv, a0)]
                    psAB_free[1] = [(s_dv, a1)]
                    P.emit("vector", lambda e, kcol=kcol, c=c, rb0=rb0: e.tensor_mul(
                        out=rwbb[:, rb0 + c * 128:rb0 + c * 128 + 128],
                        in0=Rt[:, kcol:kcol + 128], in1=wbt[:, c * 128:c * 128 + 128]),
                        waits=([(s_dv, s_dv.n)] if SAFE_CHAIN else ()), inc=(s_dv, 1))
                attw_done[n_ch] = s_dv.n

            emit_scores(0)
            for n_ch in range(L2_CHUNKS):
                kk = n_ch // 4
                ygfm_col = (n_ch % 4) * 128
                if n_ch + 1 < L2_CHUNKS and L2_STAGE >= 2:
                    emit_scores(n_ch + 1)
                if L2_STAGE < 2:
                    continue
                ab0 = (n_ch % 2) * 2048
                rb0 = (n_ch % 2) * D
                adn = attw_done[n_ch]
                for c in range(8):
                    vcol = n_ch * D + (2 * c) * 64
                    aw_a = attwb[:, ab0 + c * 256:ab0 + c * 256 + 128]
                    aw_b = attwb[:, ab0 + c * 256 + 128:ab0 + c * 256 + 256]
                    rw = rwbb[:, rb0 + c * 128:rb0 + c * 128 + 128]
                    P.emit("tensor", lambda e, vcol=vcol, aw_a=aw_a: e.matmul(
                        psoA[:], aw_a, Vt[:, vcol:vcol + 64],
                        start=True, stop=False),
                        waits=[(s_dv, adn)] + pso_free[0], inc=None)
                    P.emit("tensor", lambda e, c=c, rw=rw: e.matmul(
                        psoA[:], rw[0:64, :], Sbf[0:64, c * 64:c * 64 + 64],
                        start=False, stop=True, tile_position=(0, 0)), inc=None)
                    P.emit("tensor", lambda e, vcol=vcol, aw_b=aw_b: e.matmul(
                        psoB[:], aw_b, Vt[:, vcol + 64:vcol + 128],
                        start=True, stop=False),
                        waits=pso_free[1], inc=None)
                    P.emit("tensor", lambda e, c=c, rw=rw: e.matmul(
                        psoB[:], rw[64:128, :], Sbf[64:128, c * 64:c * 64 + 64],
                        start=False, stop=True, tile_position=(64, 0)), inc=(s_pe, 1))
                    opn = s_pe.n
                    if L2_STAGE >= 3:
                        kwcol = (n_ch % 2) * D + c * 128
                        pn0 = P.emit("tensor", lambda e, kwcol=kwcol, vcol=vcol: e.matmul(
                            pc0[:], kwt[:, kwcol:kwcol + 64], Vt[:, vcol:vcol + 64],
                            start=True, stop=True),
                            waits=[(s_ld, kw_loaded[n_ch])] + pc_free[0],
                            inc=(s_pe, 1))
                        pn1 = P.emit("tensor", lambda e, kwcol=kwcol, vcol=vcol: e.matmul(
                            pc1[:], kwt[:, kwcol + 64:kwcol + 128], Vt[:, vcol + 64:vcol + 128],
                            start=True, stop=True),
                            waits=pc_free[1], inc=(s_pe, 1))
                        for hh, bank, pn in ((0, pc0, pn0), (1, pc1, pn1)):
                            ssl = Sst[64 * hh:64 * hh + 64, c * 64:c * 64 + 64]
                            wsl = wst[64 * hh:64 * hh + 64, c:c + 1]
                            dn = P.emit("vector", lambda e, ssl=ssl, wsl=wsl, bank=bank:
                                        e.scalar_tensor_tensor(out=ssl, in0=ssl, scalar=wsl,
                                                               in1=bank[:], op0=ALU.mult,
                                                               op1=ALU.add),
                                        waits=[(s_pe, pn), (s_dv, s_dv.n)], inc=(s_dv, 1))
                            pc_free[hh] = [(s_dv, dn)]
                        P.emit("vector", lambda e, c=c: e.tensor_copy(
                            out=Sbf[:, c * 64:c * 64 + 64], in_=Sst[:, c * 64:c * 64 + 64]),
                            waits=([(s_dv, s_dv.n)] if SAFE_CHAIN else ()), inc=(s_dv, 1))
                    if L2_STAGE < 4:
                        pso_free[0] = []
                        pso_free[1] = []
                        continue
                    xo0 = (n_ch % 2) * D
                    e0 = P.emit("scalar", lambda e, c=c, xo0=xo0: e.activation(
                        out=xof[:, xo0 + c * 128:xo0 + c * 128 + 64], in_=psoA[:],
                        func=AF.Copy, scale=0.125),
                        waits=[(s_pe, opn),
                               (s_dv, st.get("gn_dv_%d" % (n_ch % 2), 0))],
                        inc=(s_ac, 1))
                    ev = P.emit("scalar", lambda e, c=c, xo0=xo0: e.activation(
                        out=xof[:, xo0 + c * 128 + 64:xo0 + c * 128 + 128], in_=psoB[:],
                        func=AF.Copy, scale=0.125),
                        inc=(s_ac, 1))
                    pso_free[0] = [(s_ac, e0)]
                    pso_free[1] = [(s_ac, ev)]
                if L2_STAGE < 4:
                    st["kw_pe_%d" % (n_ch % 2)] = s_pe.n
                    st["g_dv_%d" % (n_ch % 2)] = s_dv.n
                    continue
                st["kw_pe_%d" % (n_ch % 2)] = s_pe.n
                evn = s_ac.n
                xo0 = (n_ch % 2) * D
                x3d = xof[:, xo0:xo0 + D].rearrange("p (h k) -> p h k", h=16)
                s3d = gnscr[:, 0:D].rearrange("p (h k) -> p h k", h=16)
                ygt = [ygtm, ygtm2][n_ch % 2]
                sr0 = (n_ch % 2) * 16
                P.emit("vector", lambda e, x3d=x3d, sr0=sr0: e.reduce_sum(
                    out=srow[:, sr0:sr0 + 16], in_=x3d, axis=mybir.AxisListType.X),
                    waits=[(s_ac, evn), (s_dv, s_dv.n)], inc=(s_dv, 1))
                dvn = s_dv.n
                P.emit("scalar", lambda e, sr0=sr0: e.activation(
                    out=srow[:, 32 + sr0:48 + sr0], in_=srow[:, sr0:sr0 + 16],
                    func=AF.Copy, scale=1.0 / 64),
                    waits=[(s_dv, dvn)], inc=(s_ac, 1))
                acn = s_ac.n
                P.emit("vector", lambda e, x3d=x3d, sr0=sr0: e.tensor_tensor(
                    out=x3d, in0=x3d,
                    in1=srow[:, 32 + sr0:48 + sr0].unsqueeze(2).broadcast_to([128, 16, 64]),
                    op=ALU.subtract),
                    waits=[(s_ac, acn)], inc=(s_dv, 1))
                dvn = s_dv.n
                P.emit("vector", lambda e, xo0=xo0: e.tensor_mul(
                    out=gnscr[:, 0:D], in0=xof[:, xo0:xo0 + D],
                    in1=xof[:, xo0:xo0 + D]),
                    waits=([(s_dv, dvn)] if SAFE_CHAIN else ()), inc=(s_dv, 1))
                dvn = s_dv.n
                P.emit("vector", lambda e, s3d=s3d, sr0=sr0: e.reduce_sum(
                    out=srow[:, 64 + sr0:80 + sr0], in_=s3d, axis=mybir.AxisListType.X),
                    waits=([(s_dv, dvn)] if SAFE_CHAIN else ()), inc=(s_dv, 1))
                dvn = s_dv.n
                P.emit("scalar", lambda e, sr0=sr0: e.activation(
                    out=srow[:, 64 + sr0:80 + sr0], in_=srow[:, 64 + sr0:80 + sr0],
                    func=AF.Ln, scale=1.0 / 64, bias=epst[:]),
                    waits=[(s_dv, dvn)], inc=(s_ac, 1))
                acn = s_ac.n
                P.emit("scalar", lambda e, sr0=sr0: e.activation(
                    out=srow[:, 64 + sr0:80 + sr0], in_=srow[:, 64 + sr0:80 + sr0],
                    func=AF.Exp, scale=-0.5),
                    waits=([(s_ac, acn)] if SAFE_CHAIN else ()), inc=(s_ac, 1))
                acn = s_ac.n
                P.emit("vector", lambda e, x3d=x3d, sr0=sr0: e.tensor_tensor(
                    out=x3d, in0=x3d,
                    in1=srow[:, 64 + sr0:80 + sr0].unsqueeze(2).broadcast_to([128, 16, 64]),
                    op=ALU.mult),
                    waits=[(s_ac, acn), (s_dv, s_dv.n)], inc=(s_dv, 1))
                dvn = s_dv.n
                P.emit("vector", lambda e, xo0=xo0: e.tensor_mul(
                    out=xof[:, xo0:xo0 + D], in0=xof[:, xo0:xo0 + D], in1=lnwb[:]),
                       waits=([(s_dv, dvn)] if SAFE_CHAIN else ()), inc=(s_dv, 1))
                dvn = s_dv.n
                P.emit("vector", lambda e, xo0=xo0: e.tensor_add(
                    out=xof[:, xo0:xo0 + D], in0=xof[:, xo0:xo0 + D], in1=lnbb[:]),
                       waits=([(s_dv, dvn)] if SAFE_CHAIN else ()), inc=(s_dv, 1))
                dvn = s_dv.n
                g1 = P.emit("vector", lambda e, n_ch=n_ch, xo0=xo0, ygt=ygt: e.tensor_mul(
                    out=ygt[:], in0=xof[:, xo0:xo0 + D],
                    in1=gt2[:, (n_ch % 2) * D:(n_ch % 2) * D + D]),
                    waits=[(s_dv, dvn), (s_tr, st.get("ygtm_tr_%d" % (n_ch % 2), 0)),
                           (s_ld, g_loaded[n_ch])], inc=(s_dv, 1))
                st["gn_dv_%d" % (n_ch % 2)] = s_dv.n
                st["g_dv_%d" % (n_ch % 2)] = s_dv.n
                if L2_STAGE < 5:
                    continue
                if n_ch + 2 < NCHUNK:
                    load_kw(n_ch + 2)
                    load_g(n_ch + 2)
                ygn = g1
                for c in range(8):
                    P.emit("sync", lambda e, c=c, ygfm_col=ygfm_col, ygt=ygt: e.dma_start(
                        out=ygfm[:, c * 512 + ygfm_col:c * 512 + ygfm_col + 128],
                        in_=ygt[:, c * 128:c * 128 + 128], transpose=True),
                        waits=[(s_dv, ygn), (s_pe, st.get("wo_pe", 0))], inc=(s_tr, 16))
                yg_done.append(s_tr.n)
                st["ygtm_tr_%d" % (n_ch % 2)] = s_tr.n
                if n_ch % 4 == 3:
                    trn = s_tr.n
                    for o in range(8):
                        bank = [ppo0, ppo1][o % 2]
                        for ci in range(8):
                            w0 = ([(s_tr, trn), (s_cp, s_cp.n), (s_ld, x1_loaded[kk])]
                                  + (st.get("ppo_free_%d" % (o % 2)) or [])
                                  if ci == 0 else ())
                            P.emit("tensor", lambda e, bank=bank, ci=ci, o=o: e.matmul(
                                bank[:], wo8[:, ci * D + o * 128:ci * D + o * 128 + 128],
                                ygfm[:, ci * 512:ci * 512 + 512],
                                start=(ci == 0), stop=(ci == 7)),
                                waits=w0, inc=((s_pe, 1) if ci == 7 else None))
                        pen = s_pe.n
                        dn = P.emit("vector", lambda e, bank=bank, o=o: e.tensor_add(
                            out=x2t[:], in0=x1t[:, o * 512:o * 512 + 512], in1=bank[:]),
                            waits=[(s_pe, pen), (s_st, st.get("x2t_st", 0)),
                                   (s_dv, s_dv.n)],
                            inc=(s_dv, 1))
                        st["ppo_free_%d" % (o % 2)] = [(s_dv, dn)]
                        P.emit("sync", lambda e, o=o, kk=kk: e.dma_start(
                            out=x2o[128 * o:128 * o + 128, kk * 512:kk * 512 + 512],
                            in_=x2t[:]),
                            waits=[(s_dv, dn)], inc=(s_st, 16))
                        stn = P.emit("sync", lambda e, o=o, kk=kk: e.dma_start(
                            out=x2d.ap()[128 * o:128 * o + 128, kk * 512:kk * 512 + 512],
                            in_=x2t[:]),
                            waits=[(s_dv, dn)], inc=(s_st, 16))
                        st["x2t_st"] = stn
                    st["wo_pe"] = s_pe.n
                    if kk + 1 < 4:
                        load_x1(kk + 1)
            att_dv, att_ac, att_pe, att_st, att_tr = (s_dv.n, s_ac.n, s_pe.n, s_st.n,
                                                      s_tr.n)

        with ExitStack() as ctxb:
          if L2_FFN:
            ecb = ctxb.enter_context
            C8 = ecb(nc.sbuf_tensor("C8", [128, 32 * TG], BF16))
            acc = ecb(nc.sbuf_tensor("acc", [128, 8 * TG], F32))
            hg = ecb(nc.sbuf_tensor("hg", [128, HG_SLOTS * TG], BF16))
            cpw = ecb(nc.sbuf_tensor("cpw", [128, 2 * 4096], BF16))
            xf = ecb(nc.sbuf_tensor("xf", [128, 2 * 8 * TG], F32))
            xnb = ecb(nc.sbuf_tensor("xnb", [128, 8 * TG], BF16))
            st0 = ecb(nc.psum_tensor("st0", [1, TG], F32))
            st1 = ecb(nc.psum_tensor("st1", [1, TG], F32))
            bc0 = ecb(nc.psum_tensor("bc0", [128, TG], F32))
            bc1 = ecb(nc.psum_tensor("bc1", [128, TG], F32))
            rail0 = ecb(nc.psum_tensor("rail0", [128, TG], F32))
            rail1 = ecb(nc.psum_tensor("rail1", [128, TG], F32))
            cp0 = ecb(nc.psum_tensor("cp0", [128, TG], F32))
            cp1 = ecb(nc.psum_tensor("cp1", [128, TG], F32))
            pb.update({"st0": st0, "st1": st1, "bc0": bc0, "bc1": bc1,
                       "rail0": rail0, "rail1": rail1, "cp0": cp0, "cp1": cp1})

            for eng in ("sync", "gpsimd", "tensor", "vector", "scalar"):
                P.wait(eng, s_dv, att_dv)
                P.wait(eng, s_ac, att_ac)
                P.wait(eng, s_pe, att_pe)
                P.wait(eng, s_st, att_st)
                P.wait(eng, s_tr, att_tr)

            c8_mm = [0, 0]
            for m in range(32):
                rmw = hg[:, (m % 2) * 8192:(m % 2) * 8192 + 128]
                ubw = hg[:, (m % 2) * 8192 + 4096:(m % 2) * 8192 + 4096 + 512]
                w = [(s_pe, c8_mm[m % 2])] if m >= 2 else []
                P.emit("gpsimd", lambda e, m=m, rmw=rmw: e.dma_start(out=rmw, in_=rmbd[m]),
                       waits=w, inc=(s_cp, 16))
                P.emit("gpsimd", lambda e, m=m, ubw=ubw: e.dma_start(out=ubw, in_=ubd[m]),
                       inc=(s_cp, 16))
                cpn = s_cp.n
                bank = [rail0, rail1][m % 2]
                mmn = P.emit("tensor", lambda e, bank=bank, rmw=rmw, ubw=ubw: e.matmul(
                    bank[:], rmw, ubw, start=True, stop=True),
                    waits=[(s_cp, cpn), (s_dv, s_dv.n)], inc=(s_pe, 1))
                c8_mm[m % 2] = mmn
                P.emit("vector", lambda e, m=m, bank=bank: e.tensor_copy(
                    out=C8[:, m * TG:m * TG + TG], in_=bank[:]),
                    waits=[(s_pe, mmn)], inc=(s_dv, 1))

            bufs = {"xf": xf, "xn": xnb, "hg": hg, "cpw": cpw, "acc": acc, "C8": C8,
                    "cpstream": cpstream_d,
                    "scr": {"xb": xb_s, "sq": sq_s, "d": d_s}, "rows": rows}

            tg_list2 = [(k, k * TG, TG) for k in range(4)]

            def ap_x_load2(c, col0, n):
                return x2d.ap()[128 * c:128 * c + 128, col0:col0 + n]

            def post_tg2(t_idx, b, n, res_done):
                k = t_idx
                for c in range(8):
                    P.emit("sync", lambda e, c=c, k=k, b=b: e.dma_start(
                        out=x3o[128 * c:128 * c + 128, k * TG:k * TG + TG],
                        in_=xf[:, (b * 8 + c) * TG:(b * 8 + c) * TG + TG]),
                        waits=[(s_dv, res_done[c])], inc=(s_st, 16))
                st["post_st"] = s_st.n
                st["post_dv"] = max(res_done)

            emit_ffn(P, sem, pb, st, bufs, tg_list2, ap_x_load2, post_tg2, consts)

            P.wait("gpsimd", s_st, s_st.n)
            P.wait("gpsimd", s_tr, s_tr.n)

        with nc.Block() as block:
            P.build(block)

    return nc


def run_l2(inputs, l1_results):
    shared = _prep_shared(inputs)
    Ball = np.stack([np.asarray(l1_results[r]["Bo"], np.float32) for r in range(NCORES)])
    Ball = np.ascontiguousarray(Ball.transpose(2, 0, 1, 3)).reshape(128, 4096)

    in_maps = []
    for i in range(NCORES):
        r = l1_results[i]
        expo = np.zeros((1, 8), np.float32)
        mask8 = np.zeros((1, 8), np.float32)
        for j in range(i):
            expo[0, j] = 2048.0 * (i - 1 - j)
            mask8[0, j] = 1.0
        m = {
            "x1i": np.ascontiguousarray(r["x1o"]),
            "Ri": np.ascontiguousarray(r["Ro"]),
            "Ki": np.ascontiguousarray(r["Ko"]),
            "Gi": np.ascontiguousarray(r["Go"]),
            "Vi": np.ascontiguousarray(r["Vo"]),
            "Kwi": np.ascontiguousarray(r["Kwo"]),
            "Ball": Ball,
            "cpstream": shared["cpstream"],
            "rmbd": shared["rmbd"], "ubd": shared["ubd"],
            "wot": np.ascontiguousarray(np.asarray(inputs["L0_attn_W_o"], np.float32).T),
            "td": shared["td"],
            "tf": np.asarray(inputs["L0_attn_time_faaaa"], np.float32).reshape(1, 16),
            "lnw": np.asarray(inputs["L0_attn_ln_w"], np.float32).reshape(1, D),
            "lnb": np.asarray(inputs["L0_attn_ln_b"], np.float32).reshape(1, D),
            "expo": expo, "mask8": mask8,
        }
        in_maps.append(m)

    nc = build_l2()
    res = run_bass_kernel_spmd(nc, in_maps, core_ids=list(range(NCORES)))
    return res.results


def _prep_shared(inputs):
    W = np.asarray(inputs["L0_ffn_w_cproj"], dtype=np.float32)
    A = W.T.reshape(4, 32, 128, 8, 128).transpose(2, 0, 3, 1, 4)
    cpstream = np.ascontiguousarray(A).reshape(128, 131072)

    rm = np.asarray(inputs["L0_ffn_rail_maker"], dtype=np.float32)
    u = np.asarray(inputs["L0_ffn_rail_up_proj"], dtype=np.float32)
    rmbd = np.zeros((32, 128, 128), dtype=np.float32)
    ubd = np.zeros((32, 128, 512), dtype=np.float32)
    for h in range(4):
        for c in range(8):
            m = h * 8 + c
            for p in range(8):
                r = 8 * c + p
                rmbd[m, 16 * p:16 * p + 16, 16 * p:16 * p + 16] = rm[h, r].T
                ubd[m, 16 * p:16 * p + 16, 64 * p:64 * p + 64] = u[h, r].T

    tm = np.stack([np.asarray(inputs[f"L0_attn_time_mix_{t}"], np.float32).reshape(-1)
                   for t in ("r", "k", "v", "g")])
    tmix = np.ascontiguousarray(tm.reshape(4, 8, 128).transpose(2, 0, 1)).reshape(128, 32)

    return {
        "cpstream": cpstream, "rmbd": rmbd, "ubd": ubd,
        "wrt": np.ascontiguousarray(np.asarray(inputs["L0_attn_W_r"], np.float32).T),
        "wkt": np.ascontiguousarray(np.asarray(inputs["L0_attn_W_k"], np.float32).T),
        "wvt": np.ascontiguousarray(np.asarray(inputs["L0_attn_W_v"], np.float32).T),
        "wgt": np.ascontiguousarray(np.asarray(inputs["L0_attn_W_g"], np.float32).T),
        "tmix": tmix,
        "td": np.asarray(inputs["L0_attn_time_decay"], np.float32).reshape(1, 16),
    }


def run_l1(inputs):
    shared = _prep_shared(inputs)
    x = np.asarray(inputs["L0_attn_xq"], np.float32)[0]
    xT = np.ascontiguousarray(x.T)
    xTp = np.concatenate([np.zeros((D, OVL), np.float32), xT], axis=1)

    in_maps = []
    for i in range(NCORES):
        m = dict(shared)
        m["x0T"] = np.ascontiguousarray(xTp[:, i * T_OWN: i * T_OWN + T_IN])
        in_maps.append(m)

    nc = build_l1()
    res = run_bass_kernel_spmd(nc, in_maps, core_ids=list(range(NCORES)))
    return res.results


def kernel(**inputs):
    l1 = run_l1(inputs)
    l2 = run_l2(inputs, l1)
    x3 = np.concatenate([r["x3o"].T for r in l2], axis=0)
    return x3.reshape(1, 16384, 1024).astype(np.float32)


# version 12
# speedup vs baseline: 1.0073x; 1.0073x over previous
from contextlib import ExitStack

import numpy as np
import ml_dtypes
import concourse.bass as bass
import concourse.mybir as mybir
from concourse.bass_utils import run_bass_kernel_spmd

F32 = mybir.dt.float32
BF16 = mybir.dt.bfloat16
AF = mybir.ActivationFunctionType
ALU = mybir.AluOpType
BF = ml_dtypes.bfloat16

NCORES = 8
T_OWN = 2048
OVL = 128
T_IN = T_OWN + OVL
TG = 512
D = 1024
NHEAD = 16
HK = 64
CH = 128
NCHUNK = T_OWN // CH
EPS = 1e-5
HG_SLOTS = 63

SAFE_CHAIN = True
L2_CHUNKS = NCHUNK
L2_FFN = True
L2_STAGE = 9


class Sem:
    __slots__ = ("h", "n")

    def __init__(self, h):
        self.h = h
        self.n = 0


class Plan:
    ENGINES = ("sync", "tensor", "vector", "scalar", "gpsimd")

    def __init__(self, nc):
        self.nc = nc
        self.steps = {e: [] for e in self.ENGINES}

    def emit(self, eng, fn, waits=(), inc=None):
        ws = tuple((s, v) for (s, v) in waits if v > 0)
        incs = ()
        if inc is not None:
            s, k = inc
            s.n += k
            incs = ((s, k),)
        self.steps[eng].append((fn, ws, incs))
        return incs[0][0].n if incs else None

    def wait(self, eng, sem, val=None):
        v = sem.n if val is None else val
        if v > 0:
            self.steps[eng].append((None, ((sem, v),), ()))

    def build(self, block):
        for eng in self.ENGINES:
            steps = self.steps[eng]
            if not steps:
                continue

            def body(e, steps=steps):
                hw = {}
                for fn, ws, incs in steps:
                    for (s, v) in ws:
                        if v > hw.get(s.h.num, 0):
                            e.wait_ge(s.h, v)
                            hw[s.h.num] = v
                    if fn is None:
                        continue
                    inst = fn(e)
                    for (s, k) in incs:
                        inst = inst.then_inc(s.h, k)

            getattr(block, eng)(body)


def emit_ln(P, sem, pb, st, xf_ap, out_ap, n, scr, rows, consts,
            pre_dv=0, pre_ac=0):
    s_dv, s_ac, s_pe = sem["dv"], sem["ac"], sem["pe"]
    onesb, onesf, epst = consts["ones_bf"], consts["ones_f32"], consts["epst"]
    xb_t, sq_t, d_t = scr["xb"], scr["sq"], scr["d"]
    st0, st1, bc0, bc1 = pb["st0"], pb["st1"], pb["bc0"], pb["bc1"]
    mrow, vrow = rows["m"], rows["v"]

    if pre_dv:
        P.wait("vector", s_dv, pre_dv)
        P.wait("scalar", s_dv, pre_dv)
    if pre_ac:
        P.wait("vector", s_ac, pre_ac)
        P.wait("scalar", s_ac, pre_ac)

    mm_done = []
    for c in range(8):
        xa = xf_ap(c)
        xb = xb_t[:, 0:n]
        sq = sq_t[:, 0:n]
        w_pe = [(s_pe, mm_done[c - 1])] if c >= 1 else []
        P.emit("vector", lambda e, xb=xb, xa=xa: e.tensor_copy(out=xb, in_=xa),
               waits=w_pe, inc=(s_dv, 1))
        dvn = s_dv.n
        P.emit("scalar", lambda e, sq=sq, xa=xa: e.activation(out=sq, in_=xa, func=AF.Square),
               waits=w_pe, inc=(s_ac, 1))
        acn = s_ac.n
        w0 = [(s_dv, dvn)]
        if c == 0:
            w0.append((s_ac, st["stats_free_ac"]))
            w0.append((s_dv, st["stats_free_dv"]))
        P.emit("tensor", lambda e, xb=xb, c=c, n=n: e.matmul(
            st0[:, 0:n], onesb[:], xb, start=(c == 0), stop=(c == 7)), waits=w0)
        mmn = P.emit("tensor", lambda e, sq=sq, c=c, n=n: e.matmul(
            st1[:, 0:n], onesb[:], sq, start=(c == 0), stop=(c == 7)),
            waits=[(s_ac, acn)], inc=(s_pe, 1))
        mm_done.append(mmn)

    pe_n = s_pe.n
    P.emit("scalar", lambda e, n=n: e.activation(out=mrow[:, 0:n], in_=st0[:, 0:n],
                                                 func=AF.Copy, scale=1.0 / D),
           waits=[(s_pe, pe_n)], inc=(s_ac, 1))
    P.emit("scalar", lambda e, n=n: e.activation(out=vrow[:, 0:n], in_=st1[:, 0:n],
                                                 func=AF.Copy, scale=1.0 / D),
           inc=(s_ac, 1))
    acn = s_ac.n
    P.emit("scalar", lambda e, n=n: e.activation(out=st0[:, 0:n], in_=mrow[:, 0:n],
                                                 func=AF.Square),
           waits=([(s_ac, acn)] if SAFE_CHAIN else ()), inc=(s_ac, 1))
    st["stats_free_ac"] = s_ac.n
    acn = s_ac.n
    P.emit("vector", lambda e, n=n: e.tensor_sub(out=vrow[:, 0:n], in0=vrow[:, 0:n],
                                                 in1=st0[:, 0:n]),
           waits=[(s_ac, acn)], inc=(s_dv, 1))
    st["stats_free_dv"] = s_dv.n
    dvn = s_dv.n
    P.emit("scalar", lambda e, n=n: e.activation(out=vrow[:, 0:n], in_=vrow[:, 0:n],
                                                 func=AF.Ln, bias=epst[0:1, :]),
           waits=[(s_dv, dvn)], inc=(s_ac, 1))
    acn = s_ac.n
    P.emit("scalar", lambda e, n=n: e.activation(out=vrow[:, 0:n], in_=vrow[:, 0:n],
                                                 func=AF.Exp, scale=-0.5),
           waits=([(s_ac, acn)] if SAFE_CHAIN else ()), inc=(s_ac, 1))
    acn = s_ac.n
    P.emit("tensor", lambda e, n=n: e.matmul(bc0[:, 0:n], onesf[:], mrow[:, 0:n],
                                             start=True, stop=True),
           waits=[(s_ac, acn), (s_dv, st["bc_free_dv"])])
    P.emit("tensor", lambda e, n=n: e.matmul(bc1[:, 0:n], onesf[:], vrow[:, 0:n],
                                             start=True, stop=True), inc=(s_pe, 1))
    pe_n = s_pe.n
    mul_done = []
    for c in range(8):
        xa = xf_ap(c)
        dd = d_t[:, 0:n]
        w = [(s_pe, pe_n)]
        if c >= 1:
            w.append((s_dv, mul_done[c - 1]))
        P.emit("vector", lambda e, xa=xa, dd=dd, n=n: e.tensor_sub(out=dd, in0=xa,
                                                                   in1=bc0[:, 0:n]),
               waits=w, inc=(s_dv, 1))
        dvn = s_dv.n
        mn = P.emit("vector", lambda e, dd=dd, c=c, n=n: e.tensor_mul(
            out=out_ap(c), in0=dd, in1=bc1[:, 0:n]),
            waits=([(s_dv, dvn)] if SAFE_CHAIN else ()), inc=(s_dv, 1))
        mul_done.append(mn)
    st["bc_free_dv"] = s_dv.n
    return s_dv.n


def emit_ln_single(P, sem, pb, st, xf_ap, out_ap, scr, rows, consts, pre_dv=0):
    s_dv, s_ac, s_pe = sem["dv"], sem["ac"], sem["pe"]
    onesb, onesf, epst = consts["ones_bf"], consts["ones_f32"], consts["epst"]
    st0, st1, bc0, bc1 = pb["st0"], pb["st1"], pb["bc0"], pb["bc1"]
    xb_t, sq_t = scr["xb"], scr["sq"]
    mrow, vrow = rows["m"], rows["v"]
    if pre_dv:
        P.wait("vector", s_dv, pre_dv)
        P.wait("scalar", s_dv, pre_dv)
    for c in range(8):
        xa = xf_ap(c)
        P.emit("vector", lambda e, c=c, xa=xa: e.tensor_copy(out=xb_t[:, c:c + 1], in_=xa),
               inc=(s_dv, 1))
        P.emit("scalar", lambda e, c=c, xa=xa: e.activation(out=sq_t[:, c:c + 1], in_=xa,
                                                            func=AF.Square), inc=(s_ac, 1))
    dvn, acn = s_dv.n, s_ac.n
    for c in range(8):
        w0 = [(s_dv, dvn)]
        if c == 0:
            w0.append((s_ac, st["stats_free_ac"]))
            w0.append((s_dv, st["stats_free_dv"]))
        P.emit("tensor", lambda e, c=c: e.matmul(st0[:, 0:1], onesb[:], xb_t[:, c:c + 1],
                                                 start=(c == 0), stop=(c == 7)), waits=w0)
        P.emit("tensor", lambda e, c=c: e.matmul(st1[:, 0:1], onesb[:], sq_t[:, c:c + 1],
                                                 start=(c == 0), stop=(c == 7)),
               waits=[(s_ac, acn)], inc=((s_pe, 1) if c == 7 else None))
    pe_n = s_pe.n
    P.emit("scalar", lambda e: e.activation(out=mrow[:, 0:1], in_=st0[:, 0:1], func=AF.Copy,
                                            scale=1.0 / D), waits=[(s_pe, pe_n)], inc=(s_ac, 1))
    P.emit("scalar", lambda e: e.activation(out=vrow[:, 0:1], in_=st1[:, 0:1], func=AF.Copy,
                                            scale=1.0 / D), inc=(s_ac, 1))
    st["stats_free_ac"] = s_ac.n
    acn = s_ac.n
    P.emit("scalar", lambda e: e.activation(out=st0[:, 0:1], in_=mrow[:, 0:1], func=AF.Square),
           waits=([(s_ac, acn)] if SAFE_CHAIN else ()), inc=(s_ac, 1))
    acn = s_ac.n
    P.emit("vector", lambda e: e.tensor_sub(out=vrow[:, 0:1], in0=vrow[:, 0:1],
                                            in1=st0[:, 0:1]),
           waits=[(s_ac, acn)], inc=(s_dv, 1))
    dvn = s_dv.n
    P.emit("scalar", lambda e: e.activation(out=vrow[:, 0:1], in_=vrow[:, 0:1], func=AF.Ln,
                                            bias=epst[0:1, :]),
           waits=[(s_dv, dvn)], inc=(s_ac, 1))
    acn = s_ac.n
    P.emit("scalar", lambda e: e.activation(out=vrow[:, 0:1], in_=vrow[:, 0:1], func=AF.Exp,
                                            scale=-0.5),
           waits=([(s_ac, acn)] if SAFE_CHAIN else ()), inc=(s_ac, 1))
    acn = s_ac.n
    P.emit("tensor", lambda e: e.matmul(bc0[:, 0:1], onesf[:], mrow[:, 0:1],
                                        start=True, stop=True),
           waits=[(s_ac, acn), (s_dv, st["bc_free_dv"])])
    P.emit("tensor", lambda e: e.matmul(bc1[:, 0:1], onesf[:], vrow[:, 0:1],
                                        start=True, stop=True), inc=(s_pe, 1))
    pe_n = s_pe.n
    dd = scr["d"][:, 0:1]
    for c in range(8):
        xa = xf_ap(c)
        P.emit("vector", lambda e, xa=xa, dd=dd: e.tensor_sub(out=dd, in0=xa, in1=bc0[:, 0:1]),
               waits=[(s_pe, pe_n), (s_dv, s_dv.n)], inc=(s_dv, 1))
        dvn = s_dv.n
        P.emit("vector", lambda e, c=c, dd=dd: e.tensor_mul(out=out_ap(c), in0=dd,
                                                            in1=bc1[:, 0:1]),
               waits=[(s_dv, dvn)], inc=(s_dv, 1))
    st["bc_free_dv"] = s_dv.n
    return s_dv.n


def emit_ffn(P, sem, pb, st, bufs, tg_list, ap_x_load, post_tg, consts):
    s_ld, s_cp, s_pe = sem["ld"], sem["cp"], sem["pe"]
    s_dv, s_ac, s_kp = sem["dv"], sem["ac"], sem["kp"]
    s_xf = sem["xfree"]
    xf, xn, hg, cpw, acc, C8 = (bufs["xf"], bufs["xn"], bufs["hg"], bufs["cpw"],
                                bufs["acc"], bufs["C8"])
    cpst = bufs["cpstream"]
    pbr = [pb["rail0"], pb["rail1"]]
    pbc = [pb["cp0"], pb["cp1"]]

    kp_base = s_kp.n
    ntg = len(tg_list)

    x_loaded = {}

    def emit_x_load(t_idx):
        if t_idx >= ntg:
            return
        tg_i, col0, n = tg_list[t_idx]
        b = t_idx % 2
        for c in range(8):
            dst = xf[:, (b * 8 + c) * TG:(b * 8 + c) * TG + n]
            w = ([(s_xf, st["xf_base"])] if (t_idx < 2 and st["xf_base"]) else [])
            P.emit("sync", lambda e, dst=dst, c=c, col0=col0, n=n: e.dma_start(
                out=dst, in_=ap_x_load(c, col0, n)), waits=w, inc=(s_ld, 16))
        x_loaded[t_idx] = s_ld.n

    emit_x_load(0)
    emit_x_load(1)

    slab_loaded = []
    for gstep in range(32 * ntg):
        g, o = (gstep % 32) // 8, gstep % 8
        dst = cpw[:, (gstep % 2) * 4096:(gstep % 2) * 4096 + 4096]
        src = cpst[:, (g * 8 + o) * 4096:(g * 8 + o) * 4096 + 4096]
        w = [(s_kp, kp_base + gstep - 1)] if gstep >= 2 else []
        P.emit("gpsimd", lambda e, dst=dst, src=src: e.dma_start(out=dst, in_=src),
               waits=w, inc=(s_cp, 16))
        slab_loaded.append(s_cp.n)

    for t_idx, (tg_i, col0, n) in enumerate(tg_list):
        b = t_idx % 2

        def xap(c, b=b, n=n):
            return xf[:, (b * 8 + c) * TG:(b * 8 + c) * TG + n]

        ldw = x_loaded[t_idx]
        P.wait("vector", s_ld, ldw)
        P.wait("scalar", s_ld, ldw)
        ln_done = emit_ln(P, sem, pb, st, xap,
                          lambda c, n=n: xn[:, c * TG:c * TG + n],
                          n, bufs["scr"], bufs["rows"], consts)
        if t_idx == ntg - 1 and "dbg_ln" in bufs:
            dbg = bufs["dbg_ln"]
            rows_ = bufs["rows"]
            P.emit("sync", lambda e: e.dma_start(out=dbg[0:1, 0:TG], in_=rows_["m"][:, 0:TG]),
                   waits=[(s_dv, ln_done)], inc=(sem["st"], 16))
            P.emit("sync", lambda e: e.dma_start(out=dbg[1:2, 0:TG], in_=rows_["v"][:, 0:TG]),
                   inc=(sem["st"], 16))
            P.emit("sync", lambda e, b=b: e.dma_start(out=dbg[2:130, 0:TG][0:128, :],
                                                      in_=xf[:, b * 8 * TG:b * 8 * TG + TG]),
                   inc=(sem["st"], 16))
            P.emit("gpsimd", lambda e: e.dma_start(out=dbg[130:258, 0:TG][0:128, :],
                                                   in_=xn[:, 0:TG]),
                   waits=[(s_dv, ln_done)], inc=(sem["st"], 16))
            P.emit("sync", lambda e: e.dma_start(out=dbg[0:1, 0:128],
                                                 in_=consts["ones_f32"][:]),
                   inc=(sem["st"], 16))

        rail_j = [0]
        tg_slot0 = st["slot_ctr"]

        def emit_rails(kcnt, t_idx=t_idx, tg_slot0=tg_slot0, n=n, ln_done=ln_done):
            for _ in range(kcnt):
                j = rail_j[0]
                if j >= 128:
                    return
                rail_j[0] += 1
                h, rp = j // 32, j % 32
                c, s = rp // 4, rp % 4
                slot = (tg_slot0 + j) % HG_SLOTS
                bank = pbr[j % 2]
                lhs = C8[:, (h * 8 + c) * TG:(h * 8 + c) * TG + TG]
                lhsT = lhs[32 * s:32 * s + 32, 128 * s:128 * s + 128]
                rhs = xn[32 * s:32 * s + 32, c * TG:c * TG + n]
                w = [(s_dv, ln_done), (s_ac, st["rail_gelu"][j % 2])]
                mmn = P.emit("tensor", lambda e, bank=bank, lhsT=lhsT, rhs=rhs, s=s, n=n:
                             e.matmul(bank[:, 0:n], lhsT, rhs, start=True, stop=True,
                                      tile_position=(32 * s, 0)), waits=w, inc=(s_pe, 1))
                glob_slot_idx = tg_slot0 + j
                wg = [(s_pe, mmn)]
                prev_reader = st["slot_reader_step"][slot]
                if prev_reader > 0:
                    wg.append((s_kp, prev_reader))
                dst = hg[:, slot * TG:slot * TG + n]
                gn = P.emit("scalar", lambda e, dst=dst, bank=bank, n=n: e.activation(
                    out=dst, in_=bank[:, 0:n], func=AF.Gelu), waits=wg, inc=(s_ac, 1))
                st["gelu_done"][slot] = gn
                st["rail_gelu"][j % 2] = gn
            return

        emit_rails(32)
        st["slot_ctr"] += 128

        for g in range(4):
            for o in range(8):
                gstep = t_idx * 32 + g * 8 + o
                bank = pbc[gstep % 2]
                slab = cpw[:, (gstep % 2) * 4096:(gstep % 2) * 4096 + 4096]
                w = [(s_cp, slab_loaded[gstep]), (s_dv, st["cp_bank_dv"][gstep % 2])]
                gmax = 0
                for jj in range(32):
                    slot = (tg_slot0 + g * 32 + jj) % HG_SLOTS
                    gmax = max(gmax, st["gelu_done"][slot])
                w.append((s_ac, gmax))
                for jj in range(32):
                    slot = (tg_slot0 + g * 32 + jj) % HG_SLOTS
                    lhsT = slab[:, jj * 128:jj * 128 + 128]
                    rhs = hg[:, slot * TG:slot * TG + n]
                    P.emit("tensor", lambda e, bank=bank, lhsT=lhsT, rhs=rhs, jj=jj, n=n:
                           e.matmul(bank[:, 0:n], lhsT, rhs, start=(jj == 0),
                                    stop=(jj == 31)),
                           waits=(w if jj == 0 else ()),
                           inc=((s_kp, 1) if jj == 31 else None))
                for jj in range(32):
                    slot = (tg_slot0 + g * 32 + jj) % HG_SLOTS
                    st["slot_reader_step"][slot] = s_kp.n
                emit_rails(4)
                accsl = acc[:, o * TG:o * TG + n]
                if g == 0:
                    fn = lambda e, accsl=accsl, bank=bank, n=n: e.tensor_copy(
                        out=accsl, in_=bank[:, 0:n])
                else:
                    fn = lambda e, accsl=accsl, bank=bank, n=n: e.tensor_add(
                        out=accsl, in0=accsl, in1=bank[:, 0:n])
                wdv = [(s_kp, s_kp.n)]
                if SAFE_CHAIN:
                    wdv.append((s_dv, s_dv.n))
                dn = P.emit("vector", fn, waits=wdv, inc=(s_dv, 1))
                st["cp_bank_dv"][gstep % 2] = dn

        res_done = []
        for c in range(8):
            xa = xap(c)
            accsl = acc[:, c * TG:c * TG + n]
            dn = P.emit("vector", lambda e, xa=xa, accsl=accsl: e.scalar_tensor_tensor(
                out=xa, in0=accsl, scalar=0.5, in1=xa, op0=ALU.mult, op1=ALU.add),
                waits=[(s_dv, s_dv.n)], inc=(s_dv, 1))
            res_done.append(dn)

        post_tg(t_idx, b, n, res_done)
        P.emit("vector", lambda e: e.engine_nop(),
               waits=[(s_dv, st["post_dv"]), (sem["st"], st["post_st"])],
               inc=(s_xf, 1))
        xfn = s_xf.n
        if t_idx + 2 < ntg:
            P.wait("sync", s_xf, xfn)
            emit_x_load(t_idx + 2)

    st["xf_base"] += ntg


def build_l1():
    nc = bass.Bass("TRN2", target_bir_lowering=False)

    x0T = nc.dram_tensor("x0T", [D, T_IN], F32, kind="ExternalInput").ap()
    cpstream_d = nc.dram_tensor("cpstream", [128, 131072], F32, kind="ExternalInput").ap()
    rmbd = nc.dram_tensor("rmbd", [32, 128, 128], F32, kind="ExternalInput").ap()
    ubd = nc.dram_tensor("ubd", [32, 128, 512], F32, kind="ExternalInput").ap()
    wrt = nc.dram_tensor("wrt", [D, D], F32, kind="ExternalInput").ap()
    wkt = nc.dram_tensor("wkt", [D, D], F32, kind="ExternalInput").ap()
    wvt = nc.dram_tensor("wvt", [D, D], F32, kind="ExternalInput").ap()
    wgt = nc.dram_tensor("wgt", [D, D], F32, kind="ExternalInput").ap()
    tmix = nc.dram_tensor("tmix", [128, 32], F32, kind="ExternalInput").ap()
    td = nc.dram_tensor("td", [1, 16], F32, kind="ExternalInput").ap()

    x1o = nc.dram_tensor("x1o", [D, T_OWN], F32, kind="ExternalOutput").ap()
    Ro = nc.dram_tensor("Ro", [D, T_OWN], BF16, kind="ExternalOutput").ap()
    Ko = nc.dram_tensor("Ko", [D, T_OWN], BF16, kind="ExternalOutput").ap()
    Go = nc.dram_tensor("Go", [T_OWN, D], BF16, kind="ExternalOutput").ap()
    Vo = nc.dram_tensor("Vo", [T_OWN, D], BF16, kind="ExternalOutput").ap()
    Kwo = nc.dram_tensor("Kwo", [T_OWN, D], BF16, kind="ExternalOutput").ap()
    Bo = nc.dram_tensor("Bo", [8, 128, 64], F32, kind="ExternalOutput").ap()
    dbgo = nc.dram_tensor("dbgo", [128, 4 * TG], F32, kind="ExternalOutput").ap()
    dbgln = nc.dram_tensor("dbgln", [258, TG], F32, kind="ExternalOutput").ap()

    ones_bf_h = nc.inline_tensor(np.ones((128, 1), dtype=BF), "c_ones_bf")
    ones_f32_h = nc.inline_tensor(np.ones((1, 128), dtype=np.float32), "c_ones_f32")
    epst_h = nc.inline_tensor(np.full((128, 1), EPS, dtype=np.float32), "c_eps")
    riota_h = nc.inline_tensor((127.0 - np.arange(128, dtype=np.float32)).reshape(128, 1),
                               "c_riota")

    with ExitStack() as ctx:
        ec = ctx.enter_context
        ones_bf = ec(nc.sbuf_tensor("ones_bf", [128, 1], BF16))
        ones_f32 = ec(nc.sbuf_tensor("ones_f32", [1, 128], F32))
        epst = ec(nc.sbuf_tensor("epst", [128, 1], F32))
        riota = ec(nc.sbuf_tensor("riota", [128, 1], F32))
        H2 = ec(nc.sbuf_tensor("H2", [128, 8 * 2050], BF16))
        tmixs = ec(nc.sbuf_tensor("tmixs", [128, 32], F32))
        xb_s = ec(nc.sbuf_tensor("xb_s", [128, TG], BF16))
        sq_s = ec(nc.sbuf_tensor("sq_s", [128, TG], BF16))
        d_s = ec(nc.sbuf_tensor("d_s", [128, TG], F32))
        mrow = ec(nc.sbuf_tensor("mrow", [1, TG], F32))
        vrow = ec(nc.sbuf_tensor("vrow", [1, TG], F32))
        st0 = ec(nc.psum_tensor("st0", [1, TG], F32))
        st1 = ec(nc.psum_tensor("st1", [1, TG], F32))
        bc0 = ec(nc.psum_tensor("bc0", [128, TG], F32))
        bc1 = ec(nc.psum_tensor("bc1", [128, TG], F32))
        h_ld = ec(nc.semaphore("s_ld"))
        h_cp = ec(nc.semaphore("s_cp"))
        h_pe = ec(nc.semaphore("s_pe"))
        h_dv = ec(nc.semaphore("s_dv"))
        h_ac = ec(nc.semaphore("s_ac"))
        h_st = ec(nc.semaphore("s_st"))
        h_tr = ec(nc.semaphore("s_tr"))
        h_xf = ec(nc.semaphore("s_xf"))
        h_kp = ec(nc.semaphore("s_kp"))
        P = Plan(nc)
        sem = {"ld": Sem(h_ld), "cp": Sem(h_cp), "pe": Sem(h_pe), "dv": Sem(h_dv),
               "ac": Sem(h_ac), "st": Sem(h_st), "tr": Sem(h_tr), "xfree": Sem(h_xf),
               "kp": Sem(h_kp)}
        s_ld, s_cp, s_pe = sem["ld"], sem["cp"], sem["pe"]
        s_dv, s_ac, s_st, s_tr, s_kp = (sem["dv"], sem["ac"], sem["st"], sem["tr"],
                                        sem["kp"])

        consts = {"ones_bf": ones_bf, "ones_f32": ones_f32, "epst": epst}
        pb = {"st0": st0, "st1": st1, "bc0": bc0, "bc1": bc1}
        st = {"stats_free_ac": 0, "stats_free_dv": 0, "bc_free_dv": 0,
              "xf_base": 0, "slot_ctr": 0,
              "slot_reader_step": [0] * HG_SLOTS, "gelu_done": [0] * HG_SLOTS,
              "rail_gelu": [0, 0], "cp_bank_dv": [0, 0], "post_dv": 0,
              "post_st": 0}
        rows = {"m": mrow, "v": vrow}

        for dst, src in [(ones_bf, ones_bf_h.ap()), (ones_f32, ones_f32_h.ap()),
                         (epst, epst_h.ap()), (riota, riota_h.ap()),
                         (tmixs, tmix)]:
            P.emit("sync", lambda e, dst=dst, src=src: e.dma_start(out=dst[:], in_=src),
                   inc=(s_ld, 16))
        const_ld = s_ld.n

        with ExitStack() as ctx1:
            ec1 = ctx1.enter_context
            C8 = ec1(nc.sbuf_tensor("C8", [128, 32 * TG], BF16))
            acc = ec1(nc.sbuf_tensor("acc", [128, 8 * TG], F32))
            hg = ec1(nc.sbuf_tensor("hg", [128, HG_SLOTS * TG], BF16))
            cpw = ec1(nc.sbuf_tensor("cpw", [128, 2 * 4096], BF16))
            xf = ec1(nc.sbuf_tensor("xf", [128, 2 * 8 * TG], F32))
            xnb = ec1(nc.sbuf_tensor("xnb", [128, 8 * TG], BF16))
            rail0 = ec1(nc.psum_tensor("rail0", [128, TG], F32))
            rail1 = ec1(nc.psum_tensor("rail1", [128, TG], F32))
            cp0 = ec1(nc.psum_tensor("cp0", [128, TG], F32))
            cp1 = ec1(nc.psum_tensor("cp1", [128, TG], F32))
            pb.update({"rail0": rail0, "rail1": rail1, "cp0": cp0, "cp1": cp1})

            rmw = hg[:, 0:128]
            ubw = hg[:, 4096:4096 + 512]
            for m in range(32):
                w = [(s_pe, s_pe.n)] if m >= 1 else []
                P.emit("gpsimd", lambda e, m=m: e.dma_start(out=rmw, in_=rmbd[m]),
                       waits=w, inc=(s_cp, 16))
                P.emit("gpsimd", lambda e, m=m: e.dma_start(out=ubw, in_=ubd[m]),
                       inc=(s_cp, 16))
                cpn = s_cp.n
                bank = [rail0, rail1][m % 2]
                mmn = P.emit("tensor", lambda e, bank=bank: e.matmul(
                    bank[:], rmw, ubw, start=True, stop=True),
                    waits=[(s_cp, cpn), (s_dv, s_dv.n)], inc=(s_pe, 1))
                P.emit("vector", lambda e, m=m, bank=bank: e.tensor_copy(
                    out=C8[:, m * TG:m * TG + TG], in_=bank[:]),
                    waits=[(s_pe, mmn)], inc=(s_dv, 1))

            bufs = {"xf": xf, "xn": xnb, "hg": hg, "cpw": cpw, "acc": acc, "C8": C8,
                    "cpstream": cpstream_d, "dbg_ln": dbgln,
                    "scr": {"xb": xb_s, "sq": sq_s, "d": d_s}, "rows": rows}

            def dump_debug():
                P.emit("gpsimd", lambda e: e.dma_start(out=dbgo[:, 0:TG],
                                                       in_=C8[:, 0:TG]), inc=(s_st, 16))
                P.emit("gpsimd", lambda e: e.dma_start(out=dbgo[:, TG:2*TG],
                                                       in_=xnb[:, 0:TG]), inc=(s_st, 16))
                P.emit("gpsimd", lambda e: e.dma_start(out=dbgo[:, 2*TG:3*TG],
                                                       in_=hg[:, 0:TG]), inc=(s_st, 16))
                P.emit("sync", lambda e: e.dma_start(out=dbgo[:, 3*TG:4*TG],
                                                     in_=acc[:, 0:TG]), inc=(s_st, 16))
                P.emit("sync", lambda e: e.dma_start(out=dbgo[0:1, 2*TG:3*TG],
                                                     in_=mrow[:, 0:TG]), inc=(s_st, 16))
                P.emit("sync", lambda e: e.dma_start(out=dbgo[1:2, 2*TG:3*TG],
                                                     in_=vrow[:, 0:TG]), inc=(s_st, 16))

            tg_list = [(0, 0, OVL)] + [(1 + k, OVL + k * TG, TG) for k in range(4)]

            def ap_x_load(c, col0, n):
                return x0T[128 * c:128 * c + 128, col0:col0 + n]

            def post_tg(t_idx, b, n, res_done):
                def xap1(c, b=b, n=n):
                    return xf[:, (b * 8 + c) * TG:(b * 8 + c) * TG + n]

                st["post_st"] = sem["st"].n
                if t_idx == 0:
                    dn = emit_ln_single(
                        P, sem, pb, st,
                        lambda c, b=b: xf[:, (b * 8 + c) * TG + OVL - 1:
                                          (b * 8 + c) * TG + OVL],
                        lambda c: H2[:, c * 2050:c * 2050 + 1],
                        bufs["scr"], rows, consts, pre_dv=max(res_done))
                    st["post_dv"] = dn
                else:
                    k = t_idx - 1
                    for c in range(8):
                        P.emit("sync", lambda e, c=c, k=k, b=b: e.dma_start(
                            out=x1o[128 * c:128 * c + 128, k * TG:k * TG + TG],
                            in_=xf[:, (b * 8 + c) * TG:(b * 8 + c) * TG + TG]),
                            waits=[(s_dv, res_done[c])], inc=(s_st, 16))
                    st["post_st"] = sem["st"].n
                    dn = emit_ln(
                        P, sem, pb, st, xap1,
                        lambda c, k=k: H2[:, c * 2050 + 1 + k * TG:
                                          c * 2050 + 1 + k * TG + TG],
                        TG, bufs["scr"], rows, consts,
                        pre_dv=max(res_done))
                    st["post_dv"] = dn

            emit_ffn(P, sem, pb, st, bufs, tg_list, ap_x_load, post_tg, consts)
            dump_debug()

            ph_dv, ph_ac, ph_kp, ph_st, ph_pe = s_dv.n, s_ac.n, s_kp.n, s_st.n, s_pe.n

        with ExitStack() as ctx2:
            ec2 = ctx2.enter_context
            Kt = ec2(nc.sbuf_tensor("Kt", [128, 8 * T_OWN], BF16))
            Vt = ec2(nc.sbuf_tensor("Vt", [128, 16 * D], BF16))
            wb8 = ec2(nc.sbuf_tensor("wb8", [128, 2 * 8 * D], BF16))
            dmix = ec2(nc.sbuf_tensor("dmix", [128, 8 * T_OWN], BF16))
            xm = ec2(nc.sbuf_tensor("xm", [128, 2 * 8 * TG], BF16))
            stg = ec2(nc.sbuf_tensor("stg", [128, 4 * TG], BF16))
            ktr = ec2(nc.sbuf_tensor("ktr", [128, 2 * D], BF16))
            kwb = ec2(nc.sbuf_tensor("kwb", [128, 2 * D], BF16))
            pp0 = ec2(nc.psum_tensor("pp0", [128, TG], F32))
            pp1 = ec2(nc.psum_tensor("pp1", [128, TG], F32))
            pc0 = ec2(nc.psum_tensor("pc0", [64, 64], F32))
            pc1 = ec2(nc.psum_tensor("pc1", [64, 64], F32))
            ldrow = ec2(nc.sbuf_tensor("ldrow", [1, 16], F32))
            ldb = ec2(nc.sbuf_tensor("ldb", [128, 16], F32))
            wkp = ec2(nc.sbuf_tensor("wkp", [128, 16], F32))
            wst = ec2(nc.sbuf_tensor("wst", [128, 8], F32))
            ldk = ec2(nc.sbuf_tensor("ldk", [128, 8], F32))
            Sst = ec2(nc.sbuf_tensor("Sst", [128, 8 * 64], F32))

            for eng in ("sync", "gpsimd", "tensor", "vector", "scalar"):
                P.wait(eng, s_dv, ph_dv)
                P.wait(eng, s_ac, ph_ac)
                P.wait(eng, s_kp, ph_kp)
                P.wait(eng, s_st, ph_st)

            P.emit("sync", lambda e: e.dma_start(out=ldrow[:], in_=td), inc=(s_ld, 16))
            ldn = s_ld.n
            P.emit("scalar", lambda e: e.activation(out=ldrow[:], in_=ldrow[:], func=AF.Exp),
                   waits=[(s_ld, ldn)], inc=(s_ac, 1))
            P.emit("vector", lambda e: e.tensor_scalar_mul(out=ldrow[:], in0=ldrow[:],
                                                           scalar1=-1.0),
                   waits=[(s_ac, s_ac.n)], inc=(s_dv, 1))
            P.emit("tensor", lambda e: e.matmul(bc0[:, 0:16], ones_f32[:], ldrow[:],
                                                start=True, stop=True),
                   waits=[(s_dv, s_dv.n), (s_dv, st["bc_free_dv"])], inc=(s_pe, 1))
            P.emit("vector", lambda e: e.tensor_copy(out=ldb[:], in_=bc0[:, 0:16]),
                   waits=[(s_pe, s_pe.n)], inc=(s_dv, 1))
            st["bc_free_dv"] = s_dv.n
            P.emit("vector", lambda e: e.tensor_scalar_mul(out=wkp[:], in0=ldb[:],
                                                           scalar1=riota[:]),
                   waits=([(s_dv, s_dv.n)] if SAFE_CHAIN else ()), inc=(s_dv, 1))
            P.emit("scalar", lambda e: e.activation(out=wkp[:], in_=wkp[:], func=AF.Exp),
                   waits=[(s_dv, s_dv.n)], inc=(s_ac, 1))
            for c in range(8):
                for hh in range(2):
                    sl = slice(64 * hh, 64 * hh + 64)
                    P.emit("vector", lambda e, c=c, hh=hh, sl=sl: e.tensor_copy(
                        out=ldk[sl, c:c + 1], in_=ldb[sl, 2 * c + hh:2 * c + hh + 1]),
                        waits=([(s_dv, s_dv.n)] if SAFE_CHAIN else ()), inc=(s_dv, 1))
            P.emit("scalar", lambda e: e.activation(out=wst[:], in_=ldk[:], func=AF.Exp,
                                                    scale=128.0),
                   waits=[(s_dv, s_dv.n), (s_ac, s_ac.n)], inc=(s_ac, 1))
            P.emit("vector", lambda e: e.memset(Sst[:], 0.0), inc=(s_dv, 1))

            for k in range(4):
                for c in range(8):
                    x1ap = H2[:, c * 2050 + 1 + k * TG: c * 2050 + 1 + k * TG + TG]
                    xxap = H2[:, c * 2050 + 0 + k * TG: c * 2050 + 0 + k * TG + TG]
                    dap = dmix[:, (c * 4 + k) * TG:(c * 4 + k) * TG + TG]
                    P.emit("vector", lambda e, dap=dap, x1ap=x1ap, xxap=xxap:
                           e.tensor_sub(out=dap, in0=x1ap, in1=xxap), inc=(s_dv, 1))
            dmix_done = s_dv.n

            def emit_mix(tgt_i, k, c, guard_pe):
                xxap = H2[:, c * 2050 + 0 + k * TG: c * 2050 + 0 + k * TG + TG]
                dap = dmix[:, (c * 4 + k) * TG:(c * 4 + k) * TG + TG]
                xb0 = (k % 2) * 8 * TG
                out = xm[:, xb0 + c * TG:xb0 + c * TG + TG]
                tm = tmixs[:, tgt_i * 8 + c:tgt_i * 8 + c + 1]
                return P.emit("vector", lambda e, out=out, dap=dap, tm=tm, xxap=xxap:
                              e.scalar_tensor_tensor(out=out, in0=dap, scalar=tm, in1=xxap,
                                                     op0=ALU.mult, op1=ALU.add),
                              waits=[(s_dv, dmix_done), (s_pe, guard_pe)], inc=(s_dv, 1))

            wld_map = {}

            def load_weights(tgt_i, wsrc, guard):
                wb0 = (tgt_i % 2) * 8 * D
                for ci in range(8):
                    P.emit("gpsimd", lambda e, ci=ci, wsrc=wsrc, wb0=wb0: e.dma_start(
                        out=wb8[:, wb0 + ci * D:wb0 + ci * D + D],
                        in_=wsrc[128 * ci:128 * ci + 128, :]),
                        waits=([(s_pe, guard)] if ci == 0 else ()), inc=(s_cp, 16))
                wld_map[tgt_i] = s_cp.n

            pp = [pp0, pp1]
            pp_free = [[], []]
            stg_free_st = [0, 0, 0, 0]

            load_weights(0, wrt, 0)
            load_weights(1, wkt, 0)
            xm_free = [0, 0]
            for tgt_i, (wsrc, resident, dram_out) in enumerate(
                    [(wrt, None, Ro), (wkt, Kt, Ko)]):
                wld = wld_map[tgt_i]
                wb0 = (tgt_i % 2) * 8 * D
                for k in range(4):
                    for c in range(8):
                        emit_mix(tgt_i, k, c, xm_free[k % 2])
                    mixn = s_dv.n
                    for o in range(8):
                        stp = tgt_i * 32 + k * 8 + o
                        bank = pp[stp % 2]
                        for ci in range(8):
                            lhsT = wb8[:, wb0 + ci * D + o * 128: wb0 + ci * D + o * 128 + 128]
                            rhs = xm[:, (k % 2) * 8 * TG + ci * TG:(k % 2) * 8 * TG + ci * TG + TG]
                            w0 = ([(s_cp, wld), (s_dv, mixn)] + pp_free[stp % 2]
                                  if ci == 0 else ())
                            P.emit("tensor", lambda e, bank=bank, lhsT=lhsT, rhs=rhs,
                                   ci=ci: e.matmul(bank[:], lhsT, rhs, start=(ci == 0),
                                                   stop=(ci == 7)),
                                   waits=w0, inc=((s_pe, 1) if ci == 7 else None))
                        pen = s_pe.n
                        if o == 7:
                            xm_free[k % 2] = pen
                        if resident is not None:
                            dst = resident[:, o * T_OWN + k * TG: o * T_OWN + k * TG + TG]
                            dn = P.emit("vector", lambda e, dst=dst, bank=bank:
                                        e.tensor_copy(out=dst, in_=bank[:]),
                                        waits=[(s_pe, pen)], inc=(s_dv, 1))
                            P.emit("sync", lambda e, dst=dst, o=o, k=k,
                                   dram_out=dram_out: e.dma_start(
                                       out=dram_out[128 * o:128 * o + 128,
                                                    k * TG:k * TG + TG], in_=dst),
                                   waits=[(s_dv, dn)], inc=(s_st, 16))
                            pp_free[stp % 2] = [(s_dv, dn)]
                        else:
                            sslot = stp % 4
                            dst = stg[:, sslot * TG:sslot * TG + TG]
                            dn = P.emit("vector", lambda e, dst=dst, bank=bank:
                                        e.tensor_copy(out=dst, in_=bank[:]),
                                        waits=[(s_pe, pen),
                                               (s_st, stg_free_st[sslot])],
                                        inc=(s_dv, 1))
                            stn = P.emit("sync", lambda e, dst=dst, o=o, k=k,
                                         dram_out=dram_out: e.dma_start(
                                             out=dram_out[128 * o:128 * o + 128,
                                                          k * TG:k * TG + TG], in_=dst),
                                         waits=[(s_dv, dn)], inc=(s_st, 16))
                            stg_free_st[sslot] = stn
                            pp_free[stp % 2] = [(s_dv, dn)]

            load_weights(2, wvt, wld_map[0] * 0 + s_pe.n)
            for tgt_i, (wsrc, resident, dram_out, act) in enumerate(
                    [(wvt, Vt, Vo, None), (wgt, None, Go, AF.Silu)], start=2):
                if tgt_i == 3:
                    load_weights(3, wgt, s_pe.n)
                wld = wld_map[tgt_i]
                wb0 = (tgt_i % 2) * 8 * D
                for k in range(4):
                    for c in range(8):
                        emit_mix(tgt_i, k, c, xm_free[k % 2])
                    mixn = s_dv.n
                    for tt in range(4):
                        t_glob = k * 4 + tt
                        for oh in range(2):
                            stp = tt * 2 + oh
                            bank = pp[stp % 2]
                            for ci in range(8):
                                lhsT = xm[:, (k % 2) * 8 * TG + ci * TG + tt * 128:
                                          (k % 2) * 8 * TG + ci * TG + tt * 128 + 128]
                                rhs = wb8[:, wb0 + ci * D + oh * 512: wb0 + ci * D + oh * 512 + 512]
                                w0 = ([(s_cp, wld), (s_dv, mixn)] + pp_free[stp % 2]
                                      if ci == 0 else ())
                                P.emit("tensor", lambda e, bank=bank, lhsT=lhsT, rhs=rhs,
                                       ci=ci: e.matmul(bank[:], lhsT, rhs,
                                                       start=(ci == 0), stop=(ci == 7)),
                                       waits=w0, inc=((s_pe, 1) if ci == 7 else None))
                            pen = s_pe.n
                            if tt == 3 and oh == 1:
                                xm_free[k % 2] = pen
                            if resident is not None:
                                dst = resident[:, t_glob * D + oh * 512:
                                               t_glob * D + oh * 512 + 512]
                                dn = P.emit("vector", lambda e, dst=dst, bank=bank:
                                            e.tensor_copy(out=dst, in_=bank[:]),
                                            waits=[(s_pe, pen)], inc=(s_dv, 1))
                                P.emit("sync", lambda e, dst=dst, t_glob=t_glob, oh=oh,
                                       dram_out=dram_out: e.dma_start(
                                           out=dram_out[t_glob * 128:t_glob * 128 + 128,
                                                        oh * 512:oh * 512 + 512],
                                           in_=dst),
                                       waits=[(s_dv, dn)], inc=(s_st, 16))
                                pp_free[stp % 2] = [(s_dv, dn)]
                            else:
                                sslot = stp % 4
                                dst = stg[:, sslot * TG:sslot * TG + TG]
                                an = P.emit("scalar", lambda e, dst=dst, bank=bank:
                                            e.activation(out=dst, in_=bank[:],
                                                         func=AF.Silu),
                                            waits=[(s_pe, pen),
                                                   (s_st, stg_free_st[sslot])],
                                            inc=(s_ac, 1))
                                stn = P.emit("sync", lambda e, dst=dst, t_glob=t_glob,
                                             oh=oh, dram_out=dram_out: e.dma_start(
                                                 out=dram_out[
                                                     t_glob * 128:t_glob * 128 + 128,
                                                     oh * 512:oh * 512 + 512], in_=dst),
                                             waits=[(s_ac, an)], inc=(s_st, 16))
                                stg_free_st[sslot] = stn
                                pp_free[stp % 2] = [(s_ac, an)]

            kv_done_dv = s_dv.n

            kw_ready = []
            chunk_pe = []
            chunk_st = []
            pc_free = [[], []]
            for n_ch in range(NCHUNK):
                nb = n_ch % 2
                for c in range(8):
                    src = Kt[:, c * T_OWN + n_ch * 128: c * T_OWN + n_ch * 128 + 128]
                    dst = ktr[:, nb * D + c * 128: nb * D + c * 128 + 128]
                    w = ([(s_dv, kw_ready[n_ch - 2])] if n_ch >= 2
                         else [(s_dv, kv_done_dv)])
                    P.emit("sync", lambda e, dst=dst, src=src: e.dma_start(
                        out=dst, in_=src, transpose=True), waits=w, inc=(s_tr, 16))
                trn = s_tr.n
                for c in range(8):
                    for hh in range(2):
                        src = ktr[:, nb * D + c * 128 + 64 * hh:
                                  nb * D + c * 128 + 64 * hh + 64]
                        dst = kwb[:, nb * D + c * 128 + 64 * hh:
                                  nb * D + c * 128 + 64 * hh + 64]
                        sc = wkp[:, 2 * c + hh:2 * c + hh + 1]
                        w2 = [(s_tr, trn)]
                        if n_ch >= 2:
                            w2 += [(s_pe, chunk_pe[n_ch - 2]), (s_st, chunk_st[n_ch - 2])]
                        P.emit("vector", lambda e, dst=dst, src=src, sc=sc:
                               e.tensor_scalar_mul(out=dst, in0=src, scalar1=sc),
                               waits=w2, inc=(s_dv, 1))
                kw_ready.append(s_dv.n)
                P.emit("sync", lambda e, nb=nb, n_ch=n_ch: e.dma_start(
                    out=Kwo[n_ch * 128:n_ch * 128 + 128, :],
                    in_=kwb[:, nb * D:nb * D + D]),
                    waits=[(s_dv, kw_ready[-1])], inc=(s_st, 16))
                chunk_st.append(s_st.n)
                for c in range(8):
                    pens = []
                    for hh, bank in ((0, pc0), (1, pc1)):
                        lhsT = kwb[:, nb * D + c * 128 + 64 * hh:
                                   nb * D + c * 128 + 64 * hh + 64]
                        rhs = Vt[:, n_ch * D + (2 * c + hh) * 64:
                                 n_ch * D + (2 * c + hh) * 64 + 64]
                        pn = P.emit("tensor", lambda e, bank=bank, lhsT=lhsT, rhs=rhs:
                                    e.matmul(bank[:], lhsT, rhs, start=True, stop=True),
                                    waits=[(s_dv, kw_ready[-1])] + pc_free[hh],
                                    inc=(s_pe, 1))
                        pens.append(pn)
                    for hh, bank in ((0, pc0), (1, pc1)):
                        ssl = Sst[64 * hh:64 * hh + 64, c * 64:c * 64 + 64]
                        wsl = wst[64 * hh:64 * hh + 64, c:c + 1]
                        dn = P.emit("vector", lambda e, ssl=ssl, wsl=wsl, bank=bank:
                                    e.scalar_tensor_tensor(out=ssl, in0=ssl, scalar=wsl,
                                                           in1=bank[:], op0=ALU.mult,
                                                           op1=ALU.add),
                                    waits=[(s_pe, pens[hh]), (s_dv, s_dv.n)],
                                    inc=(s_dv, 1))
                        pc_free[hh] = [(s_dv, dn)]
                chunk_pe.append(s_pe.n)

            bdone = s_dv.n
            for c in range(8):
                P.emit("sync", lambda e, c=c: e.dma_start(
                    out=Bo[c], in_=Sst[:, c * 64:c * 64 + 64]),
                    waits=[(s_dv, bdone)], inc=(s_st, 16))

            P.wait("gpsimd", s_st, s_st.n)
            P.wait("gpsimd", s_tr, s_tr.n)

        with nc.Block() as block:
            P.build(block)

    return nc



def build_l2():
    nc = bass.Bass("TRN2", target_bir_lowering=False)

    x1i = nc.dram_tensor("x1i", [D, T_OWN], F32, kind="ExternalInput").ap()
    Ri = nc.dram_tensor("Ri", [D, T_OWN], BF16, kind="ExternalInput").ap()
    Ki = nc.dram_tensor("Ki", [D, T_OWN], BF16, kind="ExternalInput").ap()
    Gi = nc.dram_tensor("Gi", [T_OWN, D], BF16, kind="ExternalInput").ap()
    Vi = nc.dram_tensor("Vi", [T_OWN, D], BF16, kind="ExternalInput").ap()
    Kwi = nc.dram_tensor("Kwi", [T_OWN, D], BF16, kind="ExternalInput").ap()
    Ball = nc.dram_tensor("Ball", [128, 4096], F32, kind="ExternalInput").ap()
    cpstream_d = nc.dram_tensor("cpstream", [128, 131072], F32, kind="ExternalInput").ap()
    rmbd = nc.dram_tensor("rmbd", [32, 128, 128], F32, kind="ExternalInput").ap()
    ubd = nc.dram_tensor("ubd", [32, 128, 512], F32, kind="ExternalInput").ap()
    wot = nc.dram_tensor("wot", [D, D], F32, kind="ExternalInput").ap()
    td = nc.dram_tensor("td", [1, 16], F32, kind="ExternalInput").ap()
    tf = nc.dram_tensor("tf", [1, 16], F32, kind="ExternalInput").ap()
    lnw_d = nc.dram_tensor("lnw", [1, D], F32, kind="ExternalInput").ap()
    lnb_d = nc.dram_tensor("lnb", [1, D], F32, kind="ExternalInput").ap()
    expo_d = nc.dram_tensor("expo", [1, 8], F32, kind="ExternalInput").ap()
    mask8_d = nc.dram_tensor("mask8", [1, 8], F32, kind="ExternalInput").ap()

    x3o = nc.dram_tensor("x3o", [D, T_OWN], F32, kind="ExternalOutput").ap()
    x2o = nc.dram_tensor("x2o", [D, T_OWN], F32, kind="ExternalOutput").ap()
    dbg2 = nc.dram_tensor("dbg2", [128, 4 * TG], F32, kind="ExternalOutput").ap()

    x2d = nc.dram_tensor("x2d", [D, T_OWN], F32, kind="Internal")

    ones_bf_h = nc.inline_tensor(np.ones((128, 1), dtype=BF), "c_ones_bf")
    ones_f32_h = nc.inline_tensor(np.ones((1, 128), dtype=np.float32), "c_ones_f32")
    epst_h = nc.inline_tensor(np.full((128, 1), EPS, dtype=np.float32), "c_eps")
    ii = np.arange(128, dtype=np.float32)
    d1t_h = nc.inline_tensor(np.maximum(ii[None, :] - ii[:, None] - 1.0, 0.0), "c_d1t")
    maskt_h = nc.inline_tensor((ii[None, :] > ii[:, None]).astype(np.float32), "c_maskt")
    iotaf_h = nc.inline_tensor(np.broadcast_to(ii[None, :], (128, 128)).copy(), "c_iotaf")
    eye_h = nc.inline_tensor(np.eye(128, dtype=np.float32), "c_eye")

    with ExitStack() as ctx:
        ec = ctx.enter_context
        ones_bf = ec(nc.sbuf_tensor("ones_bf", [128, 1], BF16))
        ones_f32 = ec(nc.sbuf_tensor("ones_f32", [1, 128], F32))
        epst = ec(nc.sbuf_tensor("epst", [128, 1], F32))
        xb_s = ec(nc.sbuf_tensor("xb_s", [128, TG], BF16))
        sq_s = ec(nc.sbuf_tensor("sq_s", [128, TG], BF16))
        d_s = ec(nc.sbuf_tensor("d_s", [128, TG], F32))
        mrow = ec(nc.sbuf_tensor("mrow", [1, TG], F32))
        vrow = ec(nc.sbuf_tensor("vrow", [1, TG], F32))
        h_ld = ec(nc.semaphore("s_ld"))
        h_cp = ec(nc.semaphore("s_cp"))
        h_pe = ec(nc.semaphore("s_pe"))
        h_dv = ec(nc.semaphore("s_dv"))
        h_ac = ec(nc.semaphore("s_ac"))
        h_st = ec(nc.semaphore("s_st"))
        h_tr = ec(nc.semaphore("s_tr"))
        h_xf = ec(nc.semaphore("s_xf"))
        h_kp = ec(nc.semaphore("s_kp"))

        P = Plan(nc)
        sem = {"ld": Sem(h_ld), "cp": Sem(h_cp), "pe": Sem(h_pe), "dv": Sem(h_dv),
               "ac": Sem(h_ac), "st": Sem(h_st), "tr": Sem(h_tr), "xfree": Sem(h_xf),
               "kp": Sem(h_kp)}
        s_ld, s_cp, s_pe = sem["ld"], sem["cp"], sem["pe"]
        s_dv, s_ac, s_st, s_tr, s_kp = (sem["dv"], sem["ac"], sem["st"], sem["tr"],
                                        sem["kp"])
        consts = {"ones_bf": ones_bf, "ones_f32": ones_f32, "epst": epst}
        pb = {}
        st = {"stats_free_ac": 0, "stats_free_dv": 0, "bc_free_dv": 0,
              "xf_base": 0, "slot_ctr": 0,
              "slot_reader_step": [0] * HG_SLOTS, "gelu_done": [0] * HG_SLOTS,
              "rail_gelu": [0, 0], "cp_bank_dv": [0, 0], "post_dv": 0,
              "post_st": 0, "prev_g1": 0, "ygtm_tr": 0}
        rows = {"m": mrow, "v": vrow}

        for dst, src in [(ones_bf, ones_bf_h.ap()), (ones_f32, ones_f32_h.ap()),
                         (epst, epst_h.ap())]:
            P.emit("sync", lambda e, dst=dst, src=src: e.dma_start(out=dst[:], in_=src),
                   inc=(s_ld, 16))

        with ExitStack() as ctxa:
            eca = ctxa.enter_context
            Rt = eca(nc.sbuf_tensor("Rt", [128, 8 * T_OWN], BF16))
            Kt = eca(nc.sbuf_tensor("Kt", [128, 8 * T_OWN], BF16))
            Vt = eca(nc.sbuf_tensor("Vt", [128, 16 * D], BF16))
            gt2 = eca(nc.sbuf_tensor("gt2", [128, 2 * D], BF16))
            kwt = eca(nc.sbuf_tensor("kwt", [128, 2 * D], BF16))
            wmt = eca(nc.sbuf_tensor("wmt", [128, 16 * 128], BF16))
            wbt = eca(nc.sbuf_tensor("wbt", [128, 8 * 128], BF16))
            lnwb = eca(nc.sbuf_tensor("lnwb", [128, D], BF16))
            lnbb = eca(nc.sbuf_tensor("lnbb", [128, D], BF16))
            lnwrow = eca(nc.sbuf_tensor("lnwrow", [1, D], F32))
            lnbrow = eca(nc.sbuf_tensor("lnbrow", [1, D], F32))
            d1t = eca(nc.sbuf_tensor("d1t", [128, 128], F32))
            maskt = eca(nc.sbuf_tensor("maskt", [128, 128], F32))
            iotaf = eca(nc.sbuf_tensor("iotaf", [128, 128], F32))
            eyet = eca(nc.sbuf_tensor("eyet", [128, 128], F32))
            ldrow = eca(nc.sbuf_tensor("ldrow", [1, 16], F32))
            ufrow = eca(nc.sbuf_tensor("ufrow", [1, 16], F32))
            ldb = eca(nc.sbuf_tensor("ldb", [128, 16], F32))
            ufb = eca(nc.sbuf_tensor("ufb", [128, 16], F32))
            ldk = eca(nc.sbuf_tensor("ldk", [128, 8], F32))
            wst = eca(nc.sbuf_tensor("wst", [128, 8], F32))
            Sst = eca(nc.sbuf_tensor("Sst", [128, 8 * 64], F32))
            Sbf = eca(nc.sbuf_tensor("Sbf", [128, 8 * 64], BF16))
            exb = eca(nc.sbuf_tensor("exb", [128, 8], F32))
            mkb = eca(nc.sbuf_tensor("mkb", [128, 8], F32))
            cofb = eca(nc.sbuf_tensor("cofb", [128, 8 * 8], F32))
            wo8 = eca(nc.sbuf_tensor("wo8", [128, 8 * D], BF16))
            attwb = eca(nc.sbuf_tensor("attwb", [128, 2 * 2048], BF16))
            rwbb = eca(nc.sbuf_tensor("rwbb", [128, 2 * D], BF16))
            xof = eca(nc.sbuf_tensor("xof", [128, 2 * D], F32))
            gnscr = eca(nc.sbuf_tensor("gnscr", [128, D], F32))
            ygtm2 = eca(nc.sbuf_tensor("ygtm2", [128, D], BF16))
            srow = eca(nc.sbuf_tensor("srow", [128, 96], F32))
            ygtm = eca(nc.sbuf_tensor("ygtm", [128, D], BF16))
            ygfm = eca(nc.sbuf_tensor("ygfm", [128, 8 * 512], BF16))
            x1t = eca(nc.sbuf_tensor("x1t", [128, 8 * 512], F32))
            x2t = eca(nc.sbuf_tensor("x2t", [128, 512], F32))
            setup_ctx = ExitStack()
            bc0 = setup_ctx.enter_context(nc.psum_tensor("bcA0", [128, TG], F32))
            bc1 = setup_ctx.enter_context(nc.psum_tensor("bcA1", [128, TG], F32))

            P.emit("sync", lambda e: e.dma_start(out=d1t[:], in_=d1t_h.ap()), inc=(s_ld, 16))
            P.emit("sync", lambda e: e.dma_start(out=maskt[:], in_=maskt_h.ap()), inc=(s_ld, 16))
            P.emit("sync", lambda e: e.dma_start(out=iotaf[:], in_=iotaf_h.ap()), inc=(s_ld, 16))
            P.emit("sync", lambda e: e.dma_start(out=eyet[:], in_=eye_h.ap()), inc=(s_ld, 16))
            P.emit("sync", lambda e: e.dma_start(out=ldrow[:], in_=td), inc=(s_ld, 16))
            P.emit("sync", lambda e: e.dma_start(out=ufrow[:], in_=tf), inc=(s_ld, 16))
            P.emit("sync", lambda e: e.dma_start(out=lnwrow[:], in_=lnw_d), inc=(s_ld, 16))
            P.emit("sync", lambda e: e.dma_start(out=lnbrow[:], in_=lnb_d), inc=(s_ld, 16))
            P.emit("sync", lambda e: e.dma_start(out=exb[0:1, :], in_=expo_d), inc=(s_ld, 16))
            P.emit("sync", lambda e: e.dma_start(out=mkb[0:1, :], in_=mask8_d), inc=(s_ld, 16))
            for c in range(8):
                P.emit("sync", lambda e, c=c: e.dma_start(
                    out=Rt[:, c * T_OWN:(c + 1) * T_OWN], in_=Ri[128 * c:128 * c + 128, :]),
                    inc=(s_ld, 16))
                P.emit("sync", lambda e, c=c: e.dma_start(
                    out=Kt[:, c * T_OWN:(c + 1) * T_OWN], in_=Ki[128 * c:128 * c + 128, :]),
                    inc=(s_ld, 16))
            for tt in range(16):
                P.emit("sync", lambda e, tt=tt: e.dma_start(
                    out=Vt[:, tt * D:(tt + 1) * D], in_=Vi[tt * 128:tt * 128 + 128, :]),
                    inc=(s_ld, 16))
            for c in range(8):
                P.emit("gpsimd", lambda e, c=c: e.dma_start(
                    out=wo8[:, c * D:(c + 1) * D], in_=wot[128 * c:128 * c + 128, :]),
                    inc=(s_cp, 16))
            const_ld = s_ld.n

            P.emit("scalar", lambda e: e.activation(out=ldrow[:], in_=ldrow[:], func=AF.Exp),
                   waits=[(s_ld, const_ld)], inc=(s_ac, 1))
            P.emit("vector", lambda e: e.tensor_scalar_mul(out=ldrow[:], in0=ldrow[:],
                                                           scalar1=-1.0),
                   waits=[(s_ac, s_ac.n)], inc=(s_dv, 1))
            P.emit("tensor", lambda e: e.matmul(bc0[:, 0:16], ones_f32[:], ldrow[:],
                                                start=True, stop=True),
                   waits=[(s_dv, s_dv.n)], inc=(s_pe, 1))
            P.emit("tensor", lambda e: e.matmul(bc1[:, 0:16], ones_f32[:], ufrow[:],
                                                start=True, stop=True), inc=(s_pe, 1))
            P.emit("vector", lambda e: e.tensor_copy(out=ldb[:], in_=bc0[:, 0:16]),
                   waits=[(s_pe, s_pe.n)], inc=(s_dv, 1))
            P.emit("vector", lambda e: e.tensor_copy(out=ufb[:], in_=bc1[:, 0:16]),
                   waits=([(s_dv, s_dv.n)] if SAFE_CHAIN else ()), inc=(s_dv, 1))
            st["bc_free_dv"] = s_dv.n
            for c in range(8):
                for hh in range(2):
                    sl = slice(64 * hh, 64 * hh + 64)
                    P.emit("vector", lambda e, c=c, hh=hh, sl=sl: e.tensor_copy(
                        out=ldk[sl, c:c + 1], in_=ldb[sl, 2 * c + hh:2 * c + hh + 1]),
                        waits=([(s_dv, s_dv.n)] if SAFE_CHAIN else ()), inc=(s_dv, 1))
            P.emit("scalar", lambda e: e.activation(out=wst[:], in_=ldk[:], func=AF.Exp,
                                                    scale=128.0),
                   waits=[(s_dv, s_dv.n), (s_ac, s_ac.n)], inc=(s_ac, 1))
            for h in range(16):
                dst = wmt[:, h * 128:(h + 1) * 128]
                P.emit("vector", lambda e, dst=dst, h=h: e.tensor_scalar_mul(
                    out=dst, in0=d1t[:], scalar1=ldb[:, h:h + 1]),
                    waits=[(s_dv, s_dv.n)], inc=(s_dv, 1))
                P.emit("scalar", lambda e, dst=dst: e.activation(out=dst, in_=dst,
                                                                 func=AF.Exp),
                       waits=[(s_dv, s_dv.n), (s_ac, s_ac.n)], inc=(s_ac, 1))
                P.emit("vector", lambda e, dst=dst: e.tensor_mul(out=dst, in0=dst,
                                                                 in1=maskt[:]),
                       waits=[(s_ac, s_ac.n)], inc=(s_dv, 1))
                P.emit("vector", lambda e, h=h: e.tensor_scalar_mul(
                    out=gnscr[:, 0:128], in0=eyet[:], scalar1=ufb[:, h:h + 1]),
                    waits=([(s_dv, s_dv.n)] if SAFE_CHAIN else ()), inc=(s_dv, 1))
                P.emit("vector", lambda e, dst=dst: e.tensor_add(out=dst, in0=dst,
                                                                 in1=gnscr[:, 0:128]),
                       waits=[(s_dv, s_dv.n)], inc=(s_dv, 1))
            for c in range(8):
                dst = wbt[:, c * 128:(c + 1) * 128]
                P.emit("vector", lambda e, dst=dst, c=c: e.tensor_scalar_mul(
                    out=dst, in0=iotaf[:], scalar1=ldk[:, c:c + 1]),
                    waits=[(s_dv, s_dv.n)], inc=(s_dv, 1))
                P.emit("scalar", lambda e, dst=dst: e.activation(out=dst, in_=dst,
                                                                 func=AF.Exp),
                       waits=[(s_dv, s_dv.n), (s_ac, s_ac.n)], inc=(s_ac, 1))
            wmt_done_ac = s_ac.n
            for half in range(2):
                P.emit("tensor", lambda e, half=half: e.matmul(
                    bc0[:, 0:512], ones_f32[:], lnwrow[0:1, half * 512:half * 512 + 512],
                    start=True, stop=True),
                    waits=[(s_ld, const_ld), (s_dv, st["bc_free_dv"])], inc=(s_pe, 1))
                P.emit("tensor", lambda e, half=half: e.matmul(
                    bc1[:, 0:512], ones_f32[:], lnbrow[0:1, half * 512:half * 512 + 512],
                    start=True, stop=True), inc=(s_pe, 1))
                pen = s_pe.n
                P.emit("vector", lambda e, half=half: e.tensor_copy(
                    out=lnwb[:, half * 512:half * 512 + 512], in_=bc0[:, 0:512]),
                    waits=[(s_pe, pen)], inc=(s_dv, 1))
                P.emit("vector", lambda e, half=half: e.tensor_copy(
                    out=lnbb[:, half * 512:half * 512 + 512], in_=bc1[:, 0:512]),
                    waits=([(s_dv, s_dv.n)] if SAFE_CHAIN else ()), inc=(s_dv, 1))
                st["bc_free_dv"] = s_dv.n
            P.emit("tensor", lambda e: e.matmul(bc0[:, 0:8], ones_f32[:], exb[0:1, :],
                                                start=True, stop=True),
                   waits=[(s_dv, st["bc_free_dv"])], inc=(s_pe, 1))
            P.emit("tensor", lambda e: e.matmul(bc1[:, 0:8], ones_f32[:], mkb[0:1, :],
                                                start=True, stop=True), inc=(s_pe, 1))
            pen = s_pe.n
            P.emit("vector", lambda e: e.tensor_copy(out=exb[:], in_=bc0[:, 0:8]),
                   waits=[(s_pe, pen)], inc=(s_dv, 1))
            P.emit("vector", lambda e: e.tensor_copy(out=mkb[:], in_=bc1[:, 0:8]),
                   waits=([(s_dv, s_dv.n)] if SAFE_CHAIN else ()), inc=(s_dv, 1))
            st["bc_free_dv"] = s_dv.n
            for c in range(8):
                dst = cofb[:, c * 8:(c + 1) * 8]
                P.emit("vector", lambda e, dst=dst, c=c: e.tensor_scalar_mul(
                    out=dst, in0=exb[:], scalar1=ldk[:, c:c + 1]),
                    waits=[(s_dv, s_dv.n)], inc=(s_dv, 1))
                P.emit("scalar", lambda e, dst=dst: e.activation(out=dst, in_=dst,
                                                                 func=AF.Exp),
                       waits=[(s_dv, s_dv.n), (s_ac, s_ac.n)], inc=(s_ac, 1))
                P.emit("vector", lambda e, dst=dst: e.tensor_mul(out=dst, in0=dst,
                                                                 in1=mkb[:]),
                       waits=[(s_ac, s_ac.n)], inc=(s_dv, 1))
            bts = x1t
            P.emit("sync", lambda e: e.dma_start(out=bts[:], in_=Ball), inc=(s_ld, 16))
            ldn = s_ld.n
            for c in range(8):
                for r in range(8):
                    bsl = bts[:, (r * 8 + c) * 64:(r * 8 + c) * 64 + 64]
                    ssl = Sst[:, c * 64:c * 64 + 64]
                    cf = cofb[:, c * 8 + r:c * 8 + r + 1]
                    if r == 0:
                        P.emit("vector", lambda e, ssl=ssl, cf=cf, bsl=bsl:
                               e.tensor_scalar_mul(out=ssl, in0=bsl, scalar1=cf),
                               waits=[(s_ld, ldn), (s_dv, s_dv.n)], inc=(s_dv, 1))
                    else:
                        P.emit("vector", lambda e, ssl=ssl, cf=cf, bsl=bsl:
                               e.scalar_tensor_tensor(out=ssl, in0=bsl, scalar=cf,
                                                      in1=ssl, op0=ALU.mult, op1=ALU.add),
                               waits=[(s_ld, ldn), (s_dv, s_dv.n)], inc=(s_dv, 1))
            for c in range(8):
                P.emit("vector", lambda e, c=c: e.tensor_copy(
                    out=Sbf[:, c * 64:c * 64 + 64], in_=Sst[:, c * 64:c * 64 + 64]),
                    waits=[(s_dv, s_dv.n)], inc=(s_dv, 1))
            sin_done = s_dv.n
            setup_ctx.close()
            psA = eca(nc.psum_tensor("psA", [128, 128], F32))
            psB = eca(nc.psum_tensor("psB", [128, 128], F32))
            psoA = eca(nc.psum_tensor("psoA", [128, 64], F32))
            psoB = eca(nc.psum_tensor("psoB", [128, 64], F32))
            pc0 = eca(nc.psum_tensor("pc0", [64, 64], F32))
            pc1 = eca(nc.psum_tensor("pc1", [64, 64], F32))
            ppo0 = eca(nc.psum_tensor("ppo0", [128, TG], F32))
            ppo1 = eca(nc.psum_tensor("ppo1", [128, TG], F32))

            x1_loaded = {}

            def load_x1(kk):
                dvn = s_dv.n
                for c in range(8):
                    P.emit("sync", lambda e, c=c, kk=kk: e.dma_start(
                        out=x1t[:, c * 512:c * 512 + 512],
                        in_=x1i[128 * c:128 * c + 128, kk * 512:kk * 512 + 512]),
                        waits=[(s_dv, dvn if kk > 0 else sin_done)], inc=(s_ld, 16))
                x1_loaded[kk] = s_ld.n

            res_ld = s_ld.n
            kw_loaded = {}

            def load_kw(n_ch):
                P.emit("sync", lambda e, n_ch=n_ch: e.dma_start(
                    out=kwt[:, (n_ch % 2) * D:(n_ch % 2) * D + D],
                    in_=Kwi[n_ch * 128:n_ch * 128 + 128, :]),
                    waits=[(s_pe, st.get("kw_pe_%d" % ((n_ch - 2) % 2), 0))],
                    inc=(s_ld, 16))
                kw_loaded[n_ch] = s_ld.n

            g_loaded = {}

            def load_g(n_ch):
                P.emit("sync", lambda e, n_ch=n_ch: e.dma_start(
                    out=gt2[:, (n_ch % 2) * D:(n_ch % 2) * D + D],
                    in_=Gi[n_ch * 128:n_ch * 128 + 128, :]),
                    waits=[(s_dv, st.get("g_dv_%d" % ((n_ch - 2) % 2), 0))],
                    inc=(s_ld, 16))
                g_loaded[n_ch] = s_ld.n

            load_kw(0)
            load_kw(1)
            load_g(0)
            load_g(1)
            load_x1(0)

            pc_free = [[], []]
            pso_free = [[], []]
            psAB_free = [[], []]
            attw_done = {}
            yg_done = []

            def emit_scores(n_ch):
                ab0 = (n_ch % 2) * 2048
                rb0 = (n_ch % 2) * D
                for c in range(8):
                    kcol = c * T_OWN + n_ch * 128
                    w0 = [(s_ld, res_ld), (s_ac, wmt_done_ac), (s_dv, sin_done)]
                    mmA = P.emit("tensor", lambda e, kcol=kcol: e.matmul(
                        psA[:], Kt[0:64, kcol:kcol + 128], Rt[0:64, kcol:kcol + 128],
                        start=True, stop=True, tile_position=(0, 0)),
                        waits=w0 + psAB_free[0], inc=(s_pe, 1))
                    mmB = P.emit("tensor", lambda e, kcol=kcol: e.matmul(
                        psB[:], Kt[64:128, kcol:kcol + 128], Rt[64:128, kcol:kcol + 128],
                        start=True, stop=True, tile_position=(64, 0)),
                        waits=psAB_free[1], inc=(s_pe, 1))
                    a0 = P.emit("vector", lambda e, c=c, ab0=ab0: e.tensor_mul(
                        out=attwb[:, ab0 + c * 256:ab0 + c * 256 + 128], in0=psA[:],
                        in1=wmt[:, (2 * c) * 128:(2 * c) * 128 + 128]),
                        waits=[(s_pe, mmA), (s_dv, s_dv.n)], inc=(s_dv, 1))
                    a1 = P.emit("vector", lambda e, c=c, ab0=ab0: e.tensor_mul(
                        out=attwb[:, ab0 + c * 256 + 128:ab0 + c * 256 + 256], in0=psB[:],
                        in1=wmt[:, (2 * c + 1) * 128:(2 * c + 1) * 128 + 128]),
                        waits=[(s_pe, mmB)], inc=(s_dv, 1))
                    psAB_free[0] = [(s_dv, a0)]
                    psAB_free[1] = [(s_dv, a1)]
                    P.emit("vector", lambda e, kcol=kcol, c=c, rb0=rb0: e.tensor_mul(
                        out=rwbb[:, rb0 + c * 128:rb0 + c * 128 + 128],
                        in0=Rt[:, kcol:kcol + 128], in1=wbt[:, c * 128:c * 128 + 128]),
                        waits=([(s_dv, s_dv.n)] if SAFE_CHAIN else ()), inc=(s_dv, 1))
                attw_done[n_ch] = s_dv.n

            emit_scores(0)
            for n_ch in range(L2_CHUNKS):
                kk = n_ch // 4
                ygfm_col = (n_ch % 4) * 128
                if n_ch + 1 < L2_CHUNKS and L2_STAGE >= 2:
                    emit_scores(n_ch + 1)
                if L2_STAGE < 2:
                    continue
                ab0 = (n_ch % 2) * 2048
                rb0 = (n_ch % 2) * D
                adn = attw_done[n_ch]
                for c in range(8):
                    vcol = n_ch * D + (2 * c) * 64
                    aw_a = attwb[:, ab0 + c * 256:ab0 + c * 256 + 128]
                    aw_b = attwb[:, ab0 + c * 256 + 128:ab0 + c * 256 + 256]
                    rw = rwbb[:, rb0 + c * 128:rb0 + c * 128 + 128]
                    P.emit("tensor", lambda e, vcol=vcol, aw_a=aw_a: e.matmul(
                        psoA[:], aw_a, Vt[:, vcol:vcol + 64],
                        start=True, stop=False),
                        waits=[(s_dv, adn)] + pso_free[0], inc=None)
                    P.emit("tensor", lambda e, c=c, rw=rw: e.matmul(
                        psoA[:], rw[0:64, :], Sbf[0:64, c * 64:c * 64 + 64],
                        start=False, stop=True, tile_position=(0, 0)), inc=None)
                    P.emit("tensor", lambda e, vcol=vcol, aw_b=aw_b: e.matmul(
                        psoB[:], aw_b, Vt[:, vcol + 64:vcol + 128],
                        start=True, stop=False),
                        waits=pso_free[1], inc=None)
                    P.emit("tensor", lambda e, c=c, rw=rw: e.matmul(
                        psoB[:], rw[64:128, :], Sbf[64:128, c * 64:c * 64 + 64],
                        start=False, stop=True, tile_position=(64, 0)), inc=(s_pe, 1))
                    opn = s_pe.n
                    if L2_STAGE >= 3:
                        kwcol = (n_ch % 2) * D + c * 128
                        pn0 = P.emit("tensor", lambda e, kwcol=kwcol, vcol=vcol: e.matmul(
                            pc0[:], kwt[:, kwcol:kwcol + 64], Vt[:, vcol:vcol + 64],
                            start=True, stop=True),
                            waits=[(s_ld, kw_loaded[n_ch])] + pc_free[0],
                            inc=(s_pe, 1))
                        pn1 = P.emit("tensor", lambda e, kwcol=kwcol, vcol=vcol: e.matmul(
                            pc1[:], kwt[:, kwcol + 64:kwcol + 128], Vt[:, vcol + 64:vcol + 128],
                            start=True, stop=True),
                            waits=pc_free[1], inc=(s_pe, 1))
                        for hh, bank, pn in ((0, pc0, pn0), (1, pc1, pn1)):
                            ssl = Sst[64 * hh:64 * hh + 64, c * 64:c * 64 + 64]
                            wsl = wst[64 * hh:64 * hh + 64, c:c + 1]
                            dn = P.emit("vector", lambda e, ssl=ssl, wsl=wsl, bank=bank:
                                        e.scalar_tensor_tensor(out=ssl, in0=ssl, scalar=wsl,
                                                               in1=bank[:], op0=ALU.mult,
                                                               op1=ALU.add),
                                        waits=[(s_pe, pn), (s_dv, s_dv.n)], inc=(s_dv, 1))
                            pc_free[hh] = [(s_dv, dn)]
                        P.emit("vector", lambda e, c=c: e.tensor_copy(
                            out=Sbf[:, c * 64:c * 64 + 64], in_=Sst[:, c * 64:c * 64 + 64]),
                            waits=([(s_dv, s_dv.n)] if SAFE_CHAIN else ()), inc=(s_dv, 1))
                    if L2_STAGE < 4:
                        pso_free[0] = []
                        pso_free[1] = []
                        continue
                    xo0 = (n_ch % 2) * D
                    e0 = P.emit("scalar", lambda e, c=c, xo0=xo0: e.activation(
                        out=xof[:, xo0 + c * 128:xo0 + c * 128 + 64], in_=psoA[:],
                        func=AF.Copy, scale=0.125),
                        waits=[(s_pe, opn),
                               (s_dv, st.get("gn_dv_%d" % (n_ch % 2), 0))],
                        inc=(s_ac, 1))
                    ev = P.emit("scalar", lambda e, c=c, xo0=xo0: e.activation(
                        out=xof[:, xo0 + c * 128 + 64:xo0 + c * 128 + 128], in_=psoB[:],
                        func=AF.Copy, scale=0.125),
                        inc=(s_ac, 1))
                    pso_free[0] = [(s_ac, e0)]
                    pso_free[1] = [(s_ac, ev)]
                if L2_STAGE < 4:
                    st["kw_pe_%d" % (n_ch % 2)] = s_pe.n
                    st["g_dv_%d" % (n_ch % 2)] = s_dv.n
                    continue
                st["kw_pe_%d" % (n_ch % 2)] = s_pe.n
                evn = s_ac.n
                xo0 = (n_ch % 2) * D
                x3d = xof[:, xo0:xo0 + D].rearrange("p (h k) -> p h k", h=16)
                s3d = gnscr[:, 0:D].rearrange("p (h k) -> p h k", h=16)
                ygt = [ygtm, ygtm2][n_ch % 2]
                sr0 = (n_ch % 2) * 16
                P.emit("vector", lambda e, x3d=x3d, sr0=sr0: e.reduce_sum(
                    out=srow[:, sr0:sr0 + 16], in_=x3d, axis=mybir.AxisListType.X),
                    waits=[(s_ac, evn), (s_dv, s_dv.n)], inc=(s_dv, 1))
                dvn = s_dv.n
                P.emit("scalar", lambda e, sr0=sr0: e.activation(
                    out=srow[:, 32 + sr0:48 + sr0], in_=srow[:, sr0:sr0 + 16],
                    func=AF.Copy, scale=1.0 / 64),
                    waits=[(s_dv, dvn)], inc=(s_ac, 1))
                acn = s_ac.n
                P.emit("vector", lambda e, x3d=x3d, sr0=sr0: e.tensor_tensor(
                    out=x3d, in0=x3d,
                    in1=srow[:, 32 + sr0:48 + sr0].unsqueeze(2).broadcast_to([128, 16, 64]),
                    op=ALU.subtract),
                    waits=[(s_ac, acn)], inc=(s_dv, 1))
                dvn = s_dv.n
                P.emit("vector", lambda e, xo0=xo0: e.tensor_mul(
                    out=gnscr[:, 0:D], in0=xof[:, xo0:xo0 + D],
                    in1=xof[:, xo0:xo0 + D]),
                    waits=([(s_dv, dvn)] if SAFE_CHAIN else ()), inc=(s_dv, 1))
                dvn = s_dv.n
                P.emit("vector", lambda e, s3d=s3d, sr0=sr0: e.reduce_sum(
                    out=srow[:, 64 + sr0:80 + sr0], in_=s3d, axis=mybir.AxisListType.X),
                    waits=([(s_dv, dvn)] if SAFE_CHAIN else ()), inc=(s_dv, 1))
                dvn = s_dv.n
                P.emit("scalar", lambda e, sr0=sr0: e.activation(
                    out=srow[:, 64 + sr0:80 + sr0], in_=srow[:, 64 + sr0:80 + sr0],
                    func=AF.Ln, scale=1.0 / 64, bias=epst[:]),
                    waits=[(s_dv, dvn)], inc=(s_ac, 1))
                acn = s_ac.n
                P.emit("scalar", lambda e, sr0=sr0: e.activation(
                    out=srow[:, 64 + sr0:80 + sr0], in_=srow[:, 64 + sr0:80 + sr0],
                    func=AF.Exp, scale=-0.5),
                    waits=([(s_ac, acn)] if SAFE_CHAIN else ()), inc=(s_ac, 1))
                acn = s_ac.n
                P.emit("vector", lambda e, x3d=x3d, sr0=sr0: e.tensor_tensor(
                    out=x3d, in0=x3d,
                    in1=srow[:, 64 + sr0:80 + sr0].unsqueeze(2).broadcast_to([128, 16, 64]),
                    op=ALU.mult),
                    waits=[(s_ac, acn), (s_dv, s_dv.n)], inc=(s_dv, 1))
                dvn = s_dv.n
                P.emit("vector", lambda e, xo0=xo0: e.tensor_mul(
                    out=xof[:, xo0:xo0 + D], in0=xof[:, xo0:xo0 + D], in1=lnwb[:]),
                       waits=([(s_dv, dvn)] if SAFE_CHAIN else ()), inc=(s_dv, 1))
                dvn = s_dv.n
                P.emit("vector", lambda e, xo0=xo0: e.tensor_add(
                    out=xof[:, xo0:xo0 + D], in0=xof[:, xo0:xo0 + D], in1=lnbb[:]),
                       waits=([(s_dv, dvn)] if SAFE_CHAIN else ()), inc=(s_dv, 1))
                dvn = s_dv.n
                g1 = P.emit("vector", lambda e, n_ch=n_ch, xo0=xo0, ygt=ygt: e.tensor_mul(
                    out=ygt[:], in0=xof[:, xo0:xo0 + D],
                    in1=gt2[:, (n_ch % 2) * D:(n_ch % 2) * D + D]),
                    waits=[(s_dv, dvn), (s_tr, st.get("ygtm_tr_%d" % (n_ch % 2), 0)),
                           (s_ld, g_loaded[n_ch])], inc=(s_dv, 1))
                st["gn_dv_%d" % (n_ch % 2)] = s_dv.n
                st["g_dv_%d" % (n_ch % 2)] = s_dv.n
                if L2_STAGE < 5:
                    continue
                if n_ch + 2 < NCHUNK:
                    load_kw(n_ch + 2)
                    load_g(n_ch + 2)
                ygn = g1
                for c in range(8):
                    P.emit("sync", lambda e, c=c, ygfm_col=ygfm_col, ygt=ygt: e.dma_start(
                        out=ygfm[:, c * 512 + ygfm_col:c * 512 + ygfm_col + 128],
                        in_=ygt[:, c * 128:c * 128 + 128], transpose=True),
                        waits=[(s_dv, ygn), (s_pe, st.get("wo_pe", 0))], inc=(s_tr, 16))
                yg_done.append(s_tr.n)
                st["ygtm_tr_%d" % (n_ch % 2)] = s_tr.n
                if n_ch % 4 == 3:
                    trn = s_tr.n
                    for o in range(8):
                        bank = [ppo0, ppo1][o % 2]
                        for ci in range(8):
                            w0 = ([(s_tr, trn), (s_cp, s_cp.n), (s_ld, x1_loaded[kk])]
                                  + (st.get("ppo_free_%d" % (o % 2)) or [])
                                  if ci == 0 else ())
                            P.emit("tensor", lambda e, bank=bank, ci=ci, o=o: e.matmul(
                                bank[:], wo8[:, ci * D + o * 128:ci * D + o * 128 + 128],
                                ygfm[:, ci * 512:ci * 512 + 512],
                                start=(ci == 0), stop=(ci == 7)),
                                waits=w0, inc=((s_pe, 1) if ci == 7 else None))
                        pen = s_pe.n
                        dn = P.emit("vector", lambda e, bank=bank, o=o: e.tensor_add(
                            out=x2t[:], in0=x1t[:, o * 512:o * 512 + 512], in1=bank[:]),
                            waits=[(s_pe, pen), (s_st, st.get("x2t_st", 0)),
                                   (s_dv, s_dv.n)],
                            inc=(s_dv, 1))
                        st["ppo_free_%d" % (o % 2)] = [(s_dv, dn)]
                        P.emit("sync", lambda e, o=o, kk=kk: e.dma_start(
                            out=x2o[128 * o:128 * o + 128, kk * 512:kk * 512 + 512],
                            in_=x2t[:]),
                            waits=[(s_dv, dn)], inc=(s_st, 16))
                        stn = P.emit("sync", lambda e, o=o, kk=kk: e.dma_start(
                            out=x2d.ap()[128 * o:128 * o + 128, kk * 512:kk * 512 + 512],
                            in_=x2t[:]),
                            waits=[(s_dv, dn)], inc=(s_st, 16))
                        st["x2t_st"] = stn
                    st["wo_pe"] = s_pe.n
                    if kk + 1 < 4:
                        load_x1(kk + 1)
            att_dv, att_ac, att_pe, att_st, att_tr = (s_dv.n, s_ac.n, s_pe.n, s_st.n,
                                                      s_tr.n)

        with ExitStack() as ctxb:
          if L2_FFN:
            ecb = ctxb.enter_context
            C8 = ecb(nc.sbuf_tensor("C8", [128, 32 * TG], BF16))
            acc = ecb(nc.sbuf_tensor("acc", [128, 8 * TG], F32))
            hg = ecb(nc.sbuf_tensor("hg", [128, HG_SLOTS * TG], BF16))
            cpw = ecb(nc.sbuf_tensor("cpw", [128, 2 * 4096], BF16))
            xf = ecb(nc.sbuf_tensor("xf", [128, 2 * 8 * TG], F32))
            xnb = ecb(nc.sbuf_tensor("xnb", [128, 8 * TG], BF16))
            st0 = ecb(nc.psum_tensor("st0", [1, TG], F32))
            st1 = ecb(nc.psum_tensor("st1", [1, TG], F32))
            bc0 = ecb(nc.psum_tensor("bc0", [128, TG], F32))
            bc1 = ecb(nc.psum_tensor("bc1", [128, TG], F32))
            rail0 = ecb(nc.psum_tensor("rail0", [128, TG], F32))
            rail1 = ecb(nc.psum_tensor("rail1", [128, TG], F32))
            cp0 = ecb(nc.psum_tensor("cp0", [128, TG], F32))
            cp1 = ecb(nc.psum_tensor("cp1", [128, TG], F32))
            pb.update({"st0": st0, "st1": st1, "bc0": bc0, "bc1": bc1,
                       "rail0": rail0, "rail1": rail1, "cp0": cp0, "cp1": cp1})

            for eng in ("sync", "gpsimd", "tensor", "vector", "scalar"):
                P.wait(eng, s_dv, att_dv)
                P.wait(eng, s_ac, att_ac)
                P.wait(eng, s_pe, att_pe)
                P.wait(eng, s_st, att_st)
                P.wait(eng, s_tr, att_tr)

            rmw2 = hg[:, 0:128]
            ubw2 = hg[:, 4096:4096 + 512]
            for m in range(32):
                w = [(s_pe, s_pe.n)] if m >= 1 else []
                P.emit("gpsimd", lambda e, m=m: e.dma_start(out=rmw2, in_=rmbd[m]),
                       waits=w, inc=(s_cp, 16))
                P.emit("gpsimd", lambda e, m=m: e.dma_start(out=ubw2, in_=ubd[m]),
                       inc=(s_cp, 16))
                cpn = s_cp.n
                bank = [rail0, rail1][m % 2]
                mmn = P.emit("tensor", lambda e, bank=bank: e.matmul(
                    bank[:], rmw2, ubw2, start=True, stop=True),
                    waits=[(s_cp, cpn), (s_dv, s_dv.n)], inc=(s_pe, 1))
                P.emit("vector", lambda e, m=m, bank=bank: e.tensor_copy(
                    out=C8[:, m * TG:m * TG + TG], in_=bank[:]),
                    waits=[(s_pe, mmn)], inc=(s_dv, 1))

            bufs = {"xf": xf, "xn": xnb, "hg": hg, "cpw": cpw, "acc": acc, "C8": C8,
                    "cpstream": cpstream_d,
                    "scr": {"xb": xb_s, "sq": sq_s, "d": d_s}, "rows": rows}

            tg_list2 = [(k, k * TG, TG) for k in range(4)]

            def ap_x_load2(c, col0, n):
                if L2_CHUNKS == 0:
                    return x1i[128 * c:128 * c + 128, col0:col0 + n]
                return x2d.ap()[128 * c:128 * c + 128, col0:col0 + n]

            def post_tg2(t_idx, b, n, res_done):
                k = t_idx
                for c in range(8):
                    P.emit("sync", lambda e, c=c, k=k, b=b: e.dma_start(
                        out=x3o[128 * c:128 * c + 128, k * TG:k * TG + TG],
                        in_=xf[:, (b * 8 + c) * TG:(b * 8 + c) * TG + TG]),
                        waits=[(s_dv, res_done[c])], inc=(s_st, 16))
                st["post_st"] = s_st.n
                st["post_dv"] = max(res_done)

            emit_ffn(P, sem, pb, st, bufs, tg_list2, ap_x_load2, post_tg2, consts)

            P.wait("gpsimd", s_dv, s_dv.n)
            P.wait("gpsimd", s_ac, s_ac.n)
            P.wait("gpsimd", s_kp, s_kp.n)
            P.emit("gpsimd", lambda e: e.dma_start(out=dbg2[:, 0:TG], in_=C8[:, 0:TG]),
                   inc=(s_st, 16))
            P.emit("gpsimd", lambda e: e.dma_start(out=dbg2[:, TG:2*TG], in_=xnb[:, 0:TG]),
                   inc=(s_st, 16))
            P.emit("gpsimd", lambda e: e.dma_start(out=dbg2[:, 2*TG:3*TG], in_=hg[:, 0:TG]),
                   inc=(s_st, 16))
            P.emit("sync", lambda e: e.dma_start(out=dbg2[:, 3*TG:4*TG], in_=acc[:, 0:TG]),
                   inc=(s_st, 16))

            P.wait("gpsimd", s_st, s_st.n)
            P.wait("gpsimd", s_tr, s_tr.n)

        with nc.Block() as block:
            P.build(block)

    return nc


def run_l2(inputs, l1_results):
    shared = _prep_shared(inputs)
    Ball = np.stack([np.asarray(l1_results[r]["Bo"], np.float32) for r in range(NCORES)])
    Ball = np.ascontiguousarray(Ball.transpose(2, 0, 1, 3)).reshape(128, 4096)

    in_maps = []
    for i in range(NCORES):
        r = l1_results[i]
        expo = np.zeros((1, 8), np.float32)
        mask8 = np.zeros((1, 8), np.float32)
        for j in range(i):
            expo[0, j] = 2048.0 * (i - 1 - j)
            mask8[0, j] = 1.0
        m = {
            "x1i": np.ascontiguousarray(r["x1o"]),
            "Ri": np.ascontiguousarray(r["Ro"]),
            "Ki": np.ascontiguousarray(r["Ko"]),
            "Gi": np.ascontiguousarray(r["Go"]),
            "Vi": np.ascontiguousarray(r["Vo"]),
            "Kwi": np.ascontiguousarray(r["Kwo"]),
            "Ball": Ball,
            "cpstream": shared["cpstream"],
            "rmbd": shared["rmbd"], "ubd": shared["ubd"],
            "wot": np.ascontiguousarray(np.asarray(inputs["L0_attn_W_o"], np.float32).T),
            "td": shared["td"],
            "tf": np.asarray(inputs["L0_attn_time_faaaa"], np.float32).reshape(1, 16),
            "lnw": np.asarray(inputs["L0_attn_ln_w"], np.float32).reshape(1, D),
            "lnb": np.asarray(inputs["L0_attn_ln_b"], np.float32).reshape(1, D),
            "expo": expo, "mask8": mask8,
        }
        in_maps.append(m)

    nc = build_l2()
    res = run_bass_kernel_spmd(nc, in_maps, core_ids=list(range(NCORES)))
    return res.results


def _prep_shared(inputs):
    W = np.asarray(inputs["L0_ffn_w_cproj"], dtype=np.float32)
    A = W.T.reshape(4, 32, 128, 8, 128).transpose(2, 0, 3, 1, 4)
    cpstream = np.ascontiguousarray(A).reshape(128, 131072)

    rm = np.asarray(inputs["L0_ffn_rail_maker"], dtype=np.float32)
    u = np.asarray(inputs["L0_ffn_rail_up_proj"], dtype=np.float32)
    rmbd = np.zeros((32, 128, 128), dtype=np.float32)
    ubd = np.zeros((32, 128, 512), dtype=np.float32)
    for h in range(4):
        for c in range(8):
            m = h * 8 + c
            for p in range(8):
                r = 8 * c + p
                rmbd[m, 16 * p:16 * p + 16, 16 * p:16 * p + 16] = rm[h, r].T
                ubd[m, 16 * p:16 * p + 16, 64 * p:64 * p + 64] = u[h, r].T

    tm = np.stack([np.asarray(inputs[f"L0_attn_time_mix_{t}"], np.float32).reshape(-1)
                   for t in ("r", "k", "v", "g")])
    tmix = np.ascontiguousarray(tm.reshape(4, 8, 128).transpose(2, 0, 1)).reshape(128, 32)

    return {
        "cpstream": cpstream, "rmbd": rmbd, "ubd": ubd,
        "wrt": np.ascontiguousarray(np.asarray(inputs["L0_attn_W_r"], np.float32).T),
        "wkt": np.ascontiguousarray(np.asarray(inputs["L0_attn_W_k"], np.float32).T),
        "wvt": np.ascontiguousarray(np.asarray(inputs["L0_attn_W_v"], np.float32).T),
        "wgt": np.ascontiguousarray(np.asarray(inputs["L0_attn_W_g"], np.float32).T),
        "tmix": tmix,
        "td": np.asarray(inputs["L0_attn_time_decay"], np.float32).reshape(1, 16),
    }


def run_l1(inputs):
    shared = _prep_shared(inputs)
    x = np.asarray(inputs["L0_attn_xq"], np.float32)[0]
    xT = np.ascontiguousarray(x.T)
    xTp = np.concatenate([np.zeros((D, OVL), np.float32), xT], axis=1)

    in_maps = []
    for i in range(NCORES):
        m = dict(shared)
        m["x0T"] = np.ascontiguousarray(xTp[:, i * T_OWN: i * T_OWN + T_IN])
        in_maps.append(m)

    nc = build_l1()
    res = run_bass_kernel_spmd(nc, in_maps, core_ids=list(range(NCORES)))
    return res.results


def kernel(**inputs):
    l1 = run_l1(inputs)
    l2 = run_l2(inputs, l1)
    x3 = np.concatenate([r["x3o"].T for r in l2], axis=0)
    return x3.reshape(1, 16384, 1024).astype(np.float32)


# version 13 (final)
# speedup vs baseline: 1.0371x; 1.0295x over previous
from contextlib import ExitStack

import numpy as np
import ml_dtypes
import concourse.bass as bass
import concourse.mybir as mybir
from concourse.bass_utils import run_bass_kernel_spmd

F32 = mybir.dt.float32
BF16 = mybir.dt.bfloat16
AF = mybir.ActivationFunctionType
ALU = mybir.AluOpType
BF = ml_dtypes.bfloat16

NCORES = 8
T_OWN = 2048
OVL = 128
T_IN = T_OWN + OVL
TG = 512
D = 1024
NHEAD = 16
HK = 64
CH = 128
NCHUNK = T_OWN // CH
EPS = 1e-5
HG_SLOTS = 63

SAFE_CHAIN = True
L2_CHUNKS = NCHUNK
L2_FFN = True
L2_STAGE = 9


class Sem:
    __slots__ = ("h", "n")

    def __init__(self, h):
        self.h = h
        self.n = 0


class Plan:
    ENGINES = ("sync", "tensor", "vector", "scalar", "gpsimd")

    def __init__(self, nc):
        self.nc = nc
        self.steps = {e: [] for e in self.ENGINES}

    def emit(self, eng, fn, waits=(), inc=None):
        ws = tuple((s, v) for (s, v) in waits if v > 0)
        incs = ()
        if inc is not None:
            s, k = inc
            s.n += k
            incs = ((s, k),)
        self.steps[eng].append((fn, ws, incs))
        return incs[0][0].n if incs else None

    def wait(self, eng, sem, val=None):
        v = sem.n if val is None else val
        if v > 0:
            self.steps[eng].append((None, ((sem, v),), ()))

    def build(self, block):
        for eng in self.ENGINES:
            steps = self.steps[eng]
            if not steps:
                continue

            def body(e, steps=steps):
                hw = {}
                for fn, ws, incs in steps:
                    for (s, v) in ws:
                        if v > hw.get(s.h.num, 0):
                            e.wait_ge(s.h, v)
                            hw[s.h.num] = v
                    if fn is None:
                        continue
                    inst = fn(e)
                    for (s, k) in incs:
                        inst = inst.then_inc(s.h, k)

            getattr(block, eng)(body)


def emit_ln(P, sem, pb, st, xf_ap, out_ap, n, scr, rows, consts,
            pre_dv=0, pre_ac=0):
    s_dv, s_ac, s_pe = sem["dv"], sem["ac"], sem["pe"]
    onesb, onesf, epst = consts["ones_bf"], consts["ones_f32"], consts["epst"]
    xb_t, sq_t, d_t = scr["xb"], scr["sq"], scr["d"]
    st0, st1, bc0, bc1 = pb["st0"], pb["st1"], pb["bc0"], pb["bc1"]
    mrow, vrow = rows["m"], rows["v"]

    if pre_dv:
        P.wait("vector", s_dv, pre_dv)
        P.wait("scalar", s_dv, pre_dv)
    if pre_ac:
        P.wait("vector", s_ac, pre_ac)
        P.wait("scalar", s_ac, pre_ac)

    mm_done = []
    for c in range(8):
        xa = xf_ap(c)
        xb = xb_t[:, 0:n]
        sq = sq_t[:, 0:n]
        w_pe = [(s_pe, mm_done[c - 1])] if c >= 1 else []
        P.emit("vector", lambda e, xb=xb, xa=xa: e.tensor_copy(out=xb, in_=xa),
               waits=w_pe, inc=(s_dv, 1))
        dvn = s_dv.n
        P.emit("scalar", lambda e, sq=sq, xa=xa: e.activation(out=sq, in_=xa, func=AF.Square),
               waits=w_pe, inc=(s_ac, 1))
        acn = s_ac.n
        w0 = [(s_dv, dvn)]
        if c == 0:
            w0.append((s_ac, st["stats_free_ac"]))
            w0.append((s_dv, st["stats_free_dv"]))
        P.emit("tensor", lambda e, xb=xb, c=c, n=n: e.matmul(
            st0[:, 0:n], onesb[:], xb, start=(c == 0), stop=(c == 7)), waits=w0)
        mmn = P.emit("tensor", lambda e, sq=sq, c=c, n=n: e.matmul(
            st1[:, 0:n], onesb[:], sq, start=(c == 0), stop=(c == 7)),
            waits=[(s_ac, acn)], inc=(s_pe, 1))
        mm_done.append(mmn)

    pe_n = s_pe.n
    P.emit("scalar", lambda e, n=n: e.activation(out=mrow[:, 0:n], in_=st0[:, 0:n],
                                                 func=AF.Copy, scale=1.0 / D),
           waits=[(s_pe, pe_n)], inc=(s_ac, 1))
    P.emit("scalar", lambda e, n=n: e.activation(out=vrow[:, 0:n], in_=st1[:, 0:n],
                                                 func=AF.Copy, scale=1.0 / D),
           inc=(s_ac, 1))
    acn = s_ac.n
    P.emit("scalar", lambda e, n=n: e.activation(out=st0[:, 0:n], in_=mrow[:, 0:n],
                                                 func=AF.Square),
           waits=([(s_ac, acn)] if SAFE_CHAIN else ()), inc=(s_ac, 1))
    st["stats_free_ac"] = s_ac.n
    acn = s_ac.n
    P.emit("vector", lambda e, n=n: e.tensor_sub(out=vrow[:, 0:n], in0=vrow[:, 0:n],
                                                 in1=st0[:, 0:n]),
           waits=[(s_ac, acn)], inc=(s_dv, 1))
    st["stats_free_dv"] = s_dv.n
    dvn = s_dv.n
    P.emit("scalar", lambda e, n=n: e.activation(out=vrow[:, 0:n], in_=vrow[:, 0:n],
                                                 func=AF.Ln, bias=epst[0:1, :]),
           waits=[(s_dv, dvn)], inc=(s_ac, 1))
    acn = s_ac.n
    P.emit("scalar", lambda e, n=n: e.activation(out=vrow[:, 0:n], in_=vrow[:, 0:n],
                                                 func=AF.Exp, scale=-0.5),
           waits=([(s_ac, acn)] if SAFE_CHAIN else ()), inc=(s_ac, 1))
    acn = s_ac.n
    P.emit("tensor", lambda e, n=n: e.matmul(bc0[:, 0:n], onesf[:], mrow[:, 0:n],
                                             start=True, stop=True),
           waits=[(s_ac, acn), (s_dv, st["bc_free_dv"])])
    P.emit("tensor", lambda e, n=n: e.matmul(bc1[:, 0:n], onesf[:], vrow[:, 0:n],
                                             start=True, stop=True), inc=(s_pe, 1))
    pe_n = s_pe.n
    mul_done = []
    for c in range(8):
        xa = xf_ap(c)
        dd = d_t[:, 0:n]
        w = [(s_pe, pe_n)]
        if c >= 1:
            w.append((s_dv, mul_done[c - 1]))
        P.emit("vector", lambda e, xa=xa, dd=dd, n=n: e.tensor_sub(out=dd, in0=xa,
                                                                   in1=bc0[:, 0:n]),
               waits=w, inc=(s_dv, 1))
        dvn = s_dv.n
        mn = P.emit("vector", lambda e, dd=dd, c=c, n=n: e.tensor_mul(
            out=out_ap(c), in0=dd, in1=bc1[:, 0:n]),
            waits=([(s_dv, dvn)] if SAFE_CHAIN else ()), inc=(s_dv, 1))
        mul_done.append(mn)
    st["bc_free_dv"] = s_dv.n
    return s_dv.n


def emit_ln_single(P, sem, pb, st, xf_ap, out_ap, scr, rows, consts, pre_dv=0):
    s_dv, s_ac, s_pe = sem["dv"], sem["ac"], sem["pe"]
    onesb, onesf, epst = consts["ones_bf"], consts["ones_f32"], consts["epst"]
    st0, st1, bc0, bc1 = pb["st0"], pb["st1"], pb["bc0"], pb["bc1"]
    xb_t, sq_t = scr["xb"], scr["sq"]
    mrow, vrow = rows["m"], rows["v"]
    if pre_dv:
        P.wait("vector", s_dv, pre_dv)
        P.wait("scalar", s_dv, pre_dv)
    for c in range(8):
        xa = xf_ap(c)
        P.emit("vector", lambda e, c=c, xa=xa: e.tensor_copy(out=xb_t[:, c:c + 1], in_=xa),
               inc=(s_dv, 1))
        P.emit("scalar", lambda e, c=c, xa=xa: e.activation(out=sq_t[:, c:c + 1], in_=xa,
                                                            func=AF.Square), inc=(s_ac, 1))
    dvn, acn = s_dv.n, s_ac.n
    for c in range(8):
        w0 = [(s_dv, dvn)]
        if c == 0:
            w0.append((s_ac, st["stats_free_ac"]))
            w0.append((s_dv, st["stats_free_dv"]))
        P.emit("tensor", lambda e, c=c: e.matmul(st0[:, 0:1], onesb[:], xb_t[:, c:c + 1],
                                                 start=(c == 0), stop=(c == 7)), waits=w0)
        P.emit("tensor", lambda e, c=c: e.matmul(st1[:, 0:1], onesb[:], sq_t[:, c:c + 1],
                                                 start=(c == 0), stop=(c == 7)),
               waits=[(s_ac, acn)], inc=((s_pe, 1) if c == 7 else None))
    pe_n = s_pe.n
    P.emit("scalar", lambda e: e.activation(out=mrow[:, 0:1], in_=st0[:, 0:1], func=AF.Copy,
                                            scale=1.0 / D), waits=[(s_pe, pe_n)], inc=(s_ac, 1))
    P.emit("scalar", lambda e: e.activation(out=vrow[:, 0:1], in_=st1[:, 0:1], func=AF.Copy,
                                            scale=1.0 / D), inc=(s_ac, 1))
    st["stats_free_ac"] = s_ac.n
    acn = s_ac.n
    P.emit("scalar", lambda e: e.activation(out=st0[:, 0:1], in_=mrow[:, 0:1], func=AF.Square),
           waits=([(s_ac, acn)] if SAFE_CHAIN else ()), inc=(s_ac, 1))
    acn = s_ac.n
    P.emit("vector", lambda e: e.tensor_sub(out=vrow[:, 0:1], in0=vrow[:, 0:1],
                                            in1=st0[:, 0:1]),
           waits=[(s_ac, acn)], inc=(s_dv, 1))
    dvn = s_dv.n
    P.emit("scalar", lambda e: e.activation(out=vrow[:, 0:1], in_=vrow[:, 0:1], func=AF.Ln,
                                            bias=epst[0:1, :]),
           waits=[(s_dv, dvn)], inc=(s_ac, 1))
    acn = s_ac.n
    P.emit("scalar", lambda e: e.activation(out=vrow[:, 0:1], in_=vrow[:, 0:1], func=AF.Exp,
                                            scale=-0.5),
           waits=([(s_ac, acn)] if SAFE_CHAIN else ()), inc=(s_ac, 1))
    acn = s_ac.n
    P.emit("tensor", lambda e: e.matmul(bc0[:, 0:1], onesf[:], mrow[:, 0:1],
                                        start=True, stop=True),
           waits=[(s_ac, acn), (s_dv, st["bc_free_dv"])])
    P.emit("tensor", lambda e: e.matmul(bc1[:, 0:1], onesf[:], vrow[:, 0:1],
                                        start=True, stop=True), inc=(s_pe, 1))
    pe_n = s_pe.n
    dd = scr["d"][:, 0:1]
    for c in range(8):
        xa = xf_ap(c)
        P.emit("vector", lambda e, xa=xa, dd=dd: e.tensor_sub(out=dd, in0=xa, in1=bc0[:, 0:1]),
               waits=[(s_pe, pe_n), (s_dv, s_dv.n)], inc=(s_dv, 1))
        dvn = s_dv.n
        P.emit("vector", lambda e, c=c, dd=dd: e.tensor_mul(out=out_ap(c), in0=dd,
                                                            in1=bc1[:, 0:1]),
               waits=[(s_dv, dvn)], inc=(s_dv, 1))
    st["bc_free_dv"] = s_dv.n
    return s_dv.n


def emit_ffn(P, sem, pb, st, bufs, tg_list, ap_x_load, post_tg, consts):
    s_ld, s_cp, s_pe = sem["ld"], sem["cp"], sem["pe"]
    s_dv, s_ac, s_kp = sem["dv"], sem["ac"], sem["kp"]
    s_xf = sem["xfree"]
    xf, xn, hg, cpw, acc, C8 = (bufs["xf"], bufs["xn"], bufs["hg"], bufs["cpw"],
                                bufs["acc"], bufs["C8"])
    cpst = bufs["cpstream"]
    pbr = [pb["rail0"], pb["rail1"]]
    pbc = [pb["cp0"], pb["cp1"]]

    kp_base = s_kp.n
    ntg = len(tg_list)

    x_loaded = {}

    def emit_x_load(t_idx):
        if t_idx >= ntg:
            return
        tg_i, col0, n = tg_list[t_idx]
        b = t_idx % 2
        for c in range(8):
            dst = xf[:, (b * 8 + c) * TG:(b * 8 + c) * TG + n]
            w = ([(s_xf, st["xf_base"])] if (t_idx < 2 and st["xf_base"]) else [])
            P.emit("sync", lambda e, dst=dst, c=c, col0=col0, n=n: e.dma_start(
                out=dst, in_=ap_x_load(c, col0, n)), waits=w, inc=(s_ld, 16))
        x_loaded[t_idx] = s_ld.n

    emit_x_load(0)
    emit_x_load(1)

    slab_loaded = []
    for gstep in range(32 * ntg):
        g, o = (gstep % 32) // 8, gstep % 8
        dst = cpw[:, (gstep % 2) * 4096:(gstep % 2) * 4096 + 4096]
        src = cpst[:, (g * 8 + o) * 4096:(g * 8 + o) * 4096 + 4096]
        w = [(s_kp, kp_base + gstep - 1)] if gstep >= 2 else []
        P.emit("gpsimd", lambda e, dst=dst, src=src: e.dma_start(out=dst, in_=src),
               waits=w, inc=(s_cp, 16))
        slab_loaded.append(s_cp.n)

    for t_idx, (tg_i, col0, n) in enumerate(tg_list):
        b = t_idx % 2

        def xap(c, b=b, n=n):
            return xf[:, (b * 8 + c) * TG:(b * 8 + c) * TG + n]

        ldw = x_loaded[t_idx]
        P.wait("vector", s_ld, ldw)
        P.wait("scalar", s_ld, ldw)
        ln_done = emit_ln(P, sem, pb, st, xap,
                          lambda c, n=n: xn[:, c * TG:c * TG + n],
                          n, bufs["scr"], bufs["rows"], consts)
        if t_idx == ntg - 1 and "dbg_ln" in bufs:
            dbg = bufs["dbg_ln"]
            rows_ = bufs["rows"]
            P.emit("sync", lambda e: e.dma_start(out=dbg[0:1, 0:TG], in_=rows_["m"][:, 0:TG]),
                   waits=[(s_dv, ln_done)], inc=(sem["st"], 16))
            P.emit("sync", lambda e: e.dma_start(out=dbg[1:2, 0:TG], in_=rows_["v"][:, 0:TG]),
                   inc=(sem["st"], 16))
            P.emit("sync", lambda e, b=b: e.dma_start(out=dbg[2:130, 0:TG][0:128, :],
                                                      in_=xf[:, b * 8 * TG:b * 8 * TG + TG]),
                   inc=(sem["st"], 16))
            P.emit("gpsimd", lambda e: e.dma_start(out=dbg[130:258, 0:TG][0:128, :],
                                                   in_=xn[:, 0:TG]),
                   waits=[(s_dv, ln_done)], inc=(sem["st"], 16))
            P.emit("sync", lambda e: e.dma_start(out=dbg[0:1, 0:128],
                                                 in_=consts["ones_f32"][:]),
                   inc=(sem["st"], 16))

        rail_j = [0]
        tg_slot0 = st["slot_ctr"]

        def emit_rails(kcnt, t_idx=t_idx, tg_slot0=tg_slot0, n=n, ln_done=ln_done):
            for _ in range(kcnt):
                j = rail_j[0]
                if j >= 128:
                    return
                rail_j[0] += 1
                h, rp = j // 32, j % 32
                c, s = rp // 4, rp % 4
                slot = (tg_slot0 + j) % HG_SLOTS
                bank = pbr[j % 2]
                lhs = C8[:, (h * 8 + c) * TG:(h * 8 + c) * TG + TG]
                lhsT = lhs[32 * s:32 * s + 32, 128 * s:128 * s + 128]
                rhs = xn[32 * s:32 * s + 32, c * TG:c * TG + n]
                w = [(s_dv, ln_done), (s_ac, st["rail_gelu"][j % 2])]
                mmn = P.emit("tensor", lambda e, bank=bank, lhsT=lhsT, rhs=rhs, s=s, n=n:
                             e.matmul(bank[:, 0:n], lhsT, rhs, start=True, stop=True,
                                      tile_position=(32 * s, 0)), waits=w, inc=(s_pe, 1))
                glob_slot_idx = tg_slot0 + j
                wg = [(s_pe, mmn)]
                prev_reader = st["slot_reader_step"][slot]
                if prev_reader > 0:
                    wg.append((s_kp, prev_reader))
                dst = hg[:, slot * TG:slot * TG + n]
                gn = P.emit("scalar", lambda e, dst=dst, bank=bank, n=n: e.activation(
                    out=dst, in_=bank[:, 0:n], func=AF.Gelu), waits=wg, inc=(s_ac, 1))
                st["gelu_done"][slot] = gn
                st["rail_gelu"][j % 2] = gn
            return

        emit_rails(32)
        st["slot_ctr"] += 128

        for g in range(4):
            for o in range(8):
                gstep = t_idx * 32 + g * 8 + o
                bank = pbc[gstep % 2]
                slab = cpw[:, (gstep % 2) * 4096:(gstep % 2) * 4096 + 4096]
                w = [(s_cp, slab_loaded[gstep]), (s_dv, st["cp_bank_dv"][gstep % 2])]
                gmax = 0
                for jj in range(32):
                    slot = (tg_slot0 + g * 32 + jj) % HG_SLOTS
                    gmax = max(gmax, st["gelu_done"][slot])
                w.append((s_ac, gmax))
                for jj in range(32):
                    slot = (tg_slot0 + g * 32 + jj) % HG_SLOTS
                    lhsT = slab[:, jj * 128:jj * 128 + 128]
                    rhs = hg[:, slot * TG:slot * TG + n]
                    P.emit("tensor", lambda e, bank=bank, lhsT=lhsT, rhs=rhs, jj=jj, n=n:
                           e.matmul(bank[:, 0:n], lhsT, rhs, start=(jj == 0),
                                    stop=(jj == 31)),
                           waits=(w if jj == 0 else ()),
                           inc=((s_kp, 1) if jj == 31 else None))
                for jj in range(32):
                    slot = (tg_slot0 + g * 32 + jj) % HG_SLOTS
                    st["slot_reader_step"][slot] = s_kp.n
                emit_rails(4)
                accsl = acc[:, o * TG:o * TG + n]
                if g == 0:
                    fn = lambda e, accsl=accsl, bank=bank, n=n: e.tensor_copy(
                        out=accsl, in_=bank[:, 0:n])
                else:
                    fn = lambda e, accsl=accsl, bank=bank, n=n: e.tensor_add(
                        out=accsl, in0=accsl, in1=bank[:, 0:n])
                wdv = [(s_kp, s_kp.n)]
                if SAFE_CHAIN:
                    wdv.append((s_dv, s_dv.n))
                dn = P.emit("vector", fn, waits=wdv, inc=(s_dv, 1))
                st["cp_bank_dv"][gstep % 2] = dn

        res_done = []
        for c in range(8):
            xa = xap(c)
            accsl = acc[:, c * TG:c * TG + n]
            dn = P.emit("vector", lambda e, xa=xa, accsl=accsl: e.scalar_tensor_tensor(
                out=xa, in0=accsl, scalar=0.5, in1=xa, op0=ALU.mult, op1=ALU.add),
                waits=[(s_dv, s_dv.n)], inc=(s_dv, 1))
            res_done.append(dn)

        post_tg(t_idx, b, n, res_done)
        P.emit("vector", lambda e: e.engine_nop(),
               waits=[(s_dv, st["post_dv"]), (sem["st"], st["post_st"])],
               inc=(s_xf, 1))
        xfn = s_xf.n
        if t_idx + 2 < ntg:
            P.wait("sync", s_xf, xfn)
            emit_x_load(t_idx + 2)

    st["xf_base"] += ntg


def build_l1():
    nc = bass.Bass("TRN2", target_bir_lowering=False)

    x0T = nc.dram_tensor("x0T", [D, T_IN], F32, kind="ExternalInput").ap()
    cpstream_d = nc.dram_tensor("cpstream", [128, 131072], F32, kind="ExternalInput").ap()
    rmbd = nc.dram_tensor("rmbd", [32, 128, 128], F32, kind="ExternalInput").ap()
    ubd = nc.dram_tensor("ubd", [32, 128, 512], F32, kind="ExternalInput").ap()
    wrt = nc.dram_tensor("wrt", [D, D], F32, kind="ExternalInput").ap()
    wkt = nc.dram_tensor("wkt", [D, D], F32, kind="ExternalInput").ap()
    wvt = nc.dram_tensor("wvt", [D, D], F32, kind="ExternalInput").ap()
    wgt = nc.dram_tensor("wgt", [D, D], F32, kind="ExternalInput").ap()
    tmix = nc.dram_tensor("tmix", [128, 32], F32, kind="ExternalInput").ap()
    td = nc.dram_tensor("td", [1, 16], F32, kind="ExternalInput").ap()

    x1o = nc.dram_tensor("x1o", [D, T_OWN], F32, kind="ExternalOutput").ap()
    Ro = nc.dram_tensor("Ro", [D, T_OWN], BF16, kind="ExternalOutput").ap()
    Ko = nc.dram_tensor("Ko", [D, T_OWN], BF16, kind="ExternalOutput").ap()
    Go = nc.dram_tensor("Go", [T_OWN, D], BF16, kind="ExternalOutput").ap()
    Vo = nc.dram_tensor("Vo", [T_OWN, D], BF16, kind="ExternalOutput").ap()
    Kwo = nc.dram_tensor("Kwo", [T_OWN, D], BF16, kind="ExternalOutput").ap()
    Bo = nc.dram_tensor("Bo", [8, 128, 64], F32, kind="ExternalOutput").ap()
    dbgo = nc.dram_tensor("dbgo", [128, 4 * TG], F32, kind="ExternalOutput").ap()
    c8o = nc.dram_tensor("c8o", [128, 32 * TG], BF16, kind="ExternalOutput").ap()
    dbgln = nc.dram_tensor("dbgln", [258, TG], F32, kind="ExternalOutput").ap()

    ones_bf_h = nc.inline_tensor(np.ones((128, 1), dtype=BF), "c_ones_bf")
    ones_f32_h = nc.inline_tensor(np.ones((1, 128), dtype=np.float32), "c_ones_f32")
    epst_h = nc.inline_tensor(np.full((128, 1), EPS, dtype=np.float32), "c_eps")
    riota_h = nc.inline_tensor((127.0 - np.arange(128, dtype=np.float32)).reshape(128, 1),
                               "c_riota")

    with ExitStack() as ctx:
        ec = ctx.enter_context
        ones_bf = ec(nc.sbuf_tensor("ones_bf", [128, 1], BF16))
        ones_f32 = ec(nc.sbuf_tensor("ones_f32", [1, 128], F32))
        epst = ec(nc.sbuf_tensor("epst", [128, 1], F32))
        riota = ec(nc.sbuf_tensor("riota", [128, 1], F32))
        H2 = ec(nc.sbuf_tensor("H2", [128, 8 * 2050], BF16))
        tmixs = ec(nc.sbuf_tensor("tmixs", [128, 32], F32))
        xb_s = ec(nc.sbuf_tensor("xb_s", [128, TG], BF16))
        sq_s = ec(nc.sbuf_tensor("sq_s", [128, TG], BF16))
        d_s = ec(nc.sbuf_tensor("d_s", [128, TG], F32))
        mrow = ec(nc.sbuf_tensor("mrow", [1, TG], F32))
        vrow = ec(nc.sbuf_tensor("vrow", [1, TG], F32))
        st0 = ec(nc.psum_tensor("st0", [1, TG], F32))
        st1 = ec(nc.psum_tensor("st1", [1, TG], F32))
        bc0 = ec(nc.psum_tensor("bc0", [128, TG], F32))
        bc1 = ec(nc.psum_tensor("bc1", [128, TG], F32))
        h_ld = ec(nc.semaphore("s_ld"))
        h_cp = ec(nc.semaphore("s_cp"))
        h_pe = ec(nc.semaphore("s_pe"))
        h_dv = ec(nc.semaphore("s_dv"))
        h_ac = ec(nc.semaphore("s_ac"))
        h_st = ec(nc.semaphore("s_st"))
        h_tr = ec(nc.semaphore("s_tr"))
        h_xf = ec(nc.semaphore("s_xf"))
        h_kp = ec(nc.semaphore("s_kp"))
        P = Plan(nc)
        sem = {"ld": Sem(h_ld), "cp": Sem(h_cp), "pe": Sem(h_pe), "dv": Sem(h_dv),
               "ac": Sem(h_ac), "st": Sem(h_st), "tr": Sem(h_tr), "xfree": Sem(h_xf),
               "kp": Sem(h_kp)}
        s_ld, s_cp, s_pe = sem["ld"], sem["cp"], sem["pe"]
        s_dv, s_ac, s_st, s_tr, s_kp = (sem["dv"], sem["ac"], sem["st"], sem["tr"],
                                        sem["kp"])

        consts = {"ones_bf": ones_bf, "ones_f32": ones_f32, "epst": epst}
        pb = {"st0": st0, "st1": st1, "bc0": bc0, "bc1": bc1}
        st = {"stats_free_ac": 0, "stats_free_dv": 0, "bc_free_dv": 0,
              "xf_base": 0, "slot_ctr": 0,
              "slot_reader_step": [0] * HG_SLOTS, "gelu_done": [0] * HG_SLOTS,
              "rail_gelu": [0, 0], "cp_bank_dv": [0, 0], "post_dv": 0,
              "post_st": 0}
        rows = {"m": mrow, "v": vrow}

        for dst, src in [(ones_bf, ones_bf_h.ap()), (ones_f32, ones_f32_h.ap()),
                         (epst, epst_h.ap()), (riota, riota_h.ap()),
                         (tmixs, tmix)]:
            P.emit("sync", lambda e, dst=dst, src=src: e.dma_start(out=dst[:], in_=src),
                   inc=(s_ld, 16))
        const_ld = s_ld.n

        with ExitStack() as ctx1:
            ec1 = ctx1.enter_context
            C8 = ec1(nc.sbuf_tensor("C8", [128, 32 * TG], BF16))
            acc = ec1(nc.sbuf_tensor("acc", [128, 8 * TG], F32))
            hg = ec1(nc.sbuf_tensor("hg", [128, HG_SLOTS * TG], BF16))
            cpw = ec1(nc.sbuf_tensor("cpw", [128, 2 * 4096], BF16))
            xf = ec1(nc.sbuf_tensor("xf", [128, 2 * 8 * TG], F32))
            xnb = ec1(nc.sbuf_tensor("xnb", [128, 8 * TG], BF16))
            rail0 = ec1(nc.psum_tensor("rail0", [128, TG], F32))
            rail1 = ec1(nc.psum_tensor("rail1", [128, TG], F32))
            cp0 = ec1(nc.psum_tensor("cp0", [128, TG], F32))
            cp1 = ec1(nc.psum_tensor("cp1", [128, TG], F32))
            pb.update({"rail0": rail0, "rail1": rail1, "cp0": cp0, "cp1": cp1})

            rmw = hg[:, 0:128]
            ubw = hg[:, 4096:4096 + 512]
            for m in range(32):
                w = [(s_pe, s_pe.n)] if m >= 1 else []
                P.emit("gpsimd", lambda e, m=m: e.dma_start(out=rmw, in_=rmbd[m]),
                       waits=w, inc=(s_cp, 16))
                P.emit("gpsimd", lambda e, m=m: e.dma_start(out=ubw, in_=ubd[m]),
                       inc=(s_cp, 16))
                cpn = s_cp.n
                bank = [rail0, rail1][m % 2]
                mmn = P.emit("tensor", lambda e, bank=bank: e.matmul(
                    bank[:], rmw, ubw, start=True, stop=True),
                    waits=[(s_cp, cpn), (s_dv, s_dv.n)], inc=(s_pe, 1))
                P.emit("vector", lambda e, m=m, bank=bank: e.tensor_copy(
                    out=C8[:, m * TG:m * TG + TG], in_=bank[:]),
                    waits=[(s_pe, mmn)], inc=(s_dv, 1))
            c8_dv = s_dv.n

            bufs = {"xf": xf, "xn": xnb, "hg": hg, "cpw": cpw, "acc": acc, "C8": C8,
                    "cpstream": cpstream_d, "dbg_ln": dbgln,
                    "scr": {"xb": xb_s, "sq": sq_s, "d": d_s}, "rows": rows}

            def dump_debug():
                P.emit("gpsimd", lambda e: e.dma_start(out=dbgo[:, 0:TG],
                                                       in_=C8[:, 0:TG]), inc=(s_st, 16))
                P.emit("gpsimd", lambda e: e.dma_start(out=dbgo[:, TG:2*TG],
                                                       in_=xnb[:, 0:TG]), inc=(s_st, 16))
                P.emit("gpsimd", lambda e: e.dma_start(out=dbgo[:, 2*TG:3*TG],
                                                       in_=hg[:, 0:TG]), inc=(s_st, 16))
                P.emit("sync", lambda e: e.dma_start(out=dbgo[:, 3*TG:4*TG],
                                                     in_=acc[:, 0:TG]), inc=(s_st, 16))
                P.emit("sync", lambda e: e.dma_start(out=dbgo[0:1, 2*TG:3*TG],
                                                     in_=mrow[:, 0:TG]), inc=(s_st, 16))
                P.emit("sync", lambda e: e.dma_start(out=dbgo[1:2, 2*TG:3*TG],
                                                     in_=vrow[:, 0:TG]), inc=(s_st, 16))

            tg_list = [(0, 0, OVL)] + [(1 + k, OVL + k * TG, TG) for k in range(4)]

            def ap_x_load(c, col0, n):
                return x0T[128 * c:128 * c + 128, col0:col0 + n]

            def post_tg(t_idx, b, n, res_done):
                def xap1(c, b=b, n=n):
                    return xf[:, (b * 8 + c) * TG:(b * 8 + c) * TG + n]

                st["post_st"] = sem["st"].n
                if t_idx == 0:
                    dn = emit_ln_single(
                        P, sem, pb, st,
                        lambda c, b=b: xf[:, (b * 8 + c) * TG + OVL - 1:
                                          (b * 8 + c) * TG + OVL],
                        lambda c: H2[:, c * 2050:c * 2050 + 1],
                        bufs["scr"], rows, consts, pre_dv=max(res_done))
                    st["post_dv"] = dn
                else:
                    k = t_idx - 1
                    for c in range(8):
                        P.emit("sync", lambda e, c=c, k=k, b=b: e.dma_start(
                            out=x1o[128 * c:128 * c + 128, k * TG:k * TG + TG],
                            in_=xf[:, (b * 8 + c) * TG:(b * 8 + c) * TG + TG]),
                            waits=[(s_dv, res_done[c])], inc=(s_st, 16))
                    st["post_st"] = sem["st"].n
                    dn = emit_ln(
                        P, sem, pb, st, xap1,
                        lambda c, k=k: H2[:, c * 2050 + 1 + k * TG:
                                          c * 2050 + 1 + k * TG + TG],
                        TG, bufs["scr"], rows, consts,
                        pre_dv=max(res_done))
                    st["post_dv"] = dn

            emit_ffn(P, sem, pb, st, bufs, tg_list, ap_x_load, post_tg, consts)
            dump_debug()
            P.emit("sync", lambda e: e.dma_start(out=c8o, in_=C8[:]),
                   waits=[(s_dv, c8_dv)], inc=(s_st, 16))

            ph_dv, ph_ac, ph_kp, ph_st, ph_pe = s_dv.n, s_ac.n, s_kp.n, s_st.n, s_pe.n

        with ExitStack() as ctx2:
            ec2 = ctx2.enter_context
            Kt = ec2(nc.sbuf_tensor("Kt", [128, 8 * T_OWN], BF16))
            Vt = ec2(nc.sbuf_tensor("Vt", [128, 16 * D], BF16))
            wb8 = ec2(nc.sbuf_tensor("wb8", [128, 2 * 8 * D], BF16))
            dmix = ec2(nc.sbuf_tensor("dmix", [128, 8 * T_OWN], BF16))
            xm = ec2(nc.sbuf_tensor("xm", [128, 2 * 8 * TG], BF16))
            stg = ec2(nc.sbuf_tensor("stg", [128, 4 * TG], BF16))
            ktr = ec2(nc.sbuf_tensor("ktr", [128, 2 * D], BF16))
            kwb = ec2(nc.sbuf_tensor("kwb", [128, 2 * D], BF16))
            pp0 = ec2(nc.psum_tensor("pp0", [128, TG], F32))
            pp1 = ec2(nc.psum_tensor("pp1", [128, TG], F32))
            pc0 = ec2(nc.psum_tensor("pc0", [64, 64], F32))
            pc1 = ec2(nc.psum_tensor("pc1", [64, 64], F32))
            ldrow = ec2(nc.sbuf_tensor("ldrow", [1, 16], F32))
            ldb = ec2(nc.sbuf_tensor("ldb", [128, 16], F32))
            wkp = ec2(nc.sbuf_tensor("wkp", [128, 16], F32))
            wst = ec2(nc.sbuf_tensor("wst", [128, 8], F32))
            ldk = ec2(nc.sbuf_tensor("ldk", [128, 8], F32))
            Sst = ec2(nc.sbuf_tensor("Sst", [128, 8 * 64], F32))

            for eng in ("sync", "gpsimd", "tensor", "vector", "scalar"):
                P.wait(eng, s_dv, ph_dv)
                P.wait(eng, s_ac, ph_ac)
                P.wait(eng, s_kp, ph_kp)
                P.wait(eng, s_st, ph_st)

            P.emit("sync", lambda e: e.dma_start(out=ldrow[:], in_=td), inc=(s_ld, 16))
            ldn = s_ld.n
            P.emit("scalar", lambda e: e.activation(out=ldrow[:], in_=ldrow[:], func=AF.Exp),
                   waits=[(s_ld, ldn)], inc=(s_ac, 1))
            P.emit("vector", lambda e: e.tensor_scalar_mul(out=ldrow[:], in0=ldrow[:],
                                                           scalar1=-1.0),
                   waits=[(s_ac, s_ac.n)], inc=(s_dv, 1))
            P.emit("tensor", lambda e: e.matmul(bc0[:, 0:16], ones_f32[:], ldrow[:],
                                                start=True, stop=True),
                   waits=[(s_dv, s_dv.n), (s_dv, st["bc_free_dv"])], inc=(s_pe, 1))
            P.emit("vector", lambda e: e.tensor_copy(out=ldb[:], in_=bc0[:, 0:16]),
                   waits=[(s_pe, s_pe.n)], inc=(s_dv, 1))
            st["bc_free_dv"] = s_dv.n
            P.emit("vector", lambda e: e.tensor_scalar_mul(out=wkp[:], in0=ldb[:],
                                                           scalar1=riota[:]),
                   waits=([(s_dv, s_dv.n)] if SAFE_CHAIN else ()), inc=(s_dv, 1))
            P.emit("scalar", lambda e: e.activation(out=wkp[:], in_=wkp[:], func=AF.Exp),
                   waits=[(s_dv, s_dv.n)], inc=(s_ac, 1))
            for c in range(8):
                for hh in range(2):
                    sl = slice(64 * hh, 64 * hh + 64)
                    P.emit("vector", lambda e, c=c, hh=hh, sl=sl: e.tensor_copy(
                        out=ldk[sl, c:c + 1], in_=ldb[sl, 2 * c + hh:2 * c + hh + 1]),
                        waits=([(s_dv, s_dv.n)] if SAFE_CHAIN else ()), inc=(s_dv, 1))
            P.emit("scalar", lambda e: e.activation(out=wst[:], in_=ldk[:], func=AF.Exp,
                                                    scale=128.0),
                   waits=[(s_dv, s_dv.n), (s_ac, s_ac.n)], inc=(s_ac, 1))
            P.emit("vector", lambda e: e.memset(Sst[:], 0.0), inc=(s_dv, 1))

            for k in range(4):
                for c in range(8):
                    x1ap = H2[:, c * 2050 + 1 + k * TG: c * 2050 + 1 + k * TG + TG]
                    xxap = H2[:, c * 2050 + 0 + k * TG: c * 2050 + 0 + k * TG + TG]
                    dap = dmix[:, (c * 4 + k) * TG:(c * 4 + k) * TG + TG]
                    P.emit("vector", lambda e, dap=dap, x1ap=x1ap, xxap=xxap:
                           e.tensor_sub(out=dap, in0=x1ap, in1=xxap), inc=(s_dv, 1))
            dmix_done = s_dv.n

            def emit_mix(tgt_i, k, c, guard_pe):
                xxap = H2[:, c * 2050 + 0 + k * TG: c * 2050 + 0 + k * TG + TG]
                dap = dmix[:, (c * 4 + k) * TG:(c * 4 + k) * TG + TG]
                xb0 = (k % 2) * 8 * TG
                out = xm[:, xb0 + c * TG:xb0 + c * TG + TG]
                tm = tmixs[:, tgt_i * 8 + c:tgt_i * 8 + c + 1]
                return P.emit("vector", lambda e, out=out, dap=dap, tm=tm, xxap=xxap:
                              e.scalar_tensor_tensor(out=out, in0=dap, scalar=tm, in1=xxap,
                                                     op0=ALU.mult, op1=ALU.add),
                              waits=[(s_dv, dmix_done), (s_pe, guard_pe)], inc=(s_dv, 1))

            wld_map = {}

            def load_weights(tgt_i, wsrc, guard):
                wb0 = (tgt_i % 2) * 8 * D
                for ci in range(8):
                    P.emit("gpsimd", lambda e, ci=ci, wsrc=wsrc, wb0=wb0: e.dma_start(
                        out=wb8[:, wb0 + ci * D:wb0 + ci * D + D],
                        in_=wsrc[128 * ci:128 * ci + 128, :]),
                        waits=([(s_pe, guard)] if ci == 0 else ()), inc=(s_cp, 16))
                wld_map[tgt_i] = s_cp.n

            pp = [pp0, pp1]
            pp_free = [[], []]
            stg_free_st = [0, 0, 0, 0]

            load_weights(0, wrt, 0)
            load_weights(1, wkt, 0)
            xm_free = [0, 0]
            for tgt_i, (wsrc, resident, dram_out) in enumerate(
                    [(wrt, None, Ro), (wkt, Kt, Ko)]):
                wld = wld_map[tgt_i]
                wb0 = (tgt_i % 2) * 8 * D
                for k in range(4):
                    for c in range(8):
                        emit_mix(tgt_i, k, c, xm_free[k % 2])
                    mixn = s_dv.n
                    for o in range(8):
                        stp = tgt_i * 32 + k * 8 + o
                        bank = pp[stp % 2]
                        for ci in range(8):
                            lhsT = wb8[:, wb0 + ci * D + o * 128: wb0 + ci * D + o * 128 + 128]
                            rhs = xm[:, (k % 2) * 8 * TG + ci * TG:(k % 2) * 8 * TG + ci * TG + TG]
                            w0 = ([(s_cp, wld), (s_dv, mixn)] + pp_free[stp % 2]
                                  if ci == 0 else ())
                            P.emit("tensor", lambda e, bank=bank, lhsT=lhsT, rhs=rhs,
                                   ci=ci: e.matmul(bank[:], lhsT, rhs, start=(ci == 0),
                                                   stop=(ci == 7)),
                                   waits=w0, inc=((s_pe, 1) if ci == 7 else None))
                        pen = s_pe.n
                        if o == 7:
                            xm_free[k % 2] = pen
                        if resident is not None:
                            dst = resident[:, o * T_OWN + k * TG: o * T_OWN + k * TG + TG]
                            dn = P.emit("vector", lambda e, dst=dst, bank=bank:
                                        e.tensor_copy(out=dst, in_=bank[:]),
                                        waits=[(s_pe, pen)], inc=(s_dv, 1))
                            P.emit("sync", lambda e, dst=dst, o=o, k=k,
                                   dram_out=dram_out: e.dma_start(
                                       out=dram_out[128 * o:128 * o + 128,
                                                    k * TG:k * TG + TG], in_=dst),
                                   waits=[(s_dv, dn)], inc=(s_st, 16))
                            pp_free[stp % 2] = [(s_dv, dn)]
                        else:
                            sslot = stp % 4
                            dst = stg[:, sslot * TG:sslot * TG + TG]
                            dn = P.emit("vector", lambda e, dst=dst, bank=bank:
                                        e.tensor_copy(out=dst, in_=bank[:]),
                                        waits=[(s_pe, pen),
                                               (s_st, stg_free_st[sslot])],
                                        inc=(s_dv, 1))
                            stn = P.emit("sync", lambda e, dst=dst, o=o, k=k,
                                         dram_out=dram_out: e.dma_start(
                                             out=dram_out[128 * o:128 * o + 128,
                                                          k * TG:k * TG + TG], in_=dst),
                                         waits=[(s_dv, dn)], inc=(s_st, 16))
                            stg_free_st[sslot] = stn
                            pp_free[stp % 2] = [(s_dv, dn)]

            load_weights(2, wvt, wld_map[0] * 0 + s_pe.n)
            for tgt_i, (wsrc, resident, dram_out, act) in enumerate(
                    [(wvt, Vt, Vo, None), (wgt, None, Go, AF.Silu)], start=2):
                if tgt_i == 3:
                    load_weights(3, wgt, s_pe.n)
                wld = wld_map[tgt_i]
                wb0 = (tgt_i % 2) * 8 * D
                for k in range(4):
                    for c in range(8):
                        emit_mix(tgt_i, k, c, xm_free[k % 2])
                    mixn = s_dv.n
                    for tt in range(4):
                        t_glob = k * 4 + tt
                        for oh in range(2):
                            stp = tt * 2 + oh
                            bank = pp[stp % 2]
                            for ci in range(8):
                                lhsT = xm[:, (k % 2) * 8 * TG + ci * TG + tt * 128:
                                          (k % 2) * 8 * TG + ci * TG + tt * 128 + 128]
                                rhs = wb8[:, wb0 + ci * D + oh * 512: wb0 + ci * D + oh * 512 + 512]
                                w0 = ([(s_cp, wld), (s_dv, mixn)] + pp_free[stp % 2]
                                      if ci == 0 else ())
                                P.emit("tensor", lambda e, bank=bank, lhsT=lhsT, rhs=rhs,
                                       ci=ci: e.matmul(bank[:], lhsT, rhs,
                                                       start=(ci == 0), stop=(ci == 7)),
                                       waits=w0, inc=((s_pe, 1) if ci == 7 else None))
                            pen = s_pe.n
                            if tt == 3 and oh == 1:
                                xm_free[k % 2] = pen
                            if resident is not None:
                                dst = resident[:, t_glob * D + oh * 512:
                                               t_glob * D + oh * 512 + 512]
                                dn = P.emit("vector", lambda e, dst=dst, bank=bank:
                                            e.tensor_copy(out=dst, in_=bank[:]),
                                            waits=[(s_pe, pen)], inc=(s_dv, 1))
                                P.emit("sync", lambda e, dst=dst, t_glob=t_glob, oh=oh,
                                       dram_out=dram_out: e.dma_start(
                                           out=dram_out[t_glob * 128:t_glob * 128 + 128,
                                                        oh * 512:oh * 512 + 512],
                                           in_=dst),
                                       waits=[(s_dv, dn)], inc=(s_st, 16))
                                pp_free[stp % 2] = [(s_dv, dn)]
                            else:
                                sslot = stp % 4
                                dst = stg[:, sslot * TG:sslot * TG + TG]
                                an = P.emit("scalar", lambda e, dst=dst, bank=bank:
                                            e.activation(out=dst, in_=bank[:],
                                                         func=AF.Silu),
                                            waits=[(s_pe, pen),
                                                   (s_st, stg_free_st[sslot])],
                                            inc=(s_ac, 1))
                                stn = P.emit("sync", lambda e, dst=dst, t_glob=t_glob,
                                             oh=oh, dram_out=dram_out: e.dma_start(
                                                 out=dram_out[
                                                     t_glob * 128:t_glob * 128 + 128,
                                                     oh * 512:oh * 512 + 512], in_=dst),
                                             waits=[(s_ac, an)], inc=(s_st, 16))
                                stg_free_st[sslot] = stn
                                pp_free[stp % 2] = [(s_ac, an)]

            kv_done_dv = s_dv.n

            kw_ready = []
            chunk_pe = []
            chunk_st = []
            pc_free = [[], []]
            for n_ch in range(NCHUNK):
                nb = n_ch % 2
                for c in range(8):
                    src = Kt[:, c * T_OWN + n_ch * 128: c * T_OWN + n_ch * 128 + 128]
                    dst = ktr[:, nb * D + c * 128: nb * D + c * 128 + 128]
                    w = ([(s_dv, kw_ready[n_ch - 2])] if n_ch >= 2
                         else [(s_dv, kv_done_dv)])
                    P.emit("sync", lambda e, dst=dst, src=src: e.dma_start(
                        out=dst, in_=src, transpose=True), waits=w, inc=(s_tr, 16))
                trn = s_tr.n
                for c in range(8):
                    for hh in range(2):
                        src = ktr[:, nb * D + c * 128 + 64 * hh:
                                  nb * D + c * 128 + 64 * hh + 64]
                        dst = kwb[:, nb * D + c * 128 + 64 * hh:
                                  nb * D + c * 128 + 64 * hh + 64]
                        sc = wkp[:, 2 * c + hh:2 * c + hh + 1]
                        w2 = [(s_tr, trn)]
                        if n_ch >= 2:
                            w2 += [(s_pe, chunk_pe[n_ch - 2]), (s_st, chunk_st[n_ch - 2])]
                        P.emit("vector", lambda e, dst=dst, src=src, sc=sc:
                               e.tensor_scalar_mul(out=dst, in0=src, scalar1=sc),
                               waits=w2, inc=(s_dv, 1))
                kw_ready.append(s_dv.n)
                P.emit("sync", lambda e, nb=nb, n_ch=n_ch: e.dma_start(
                    out=Kwo[n_ch * 128:n_ch * 128 + 128, :],
                    in_=kwb[:, nb * D:nb * D + D]),
                    waits=[(s_dv, kw_ready[-1])], inc=(s_st, 16))
                chunk_st.append(s_st.n)
                for c in range(8):
                    pens = []
                    for hh, bank in ((0, pc0), (1, pc1)):
                        lhsT = kwb[:, nb * D + c * 128 + 64 * hh:
                                   nb * D + c * 128 + 64 * hh + 64]
                        rhs = Vt[:, n_ch * D + (2 * c + hh) * 64:
                                 n_ch * D + (2 * c + hh) * 64 + 64]
                        pn = P.emit("tensor", lambda e, bank=bank, lhsT=lhsT, rhs=rhs:
                                    e.matmul(bank[:], lhsT, rhs, start=True, stop=True),
                                    waits=[(s_dv, kw_ready[-1])] + pc_free[hh],
                                    inc=(s_pe, 1))
                        pens.append(pn)
                    for hh, bank in ((0, pc0), (1, pc1)):
                        ssl = Sst[64 * hh:64 * hh + 64, c * 64:c * 64 + 64]
                        wsl = wst[64 * hh:64 * hh + 64, c:c + 1]
                        dn = P.emit("vector", lambda e, ssl=ssl, wsl=wsl, bank=bank:
                                    e.scalar_tensor_tensor(out=ssl, in0=ssl, scalar=wsl,
                                                           in1=bank[:], op0=ALU.mult,
                                                           op1=ALU.add),
                                    waits=[(s_pe, pens[hh]), (s_dv, s_dv.n)],
                                    inc=(s_dv, 1))
                        pc_free[hh] = [(s_dv, dn)]
                chunk_pe.append(s_pe.n)

            bdone = s_dv.n
            for c in range(8):
                P.emit("sync", lambda e, c=c: e.dma_start(
                    out=Bo[c], in_=Sst[:, c * 64:c * 64 + 64]),
                    waits=[(s_dv, bdone)], inc=(s_st, 16))

            P.wait("gpsimd", s_st, s_st.n)
            P.wait("gpsimd", s_tr, s_tr.n)

        with nc.Block() as block:
            P.build(block)

    return nc



def build_l2():
    nc = bass.Bass("TRN2", target_bir_lowering=False)

    x1i = nc.dram_tensor("x1i", [D, T_OWN], F32, kind="ExternalInput").ap()
    Ri = nc.dram_tensor("Ri", [D, T_OWN], BF16, kind="ExternalInput").ap()
    Ki = nc.dram_tensor("Ki", [D, T_OWN], BF16, kind="ExternalInput").ap()
    Gi = nc.dram_tensor("Gi", [T_OWN, D], BF16, kind="ExternalInput").ap()
    Vi = nc.dram_tensor("Vi", [T_OWN, D], BF16, kind="ExternalInput").ap()
    Kwi = nc.dram_tensor("Kwi", [T_OWN, D], BF16, kind="ExternalInput").ap()
    Ball = nc.dram_tensor("Ball", [128, 4096], F32, kind="ExternalInput").ap()
    cpstream_d = nc.dram_tensor("cpstream", [128, 131072], F32, kind="ExternalInput").ap()
    c8i = nc.dram_tensor("c8i", [128, 32 * TG], BF16, kind="ExternalInput").ap()
    wot = nc.dram_tensor("wot", [D, D], F32, kind="ExternalInput").ap()
    td = nc.dram_tensor("td", [1, 16], F32, kind="ExternalInput").ap()
    tf = nc.dram_tensor("tf", [1, 16], F32, kind="ExternalInput").ap()
    lnw_d = nc.dram_tensor("lnw", [1, D], F32, kind="ExternalInput").ap()
    lnb_d = nc.dram_tensor("lnb", [1, D], F32, kind="ExternalInput").ap()
    expo_d = nc.dram_tensor("expo", [1, 8], F32, kind="ExternalInput").ap()
    mask8_d = nc.dram_tensor("mask8", [1, 8], F32, kind="ExternalInput").ap()

    x3o = nc.dram_tensor("x3o", [D, T_OWN], F32, kind="ExternalOutput").ap()
    x2o = nc.dram_tensor("x2o", [D, T_OWN], F32, kind="ExternalOutput").ap()
    dbg2 = nc.dram_tensor("dbg2", [128, 4 * TG], F32, kind="ExternalOutput").ap()

    x2d = nc.dram_tensor("x2d", [D, T_OWN], F32, kind="Internal")

    ones_bf_h = nc.inline_tensor(np.ones((128, 1), dtype=BF), "c_ones_bf")
    ones_f32_h = nc.inline_tensor(np.ones((1, 128), dtype=np.float32), "c_ones_f32")
    epst_h = nc.inline_tensor(np.full((128, 1), EPS, dtype=np.float32), "c_eps")
    ii = np.arange(128, dtype=np.float32)
    d1t_h = nc.inline_tensor(np.maximum(ii[None, :] - ii[:, None] - 1.0, 0.0), "c_d1t")
    maskt_h = nc.inline_tensor((ii[None, :] > ii[:, None]).astype(np.float32), "c_maskt")
    iotaf_h = nc.inline_tensor(np.broadcast_to(ii[None, :], (128, 128)).copy(), "c_iotaf")
    eye_h = nc.inline_tensor(np.eye(128, dtype=np.float32), "c_eye")

    with ExitStack() as ctx:
        ec = ctx.enter_context
        ones_bf = ec(nc.sbuf_tensor("ones_bf", [128, 1], BF16))
        ones_f32 = ec(nc.sbuf_tensor("ones_f32", [1, 128], F32))
        epst = ec(nc.sbuf_tensor("epst", [128, 1], F32))
        xb_s = ec(nc.sbuf_tensor("xb_s", [128, TG], BF16))
        sq_s = ec(nc.sbuf_tensor("sq_s", [128, TG], BF16))
        d_s = ec(nc.sbuf_tensor("d_s", [128, TG], F32))
        mrow = ec(nc.sbuf_tensor("mrow", [1, TG], F32))
        vrow = ec(nc.sbuf_tensor("vrow", [1, TG], F32))
        h_ld = ec(nc.semaphore("s_ld"))
        h_cp = ec(nc.semaphore("s_cp"))
        h_pe = ec(nc.semaphore("s_pe"))
        h_dv = ec(nc.semaphore("s_dv"))
        h_ac = ec(nc.semaphore("s_ac"))
        h_st = ec(nc.semaphore("s_st"))
        h_tr = ec(nc.semaphore("s_tr"))
        h_xf = ec(nc.semaphore("s_xf"))
        h_kp = ec(nc.semaphore("s_kp"))

        P = Plan(nc)
        sem = {"ld": Sem(h_ld), "cp": Sem(h_cp), "pe": Sem(h_pe), "dv": Sem(h_dv),
               "ac": Sem(h_ac), "st": Sem(h_st), "tr": Sem(h_tr), "xfree": Sem(h_xf),
               "kp": Sem(h_kp)}
        s_ld, s_cp, s_pe = sem["ld"], sem["cp"], sem["pe"]
        s_dv, s_ac, s_st, s_tr, s_kp = (sem["dv"], sem["ac"], sem["st"], sem["tr"],
                                        sem["kp"])
        consts = {"ones_bf": ones_bf, "ones_f32": ones_f32, "epst": epst}
        pb = {}
        st = {"stats_free_ac": 0, "stats_free_dv": 0, "bc_free_dv": 0,
              "xf_base": 0, "slot_ctr": 0,
              "slot_reader_step": [0] * HG_SLOTS, "gelu_done": [0] * HG_SLOTS,
              "rail_gelu": [0, 0], "cp_bank_dv": [0, 0], "post_dv": 0,
              "post_st": 0, "prev_g1": 0, "ygtm_tr": 0}
        rows = {"m": mrow, "v": vrow}

        for dst, src in [(ones_bf, ones_bf_h.ap()), (ones_f32, ones_f32_h.ap()),
                         (epst, epst_h.ap())]:
            P.emit("sync", lambda e, dst=dst, src=src: e.dma_start(out=dst[:], in_=src),
                   inc=(s_ld, 16))

        with ExitStack() as ctxa:
            eca = ctxa.enter_context
            Rt = eca(nc.sbuf_tensor("Rt", [128, 8 * T_OWN], BF16))
            Kt = eca(nc.sbuf_tensor("Kt", [128, 8 * T_OWN], BF16))
            Vt = eca(nc.sbuf_tensor("Vt", [128, 16 * D], BF16))
            gt2 = eca(nc.sbuf_tensor("gt2", [128, 2 * D], BF16))
            kwt = eca(nc.sbuf_tensor("kwt", [128, 2 * D], BF16))
            wmt = eca(nc.sbuf_tensor("wmt", [128, 16 * 128], BF16))
            wbt = eca(nc.sbuf_tensor("wbt", [128, 8 * 128], BF16))
            lnwb = eca(nc.sbuf_tensor("lnwb", [128, D], BF16))
            lnbb = eca(nc.sbuf_tensor("lnbb", [128, D], BF16))
            lnwrow = eca(nc.sbuf_tensor("lnwrow", [1, D], F32))
            lnbrow = eca(nc.sbuf_tensor("lnbrow", [1, D], F32))
            d1t = eca(nc.sbuf_tensor("d1t", [128, 128], F32))
            maskt = eca(nc.sbuf_tensor("maskt", [128, 128], F32))
            iotaf = eca(nc.sbuf_tensor("iotaf", [128, 128], F32))
            eyet = eca(nc.sbuf_tensor("eyet", [128, 128], F32))
            ldrow = eca(nc.sbuf_tensor("ldrow", [1, 16], F32))
            ufrow = eca(nc.sbuf_tensor("ufrow", [1, 16], F32))
            ldb = eca(nc.sbuf_tensor("ldb", [128, 16], F32))
            ufb = eca(nc.sbuf_tensor("ufb", [128, 16], F32))
            ldk = eca(nc.sbuf_tensor("ldk", [128, 8], F32))
            wst = eca(nc.sbuf_tensor("wst", [128, 8], F32))
            Sst = eca(nc.sbuf_tensor("Sst", [128, 8 * 64], F32))
            Sbf = eca(nc.sbuf_tensor("Sbf", [128, 8 * 64], BF16))
            exb = eca(nc.sbuf_tensor("exb", [128, 8], F32))
            mkb = eca(nc.sbuf_tensor("mkb", [128, 8], F32))
            cofb = eca(nc.sbuf_tensor("cofb", [128, 8 * 8], F32))
            wo8 = eca(nc.sbuf_tensor("wo8", [128, 8 * D], BF16))
            attwb = eca(nc.sbuf_tensor("attwb", [128, 2 * 2048], BF16))
            rwbb = eca(nc.sbuf_tensor("rwbb", [128, 2 * D], BF16))
            xof = eca(nc.sbuf_tensor("xof", [128, 2 * D], F32))
            gnscr = eca(nc.sbuf_tensor("gnscr", [128, D], F32))
            ygtm2 = eca(nc.sbuf_tensor("ygtm2", [128, D], BF16))
            srow = eca(nc.sbuf_tensor("srow", [128, 96], F32))
            ygtm = eca(nc.sbuf_tensor("ygtm", [128, D], BF16))
            ygfm = eca(nc.sbuf_tensor("ygfm", [128, 8 * 512], BF16))
            x1t = eca(nc.sbuf_tensor("x1t", [128, 8 * 512], F32))
            x2t = eca(nc.sbuf_tensor("x2t", [128, 512], F32))
            setup_ctx = ExitStack()
            bc0 = setup_ctx.enter_context(nc.psum_tensor("bcA0", [128, TG], F32))
            bc1 = setup_ctx.enter_context(nc.psum_tensor("bcA1", [128, TG], F32))

            P.emit("sync", lambda e: e.dma_start(out=d1t[:], in_=d1t_h.ap()), inc=(s_ld, 16))
            P.emit("sync", lambda e: e.dma_start(out=maskt[:], in_=maskt_h.ap()), inc=(s_ld, 16))
            P.emit("sync", lambda e: e.dma_start(out=iotaf[:], in_=iotaf_h.ap()), inc=(s_ld, 16))
            P.emit("sync", lambda e: e.dma_start(out=eyet[:], in_=eye_h.ap()), inc=(s_ld, 16))
            P.emit("sync", lambda e: e.dma_start(out=ldrow[:], in_=td), inc=(s_ld, 16))
            P.emit("sync", lambda e: e.dma_start(out=ufrow[:], in_=tf), inc=(s_ld, 16))
            P.emit("sync", lambda e: e.dma_start(out=lnwrow[:], in_=lnw_d), inc=(s_ld, 16))
            P.emit("sync", lambda e: e.dma_start(out=lnbrow[:], in_=lnb_d), inc=(s_ld, 16))
            P.emit("sync", lambda e: e.dma_start(out=exb[0:1, :], in_=expo_d), inc=(s_ld, 16))
            P.emit("sync", lambda e: e.dma_start(out=mkb[0:1, :], in_=mask8_d), inc=(s_ld, 16))
            for c in range(8):
                P.emit("sync", lambda e, c=c: e.dma_start(
                    out=Rt[:, c * T_OWN:(c + 1) * T_OWN], in_=Ri[128 * c:128 * c + 128, :]),
                    inc=(s_ld, 16))
                P.emit("sync", lambda e, c=c: e.dma_start(
                    out=Kt[:, c * T_OWN:(c + 1) * T_OWN], in_=Ki[128 * c:128 * c + 128, :]),
                    inc=(s_ld, 16))
            for tt in range(16):
                P.emit("sync", lambda e, tt=tt: e.dma_start(
                    out=Vt[:, tt * D:(tt + 1) * D], in_=Vi[tt * 128:tt * 128 + 128, :]),
                    inc=(s_ld, 16))
            for c in range(8):
                P.emit("gpsimd", lambda e, c=c: e.dma_start(
                    out=wo8[:, c * D:(c + 1) * D], in_=wot[128 * c:128 * c + 128, :]),
                    inc=(s_cp, 16))
            const_ld = s_ld.n

            P.emit("scalar", lambda e: e.activation(out=ldrow[:], in_=ldrow[:], func=AF.Exp),
                   waits=[(s_ld, const_ld)], inc=(s_ac, 1))
            P.emit("vector", lambda e: e.tensor_scalar_mul(out=ldrow[:], in0=ldrow[:],
                                                           scalar1=-1.0),
                   waits=[(s_ac, s_ac.n)], inc=(s_dv, 1))
            P.emit("tensor", lambda e: e.matmul(bc0[:, 0:16], ones_f32[:], ldrow[:],
                                                start=True, stop=True),
                   waits=[(s_dv, s_dv.n)], inc=(s_pe, 1))
            P.emit("tensor", lambda e: e.matmul(bc1[:, 0:16], ones_f32[:], ufrow[:],
                                                start=True, stop=True), inc=(s_pe, 1))
            P.emit("vector", lambda e: e.tensor_copy(out=ldb[:], in_=bc0[:, 0:16]),
                   waits=[(s_pe, s_pe.n)], inc=(s_dv, 1))
            P.emit("vector", lambda e: e.tensor_copy(out=ufb[:], in_=bc1[:, 0:16]),
                   waits=([(s_dv, s_dv.n)] if SAFE_CHAIN else ()), inc=(s_dv, 1))
            st["bc_free_dv"] = s_dv.n
            for c in range(8):
                for hh in range(2):
                    sl = slice(64 * hh, 64 * hh + 64)
                    P.emit("vector", lambda e, c=c, hh=hh, sl=sl: e.tensor_copy(
                        out=ldk[sl, c:c + 1], in_=ldb[sl, 2 * c + hh:2 * c + hh + 1]),
                        waits=([(s_dv, s_dv.n)] if SAFE_CHAIN else ()), inc=(s_dv, 1))
            P.emit("scalar", lambda e: e.activation(out=wst[:], in_=ldk[:], func=AF.Exp,
                                                    scale=128.0),
                   waits=[(s_dv, s_dv.n), (s_ac, s_ac.n)], inc=(s_ac, 1))
            for h in range(16):
                dst = wmt[:, h * 128:(h + 1) * 128]
                P.emit("vector", lambda e, dst=dst, h=h: e.tensor_scalar_mul(
                    out=dst, in0=d1t[:], scalar1=ldb[:, h:h + 1]),
                    waits=[(s_dv, s_dv.n)], inc=(s_dv, 1))
                P.emit("scalar", lambda e, dst=dst: e.activation(out=dst, in_=dst,
                                                                 func=AF.Exp),
                       waits=[(s_dv, s_dv.n), (s_ac, s_ac.n)], inc=(s_ac, 1))
                P.emit("vector", lambda e, dst=dst: e.tensor_mul(out=dst, in0=dst,
                                                                 in1=maskt[:]),
                       waits=[(s_ac, s_ac.n)], inc=(s_dv, 1))
                P.emit("vector", lambda e, h=h: e.tensor_scalar_mul(
                    out=gnscr[:, 0:128], in0=eyet[:], scalar1=ufb[:, h:h + 1]),
                    waits=([(s_dv, s_dv.n)] if SAFE_CHAIN else ()), inc=(s_dv, 1))
                P.emit("vector", lambda e, dst=dst: e.tensor_add(out=dst, in0=dst,
                                                                 in1=gnscr[:, 0:128]),
                       waits=[(s_dv, s_dv.n)], inc=(s_dv, 1))
            for c in range(8):
                dst = wbt[:, c * 128:(c + 1) * 128]
                P.emit("vector", lambda e, dst=dst, c=c: e.tensor_scalar_mul(
                    out=dst, in0=iotaf[:], scalar1=ldk[:, c:c + 1]),
                    waits=[(s_dv, s_dv.n)], inc=(s_dv, 1))
                P.emit("scalar", lambda e, dst=dst: e.activation(out=dst, in_=dst,
                                                                 func=AF.Exp),
                       waits=[(s_dv, s_dv.n), (s_ac, s_ac.n)], inc=(s_ac, 1))
            wmt_done_ac = s_ac.n
            for half in range(2):
                P.emit("tensor", lambda e, half=half: e.matmul(
                    bc0[:, 0:512], ones_f32[:], lnwrow[0:1, half * 512:half * 512 + 512],
                    start=True, stop=True),
                    waits=[(s_ld, const_ld), (s_dv, st["bc_free_dv"])], inc=(s_pe, 1))
                P.emit("tensor", lambda e, half=half: e.matmul(
                    bc1[:, 0:512], ones_f32[:], lnbrow[0:1, half * 512:half * 512 + 512],
                    start=True, stop=True), inc=(s_pe, 1))
                pen = s_pe.n
                P.emit("vector", lambda e, half=half: e.tensor_copy(
                    out=lnwb[:, half * 512:half * 512 + 512], in_=bc0[:, 0:512]),
                    waits=[(s_pe, pen)], inc=(s_dv, 1))
                P.emit("vector", lambda e, half=half: e.tensor_copy(
                    out=lnbb[:, half * 512:half * 512 + 512], in_=bc1[:, 0:512]),
                    waits=([(s_dv, s_dv.n)] if SAFE_CHAIN else ()), inc=(s_dv, 1))
                st["bc_free_dv"] = s_dv.n
            P.emit("tensor", lambda e: e.matmul(bc0[:, 0:8], ones_f32[:], exb[0:1, :],
                                                start=True, stop=True),
                   waits=[(s_dv, st["bc_free_dv"])], inc=(s_pe, 1))
            P.emit("tensor", lambda e: e.matmul(bc1[:, 0:8], ones_f32[:], mkb[0:1, :],
                                                start=True, stop=True), inc=(s_pe, 1))
            pen = s_pe.n
            P.emit("vector", lambda e: e.tensor_copy(out=exb[:], in_=bc0[:, 0:8]),
                   waits=[(s_pe, pen)], inc=(s_dv, 1))
            P.emit("vector", lambda e: e.tensor_copy(out=mkb[:], in_=bc1[:, 0:8]),
                   waits=([(s_dv, s_dv.n)] if SAFE_CHAIN else ()), inc=(s_dv, 1))
            st["bc_free_dv"] = s_dv.n
            for c in range(8):
                dst = cofb[:, c * 8:(c + 1) * 8]
                P.emit("vector", lambda e, dst=dst, c=c: e.tensor_scalar_mul(
                    out=dst, in0=exb[:], scalar1=ldk[:, c:c + 1]),
                    waits=[(s_dv, s_dv.n)], inc=(s_dv, 1))
                P.emit("scalar", lambda e, dst=dst: e.activation(out=dst, in_=dst,
                                                                 func=AF.Exp),
                       waits=[(s_dv, s_dv.n), (s_ac, s_ac.n)], inc=(s_ac, 1))
                P.emit("vector", lambda e, dst=dst: e.tensor_mul(out=dst, in0=dst,
                                                                 in1=mkb[:]),
                       waits=[(s_ac, s_ac.n)], inc=(s_dv, 1))
            bts = x1t
            P.emit("sync", lambda e: e.dma_start(out=bts[:], in_=Ball), inc=(s_ld, 16))
            ldn = s_ld.n
            for c in range(8):
                for r in range(8):
                    bsl = bts[:, (r * 8 + c) * 64:(r * 8 + c) * 64 + 64]
                    ssl = Sst[:, c * 64:c * 64 + 64]
                    cf = cofb[:, c * 8 + r:c * 8 + r + 1]
                    if r == 0:
                        P.emit("vector", lambda e, ssl=ssl, cf=cf, bsl=bsl:
                               e.tensor_scalar_mul(out=ssl, in0=bsl, scalar1=cf),
                               waits=[(s_ld, ldn), (s_dv, s_dv.n)], inc=(s_dv, 1))
                    else:
                        P.emit("vector", lambda e, ssl=ssl, cf=cf, bsl=bsl:
                               e.scalar_tensor_tensor(out=ssl, in0=bsl, scalar=cf,
                                                      in1=ssl, op0=ALU.mult, op1=ALU.add),
                               waits=[(s_ld, ldn), (s_dv, s_dv.n)], inc=(s_dv, 1))
            for c in range(8):
                P.emit("vector", lambda e, c=c: e.tensor_copy(
                    out=Sbf[:, c * 64:c * 64 + 64], in_=Sst[:, c * 64:c * 64 + 64]),
                    waits=[(s_dv, s_dv.n)], inc=(s_dv, 1))
            sin_done = s_dv.n
            setup_ctx.close()
            psA = eca(nc.psum_tensor("psA", [128, 128], F32))
            psB = eca(nc.psum_tensor("psB", [128, 128], F32))
            psoA = eca(nc.psum_tensor("psoA", [128, 64], F32))
            psoB = eca(nc.psum_tensor("psoB", [128, 64], F32))
            pc0 = eca(nc.psum_tensor("pc0", [64, 64], F32))
            pc1 = eca(nc.psum_tensor("pc1", [64, 64], F32))
            ppo0 = eca(nc.psum_tensor("ppo0", [128, TG], F32))
            ppo1 = eca(nc.psum_tensor("ppo1", [128, TG], F32))

            x1_loaded = {}

            def load_x1(kk):
                dvn = s_dv.n
                for c in range(8):
                    P.emit("sync", lambda e, c=c, kk=kk: e.dma_start(
                        out=x1t[:, c * 512:c * 512 + 512],
                        in_=x1i[128 * c:128 * c + 128, kk * 512:kk * 512 + 512]),
                        waits=[(s_dv, dvn if kk > 0 else sin_done)], inc=(s_ld, 16))
                x1_loaded[kk] = s_ld.n

            res_ld = s_ld.n
            kw_loaded = {}

            def load_kw(n_ch):
                P.emit("sync", lambda e, n_ch=n_ch: e.dma_start(
                    out=kwt[:, (n_ch % 2) * D:(n_ch % 2) * D + D],
                    in_=Kwi[n_ch * 128:n_ch * 128 + 128, :]),
                    waits=[(s_pe, st.get("kw_pe_%d" % ((n_ch - 2) % 2), 0))],
                    inc=(s_ld, 16))
                kw_loaded[n_ch] = s_ld.n

            g_loaded = {}

            def load_g(n_ch):
                P.emit("sync", lambda e, n_ch=n_ch: e.dma_start(
                    out=gt2[:, (n_ch % 2) * D:(n_ch % 2) * D + D],
                    in_=Gi[n_ch * 128:n_ch * 128 + 128, :]),
                    waits=[(s_dv, st.get("g_dv_%d" % ((n_ch - 2) % 2), 0))],
                    inc=(s_ld, 16))
                g_loaded[n_ch] = s_ld.n

            load_kw(0)
            load_kw(1)
            load_g(0)
            load_g(1)
            load_x1(0)

            pc_free = [[], []]
            pso_free = [[], []]
            psAB_free = [[], []]
            attw_done = {}
            yg_done = []

            def emit_scores(n_ch):
                ab0 = (n_ch % 2) * 2048
                rb0 = (n_ch % 2) * D
                for c in range(8):
                    kcol = c * T_OWN + n_ch * 128
                    w0 = [(s_ld, res_ld), (s_ac, wmt_done_ac), (s_dv, sin_done)]
                    mmA = P.emit("tensor", lambda e, kcol=kcol: e.matmul(
                        psA[:], Kt[0:64, kcol:kcol + 128], Rt[0:64, kcol:kcol + 128],
                        start=True, stop=True, tile_position=(0, 0)),
                        waits=w0 + psAB_free[0], inc=(s_pe, 1))
                    mmB = P.emit("tensor", lambda e, kcol=kcol: e.matmul(
                        psB[:], Kt[64:128, kcol:kcol + 128], Rt[64:128, kcol:kcol + 128],
                        start=True, stop=True, tile_position=(64, 0)),
                        waits=psAB_free[1], inc=(s_pe, 1))
                    a0 = P.emit("vector", lambda e, c=c, ab0=ab0: e.tensor_mul(
                        out=attwb[:, ab0 + c * 256:ab0 + c * 256 + 128], in0=psA[:],
                        in1=wmt[:, (2 * c) * 128:(2 * c) * 128 + 128]),
                        waits=[(s_pe, mmA), (s_dv, s_dv.n)], inc=(s_dv, 1))
                    a1 = P.emit("vector", lambda e, c=c, ab0=ab0: e.tensor_mul(
                        out=attwb[:, ab0 + c * 256 + 128:ab0 + c * 256 + 256], in0=psB[:],
                        in1=wmt[:, (2 * c + 1) * 128:(2 * c + 1) * 128 + 128]),
                        waits=[(s_pe, mmB)], inc=(s_dv, 1))
                    psAB_free[0] = [(s_dv, a0)]
                    psAB_free[1] = [(s_dv, a1)]
                    P.emit("vector", lambda e, kcol=kcol, c=c, rb0=rb0: e.tensor_mul(
                        out=rwbb[:, rb0 + c * 128:rb0 + c * 128 + 128],
                        in0=Rt[:, kcol:kcol + 128], in1=wbt[:, c * 128:c * 128 + 128]),
                        waits=([(s_dv, s_dv.n)] if SAFE_CHAIN else ()), inc=(s_dv, 1))
                attw_done[n_ch] = s_dv.n

            emit_scores(0)
            for n_ch in range(L2_CHUNKS):
                kk = n_ch // 4
                ygfm_col = (n_ch % 4) * 128
                if n_ch + 1 < L2_CHUNKS and L2_STAGE >= 2:
                    emit_scores(n_ch + 1)
                if L2_STAGE < 2:
                    continue
                ab0 = (n_ch % 2) * 2048
                rb0 = (n_ch % 2) * D
                adn = attw_done[n_ch]
                for c in range(8):
                    vcol = n_ch * D + (2 * c) * 64
                    aw_a = attwb[:, ab0 + c * 256:ab0 + c * 256 + 128]
                    aw_b = attwb[:, ab0 + c * 256 + 128:ab0 + c * 256 + 256]
                    rw = rwbb[:, rb0 + c * 128:rb0 + c * 128 + 128]
                    P.emit("tensor", lambda e, vcol=vcol, aw_a=aw_a: e.matmul(
                        psoA[:], aw_a, Vt[:, vcol:vcol + 64],
                        start=True, stop=False),
                        waits=[(s_dv, adn)] + pso_free[0], inc=None)
                    P.emit("tensor", lambda e, c=c, rw=rw: e.matmul(
                        psoA[:], rw[0:64, :], Sbf[0:64, c * 64:c * 64 + 64],
                        start=False, stop=True, tile_position=(0, 0)), inc=None)
                    P.emit("tensor", lambda e, vcol=vcol, aw_b=aw_b: e.matmul(
                        psoB[:], aw_b, Vt[:, vcol + 64:vcol + 128],
                        start=True, stop=False),
                        waits=pso_free[1], inc=None)
                    P.emit("tensor", lambda e, c=c, rw=rw: e.matmul(
                        psoB[:], rw[64:128, :], Sbf[64:128, c * 64:c * 64 + 64],
                        start=False, stop=True, tile_position=(64, 0)), inc=(s_pe, 1))
                    opn = s_pe.n
                    if L2_STAGE >= 3:
                        kwcol = (n_ch % 2) * D + c * 128
                        pn0 = P.emit("tensor", lambda e, kwcol=kwcol, vcol=vcol: e.matmul(
                            pc0[:], kwt[:, kwcol:kwcol + 64], Vt[:, vcol:vcol + 64],
                            start=True, stop=True),
                            waits=[(s_ld, kw_loaded[n_ch])] + pc_free[0],
                            inc=(s_pe, 1))
                        pn1 = P.emit("tensor", lambda e, kwcol=kwcol, vcol=vcol: e.matmul(
                            pc1[:], kwt[:, kwcol + 64:kwcol + 128], Vt[:, vcol + 64:vcol + 128],
                            start=True, stop=True),
                            waits=pc_free[1], inc=(s_pe, 1))
                        for hh, bank, pn in ((0, pc0, pn0), (1, pc1, pn1)):
                            ssl = Sst[64 * hh:64 * hh + 64, c * 64:c * 64 + 64]
                            wsl = wst[64 * hh:64 * hh + 64, c:c + 1]
                            dn = P.emit("vector", lambda e, ssl=ssl, wsl=wsl, bank=bank:
                                        e.scalar_tensor_tensor(out=ssl, in0=ssl, scalar=wsl,
                                                               in1=bank[:], op0=ALU.mult,
                                                               op1=ALU.add),
                                        waits=[(s_pe, pn), (s_dv, s_dv.n)], inc=(s_dv, 1))
                            pc_free[hh] = [(s_dv, dn)]
                        P.emit("vector", lambda e, c=c: e.tensor_copy(
                            out=Sbf[:, c * 64:c * 64 + 64], in_=Sst[:, c * 64:c * 64 + 64]),
                            waits=([(s_dv, s_dv.n)] if SAFE_CHAIN else ()), inc=(s_dv, 1))
                    if L2_STAGE < 4:
                        pso_free[0] = []
                        pso_free[1] = []
                        continue
                    xo0 = (n_ch % 2) * D
                    e0 = P.emit("scalar", lambda e, c=c, xo0=xo0: e.activation(
                        out=xof[:, xo0 + c * 128:xo0 + c * 128 + 64], in_=psoA[:],
                        func=AF.Copy, scale=0.125),
                        waits=[(s_pe, opn),
                               (s_dv, st.get("gn_dv_%d" % (n_ch % 2), 0))],
                        inc=(s_ac, 1))
                    ev = P.emit("scalar", lambda e, c=c, xo0=xo0: e.activation(
                        out=xof[:, xo0 + c * 128 + 64:xo0 + c * 128 + 128], in_=psoB[:],
                        func=AF.Copy, scale=0.125),
                        inc=(s_ac, 1))
                    pso_free[0] = [(s_ac, e0)]
                    pso_free[1] = [(s_ac, ev)]
                if L2_STAGE < 4:
                    st["kw_pe_%d" % (n_ch % 2)] = s_pe.n
                    st["g_dv_%d" % (n_ch % 2)] = s_dv.n
                    continue
                st["kw_pe_%d" % (n_ch % 2)] = s_pe.n
                evn = s_ac.n
                xo0 = (n_ch % 2) * D
                x3d = xof[:, xo0:xo0 + D].rearrange("p (h k) -> p h k", h=16)
                s3d = gnscr[:, 0:D].rearrange("p (h k) -> p h k", h=16)
                ygt = [ygtm, ygtm2][n_ch % 2]
                sr0 = (n_ch % 2) * 16
                P.emit("vector", lambda e, x3d=x3d, sr0=sr0: e.reduce_sum(
                    out=srow[:, sr0:sr0 + 16], in_=x3d, axis=mybir.AxisListType.X),
                    waits=[(s_ac, evn), (s_dv, s_dv.n)], inc=(s_dv, 1))
                dvn = s_dv.n
                P.emit("scalar", lambda e, sr0=sr0: e.activation(
                    out=srow[:, 32 + sr0:48 + sr0], in_=srow[:, sr0:sr0 + 16],
                    func=AF.Copy, scale=1.0 / 64),
                    waits=[(s_dv, dvn)], inc=(s_ac, 1))
                acn = s_ac.n
                P.emit("vector", lambda e, x3d=x3d, sr0=sr0: e.tensor_tensor(
                    out=x3d, in0=x3d,
                    in1=srow[:, 32 + sr0:48 + sr0].unsqueeze(2).broadcast_to([128, 16, 64]),
                    op=ALU.subtract),
                    waits=[(s_ac, acn)], inc=(s_dv, 1))
                dvn = s_dv.n
                P.emit("vector", lambda e, xo0=xo0: e.tensor_mul(
                    out=gnscr[:, 0:D], in0=xof[:, xo0:xo0 + D],
                    in1=xof[:, xo0:xo0 + D]),
                    waits=([(s_dv, dvn)] if SAFE_CHAIN else ()), inc=(s_dv, 1))
                dvn = s_dv.n
                P.emit("vector", lambda e, s3d=s3d, sr0=sr0: e.reduce_sum(
                    out=srow[:, 64 + sr0:80 + sr0], in_=s3d, axis=mybir.AxisListType.X),
                    waits=([(s_dv, dvn)] if SAFE_CHAIN else ()), inc=(s_dv, 1))
                dvn = s_dv.n
                P.emit("scalar", lambda e, sr0=sr0: e.activation(
                    out=srow[:, 64 + sr0:80 + sr0], in_=srow[:, 64 + sr0:80 + sr0],
                    func=AF.Ln, scale=1.0 / 64, bias=epst[:]),
                    waits=[(s_dv, dvn)], inc=(s_ac, 1))
                acn = s_ac.n
                P.emit("scalar", lambda e, sr0=sr0: e.activation(
                    out=srow[:, 64 + sr0:80 + sr0], in_=srow[:, 64 + sr0:80 + sr0],
                    func=AF.Exp, scale=-0.5),
                    waits=([(s_ac, acn)] if SAFE_CHAIN else ()), inc=(s_ac, 1))
                acn = s_ac.n
                P.emit("vector", lambda e, x3d=x3d, sr0=sr0: e.tensor_tensor(
                    out=x3d, in0=x3d,
                    in1=srow[:, 64 + sr0:80 + sr0].unsqueeze(2).broadcast_to([128, 16, 64]),
                    op=ALU.mult),
                    waits=[(s_ac, acn), (s_dv, s_dv.n)], inc=(s_dv, 1))
                dvn = s_dv.n
                P.emit("vector", lambda e, xo0=xo0: e.tensor_mul(
                    out=xof[:, xo0:xo0 + D], in0=xof[:, xo0:xo0 + D], in1=lnwb[:]),
                       waits=([(s_dv, dvn)] if SAFE_CHAIN else ()), inc=(s_dv, 1))
                dvn = s_dv.n
                P.emit("vector", lambda e, xo0=xo0: e.tensor_add(
                    out=xof[:, xo0:xo0 + D], in0=xof[:, xo0:xo0 + D], in1=lnbb[:]),
                       waits=([(s_dv, dvn)] if SAFE_CHAIN else ()), inc=(s_dv, 1))
                dvn = s_dv.n
                g1 = P.emit("vector", lambda e, n_ch=n_ch, xo0=xo0, ygt=ygt: e.tensor_mul(
                    out=ygt[:], in0=xof[:, xo0:xo0 + D],
                    in1=gt2[:, (n_ch % 2) * D:(n_ch % 2) * D + D]),
                    waits=[(s_dv, dvn), (s_tr, st.get("ygtm_tr_%d" % (n_ch % 2), 0)),
                           (s_ld, g_loaded[n_ch])], inc=(s_dv, 1))
                st["gn_dv_%d" % (n_ch % 2)] = s_dv.n
                st["g_dv_%d" % (n_ch % 2)] = s_dv.n
                if L2_STAGE < 5:
                    continue
                if n_ch + 2 < NCHUNK:
                    load_kw(n_ch + 2)
                    load_g(n_ch + 2)
                ygn = g1
                for c in range(8):
                    P.emit("sync", lambda e, c=c, ygfm_col=ygfm_col, ygt=ygt: e.dma_start(
                        out=ygfm[:, c * 512 + ygfm_col:c * 512 + ygfm_col + 128],
                        in_=ygt[:, c * 128:c * 128 + 128], transpose=True),
                        waits=[(s_dv, ygn), (s_pe, st.get("wo_pe", 0))], inc=(s_tr, 16))
                yg_done.append(s_tr.n)
                st["ygtm_tr_%d" % (n_ch % 2)] = s_tr.n
                if n_ch % 4 == 3:
                    trn = s_tr.n
                    for o in range(8):
                        bank = [ppo0, ppo1][o % 2]
                        for ci in range(8):
                            w0 = ([(s_tr, trn), (s_cp, s_cp.n), (s_ld, x1_loaded[kk])]
                                  + (st.get("ppo_free_%d" % (o % 2)) or [])
                                  if ci == 0 else ())
                            P.emit("tensor", lambda e, bank=bank, ci=ci, o=o: e.matmul(
                                bank[:], wo8[:, ci * D + o * 128:ci * D + o * 128 + 128],
                                ygfm[:, ci * 512:ci * 512 + 512],
                                start=(ci == 0), stop=(ci == 7)),
                                waits=w0, inc=((s_pe, 1) if ci == 7 else None))
                        pen = s_pe.n
                        dn = P.emit("vector", lambda e, bank=bank, o=o: e.tensor_add(
                            out=x2t[:], in0=x1t[:, o * 512:o * 512 + 512], in1=bank[:]),
                            waits=[(s_pe, pen), (s_st, st.get("x2t_st", 0)),
                                   (s_dv, s_dv.n)],
                            inc=(s_dv, 1))
                        st["ppo_free_%d" % (o % 2)] = [(s_dv, dn)]
                        P.emit("sync", lambda e, o=o, kk=kk: e.dma_start(
                            out=x2o[128 * o:128 * o + 128, kk * 512:kk * 512 + 512],
                            in_=x2t[:]),
                            waits=[(s_dv, dn)], inc=(s_st, 16))
                        stn = P.emit("sync", lambda e, o=o, kk=kk: e.dma_start(
                            out=x2d.ap()[128 * o:128 * o + 128, kk * 512:kk * 512 + 512],
                            in_=x2t[:]),
                            waits=[(s_dv, dn)], inc=(s_st, 16))
                        st["x2t_st"] = stn
                    st["wo_pe"] = s_pe.n
                    if kk + 1 < 4:
                        load_x1(kk + 1)
            att_dv, att_ac, att_pe, att_st, att_tr = (s_dv.n, s_ac.n, s_pe.n, s_st.n,
                                                      s_tr.n)

        with ExitStack() as ctxb:
          if L2_FFN:
            ecb = ctxb.enter_context
            C8 = ecb(nc.sbuf_tensor("C8", [128, 32 * TG], BF16))
            acc = ecb(nc.sbuf_tensor("acc", [128, 8 * TG], F32))
            hg = ecb(nc.sbuf_tensor("hg", [128, HG_SLOTS * TG], BF16))
            cpw = ecb(nc.sbuf_tensor("cpw", [128, 2 * 4096], BF16))
            xf = ecb(nc.sbuf_tensor("xf", [128, 2 * 8 * TG], F32))
            xnb = ecb(nc.sbuf_tensor("xnb", [128, 8 * TG], BF16))
            st0 = ecb(nc.psum_tensor("st0", [1, TG], F32))
            st1 = ecb(nc.psum_tensor("st1", [1, TG], F32))
            bc0 = ecb(nc.psum_tensor("bc0", [128, TG], F32))
            bc1 = ecb(nc.psum_tensor("bc1", [128, TG], F32))
            rail0 = ecb(nc.psum_tensor("rail0", [128, TG], F32))
            rail1 = ecb(nc.psum_tensor("rail1", [128, TG], F32))
            cp0 = ecb(nc.psum_tensor("cp0", [128, TG], F32))
            cp1 = ecb(nc.psum_tensor("cp1", [128, TG], F32))
            pb.update({"st0": st0, "st1": st1, "bc0": bc0, "bc1": bc1,
                       "rail0": rail0, "rail1": rail1, "cp0": cp0, "cp1": cp1})

            for eng in ("sync", "gpsimd", "tensor", "vector", "scalar"):
                P.wait(eng, s_dv, att_dv)
                P.wait(eng, s_ac, att_ac)
                P.wait(eng, s_pe, att_pe)
                P.wait(eng, s_st, att_st)
                P.wait(eng, s_tr, att_tr)

            cpn0 = P.emit("sync", lambda e: e.dma_start(out=C8[:], in_=c8i),
                          inc=(s_cp, 16))
            P.wait("tensor", s_cp, cpn0)

            bufs = {"xf": xf, "xn": xnb, "hg": hg, "cpw": cpw, "acc": acc, "C8": C8,
                    "cpstream": cpstream_d,
                    "scr": {"xb": xb_s, "sq": sq_s, "d": d_s}, "rows": rows}

            tg_list2 = [(k, k * TG, TG) for k in range(4)]

            def ap_x_load2(c, col0, n):
                if L2_CHUNKS == 0:
                    return x1i[128 * c:128 * c + 128, col0:col0 + n]
                return x2d.ap()[128 * c:128 * c + 128, col0:col0 + n]

            def post_tg2(t_idx, b, n, res_done):
                k = t_idx
                for c in range(8):
                    P.emit("sync", lambda e, c=c, k=k, b=b: e.dma_start(
                        out=x3o[128 * c:128 * c + 128, k * TG:k * TG + TG],
                        in_=xf[:, (b * 8 + c) * TG:(b * 8 + c) * TG + TG]),
                        waits=[(s_dv, res_done[c])], inc=(s_st, 16))
                st["post_st"] = s_st.n
                st["post_dv"] = max(res_done)

            emit_ffn(P, sem, pb, st, bufs, tg_list2, ap_x_load2, post_tg2, consts)

            P.wait("gpsimd", s_dv, s_dv.n)
            P.wait("gpsimd", s_ac, s_ac.n)
            P.wait("gpsimd", s_kp, s_kp.n)
            P.emit("gpsimd", lambda e: e.dma_start(out=dbg2[:, 0:TG], in_=C8[:, 0:TG]),
                   inc=(s_st, 16))
            P.emit("gpsimd", lambda e: e.dma_start(out=dbg2[:, TG:2*TG], in_=xnb[:, 0:TG]),
                   inc=(s_st, 16))
            P.emit("gpsimd", lambda e: e.dma_start(out=dbg2[:, 2*TG:3*TG], in_=hg[:, 0:TG]),
                   inc=(s_st, 16))
            P.emit("sync", lambda e: e.dma_start(out=dbg2[:, 3*TG:4*TG], in_=acc[:, 0:TG]),
                   inc=(s_st, 16))

            P.wait("gpsimd", s_st, s_st.n)
            P.wait("gpsimd", s_tr, s_tr.n)

        with nc.Block() as block:
            P.build(block)

    return nc


def run_l2(inputs, l1_results):
    shared = _prep_shared(inputs)
    Ball = np.stack([np.asarray(l1_results[r]["Bo"], np.float32) for r in range(NCORES)])
    Ball = np.ascontiguousarray(Ball.transpose(2, 0, 1, 3)).reshape(128, 4096)

    in_maps = []
    for i in range(NCORES):
        r = l1_results[i]
        expo = np.zeros((1, 8), np.float32)
        mask8 = np.zeros((1, 8), np.float32)
        for j in range(i):
            expo[0, j] = 2048.0 * (i - 1 - j)
            mask8[0, j] = 1.0
        m = {
            "x1i": np.ascontiguousarray(r["x1o"]),
            "Ri": np.ascontiguousarray(r["Ro"]),
            "Ki": np.ascontiguousarray(r["Ko"]),
            "Gi": np.ascontiguousarray(r["Go"]),
            "Vi": np.ascontiguousarray(r["Vo"]),
            "Kwi": np.ascontiguousarray(r["Kwo"]),
            "Ball": Ball,
            "cpstream": shared["cpstream"],
            "c8i": np.ascontiguousarray(r["c8o"]),
            "wot": np.ascontiguousarray(np.asarray(inputs["L0_attn_W_o"], np.float32).T),
            "td": shared["td"],
            "tf": np.asarray(inputs["L0_attn_time_faaaa"], np.float32).reshape(1, 16),
            "lnw": np.asarray(inputs["L0_attn_ln_w"], np.float32).reshape(1, D),
            "lnb": np.asarray(inputs["L0_attn_ln_b"], np.float32).reshape(1, D),
            "expo": expo, "mask8": mask8,
        }
        in_maps.append(m)

    nc = build_l2()
    res = run_bass_kernel_spmd(nc, in_maps, core_ids=list(range(NCORES)))
    return res.results


def _prep_shared(inputs):
    W = np.asarray(inputs["L0_ffn_w_cproj"], dtype=np.float32)
    A = W.T.reshape(4, 32, 128, 8, 128).transpose(2, 0, 3, 1, 4)
    cpstream = np.ascontiguousarray(A).reshape(128, 131072)

    rm = np.asarray(inputs["L0_ffn_rail_maker"], dtype=np.float32)
    u = np.asarray(inputs["L0_ffn_rail_up_proj"], dtype=np.float32)
    rmbd = np.zeros((32, 128, 128), dtype=np.float32)
    ubd = np.zeros((32, 128, 512), dtype=np.float32)
    for h in range(4):
        for c in range(8):
            m = h * 8 + c
            for p in range(8):
                r = 8 * c + p
                rmbd[m, 16 * p:16 * p + 16, 16 * p:16 * p + 16] = rm[h, r].T
                ubd[m, 16 * p:16 * p + 16, 64 * p:64 * p + 64] = u[h, r].T

    tm = np.stack([np.asarray(inputs[f"L0_attn_time_mix_{t}"], np.float32).reshape(-1)
                   for t in ("r", "k", "v", "g")])
    tmix = np.ascontiguousarray(tm.reshape(4, 8, 128).transpose(2, 0, 1)).reshape(128, 32)

    return {
        "cpstream": cpstream, "rmbd": rmbd, "ubd": ubd,
        "wrt": np.ascontiguousarray(np.asarray(inputs["L0_attn_W_r"], np.float32).T),
        "wkt": np.ascontiguousarray(np.asarray(inputs["L0_attn_W_k"], np.float32).T),
        "wvt": np.ascontiguousarray(np.asarray(inputs["L0_attn_W_v"], np.float32).T),
        "wgt": np.ascontiguousarray(np.asarray(inputs["L0_attn_W_g"], np.float32).T),
        "tmix": tmix,
        "td": np.asarray(inputs["L0_attn_time_decay"], np.float32).reshape(1, 16),
    }


def run_l1(inputs):
    shared = _prep_shared(inputs)
    x = np.asarray(inputs["L0_attn_xq"], np.float32)[0]
    xT = np.ascontiguousarray(x.T)
    xTp = np.concatenate([np.zeros((D, OVL), np.float32), xT], axis=1)

    in_maps = []
    for i in range(NCORES):
        m = dict(shared)
        m["x0T"] = np.ascontiguousarray(xTp[:, i * T_OWN: i * T_OWN + T_IN])
        in_maps.append(m)

    nc = build_l1()
    res = run_bass_kernel_spmd(nc, in_maps, core_ids=list(range(NCORES)))
    return res.results


def kernel(**inputs):
    l1 = run_l1(inputs)
    l2 = run_l2(inputs, l1)
    x3 = np.concatenate([r["x3o"].T for r in l2], axis=0)
    return x3.reshape(1, 16384, 1024).astype(np.float32)


# version 14
# speedup vs baseline: 1.0443x; 1.0070x over previous
from contextlib import ExitStack

import numpy as np
import ml_dtypes
import concourse.bass as bass
import concourse.mybir as mybir
from concourse.bass_utils import run_bass_kernel_spmd

F32 = mybir.dt.float32
BF16 = mybir.dt.bfloat16
AF = mybir.ActivationFunctionType
ALU = mybir.AluOpType
BF = ml_dtypes.bfloat16

NCORES = 8
T_OWN = 2048
OVL = 128
T_IN = T_OWN + OVL
TG = 512
D = 1024
NHEAD = 16
HK = 64
CH = 128
NCHUNK = T_OWN // CH
EPS = 1e-5
HG_SLOTS = 63

SAFE_CHAIN = True
L2_CHUNKS = NCHUNK
L2_FFN = True
L2_STAGE = 9



class Sem:
    __slots__ = ("h", "n")

    def __init__(self, h):
        self.h = h
        self.n = 0


class Plan:
    ENGINES = ("sync", "tensor", "vector", "scalar", "gpsimd")

    def __init__(self, nc):
        self.nc = nc
        self.steps = {e: [] for e in self.ENGINES}

    def emit(self, eng, fn, waits=(), inc=None):
        ws = tuple((s, v) for (s, v) in waits if v > 0)
        incs = ()
        if inc is not None:
            s, k = inc
            s.n += k
            incs = ((s, k),)
        self.steps[eng].append((fn, ws, incs))
        return incs[0][0].n if incs else None

    def wait(self, eng, sem, val=None):
        v = sem.n if val is None else val
        if v > 0:
            self.steps[eng].append((None, ((sem, v),), ()))

    def build(self, block):
        for eng in self.ENGINES:
            steps = self.steps[eng]
            if not steps:
                continue

            def body(e, steps=steps):
                hw = {}
                for fn, ws, incs in steps:
                    for (s, v) in ws:
                        if v > hw.get(s.h.num, 0):
                            e.wait_ge(s.h, v)
                            hw[s.h.num] = v
                    if fn is None:
                        continue
                    inst = fn(e)
                    for (s, k) in incs:
                        inst = inst.then_inc(s.h, k)

            getattr(block, eng)(body)


def emit_ln(P, sem, pb, st, xf_ap, out_ap, n, scr, rows, consts,
            pre_dv=0, pre_ac=0):
    s_dv, s_ac, s_pe = sem["dv"], sem["ac"], sem["pe"]
    onesb, onesf, epst = consts["ones_bf"], consts["ones_f32"], consts["epst"]
    xb_t, sq_t, d_t = scr["xb"], scr["sq"], scr["d"]
    st0, st1, bc0, bc1 = pb["st0"], pb["st1"], pb["bc0"], pb["bc1"]
    mrow, vrow = rows["m"], rows["v"]

    if pre_dv:
        P.wait("vector", s_dv, pre_dv)
        P.wait("scalar", s_dv, pre_dv)
    if pre_ac:
        P.wait("vector", s_ac, pre_ac)
        P.wait("scalar", s_ac, pre_ac)

    mm_done = []
    for c in range(8):
        xa = xf_ap(c)
        xb = xb_t[:, 0:n]
        sq = sq_t[:, 0:n]
        w_pe = [(s_pe, mm_done[c - 1])] if c >= 1 else []
        P.emit("vector", lambda e, xb=xb, xa=xa: e.tensor_copy(out=xb, in_=xa),
               waits=w_pe, inc=(s_dv, 1))
        dvn = s_dv.n
        P.emit("scalar", lambda e, sq=sq, xa=xa: e.activation(out=sq, in_=xa, func=AF.Square),
               waits=w_pe, inc=(s_ac, 1))
        acn = s_ac.n
        w0 = [(s_dv, dvn)]
        if c == 0:
            w0.append((s_ac, st["stats_free_ac"]))
            w0.append((s_dv, st["stats_free_dv"]))
        P.emit("tensor", lambda e, xb=xb, c=c, n=n: e.matmul(
            st0[:, 0:n], onesb[:], xb, start=(c == 0), stop=(c == 7)), waits=w0)
        mmn = P.emit("tensor", lambda e, sq=sq, c=c, n=n: e.matmul(
            st1[:, 0:n], onesb[:], sq, start=(c == 0), stop=(c == 7)),
            waits=[(s_ac, acn)], inc=(s_pe, 1))
        mm_done.append(mmn)

    pe_n = s_pe.n
    P.emit("scalar", lambda e, n=n: e.activation(out=mrow[:, 0:n], in_=st0[:, 0:n],
                                                 func=AF.Copy, scale=1.0 / D),
           waits=[(s_pe, pe_n)], inc=(s_ac, 1))
    P.emit("scalar", lambda e, n=n: e.activation(out=vrow[:, 0:n], in_=st1[:, 0:n],
                                                 func=AF.Copy, scale=1.0 / D),
           inc=(s_ac, 1))
    acn = s_ac.n
    P.emit("scalar", lambda e, n=n: e.activation(out=st0[:, 0:n], in_=mrow[:, 0:n],
                                                 func=AF.Square),
           waits=([(s_ac, acn)] if SAFE_CHAIN else ()), inc=(s_ac, 1))
    st["stats_free_ac"] = s_ac.n
    acn = s_ac.n
    P.emit("vector", lambda e, n=n: e.tensor_sub(out=vrow[:, 0:n], in0=vrow[:, 0:n],
                                                 in1=st0[:, 0:n]),
           waits=[(s_ac, acn)], inc=(s_dv, 1))
    st["stats_free_dv"] = s_dv.n
    dvn = s_dv.n
    P.emit("scalar", lambda e, n=n: e.activation(out=vrow[:, 0:n], in_=vrow[:, 0:n],
                                                 func=AF.Ln, bias=epst[0:1, :]),
           waits=[(s_dv, dvn)], inc=(s_ac, 1))
    acn = s_ac.n
    P.emit("scalar", lambda e, n=n: e.activation(out=vrow[:, 0:n], in_=vrow[:, 0:n],
                                                 func=AF.Exp, scale=-0.5),
           waits=([(s_ac, acn)] if SAFE_CHAIN else ()), inc=(s_ac, 1))
    acn = s_ac.n
    P.emit("tensor", lambda e, n=n: e.matmul(bc0[:, 0:n], onesf[:], mrow[:, 0:n],
                                             start=True, stop=True),
           waits=[(s_ac, acn), (s_dv, st["bc_free_dv"])])
    P.emit("tensor", lambda e, n=n: e.matmul(bc1[:, 0:n], onesf[:], vrow[:, 0:n],
                                             start=True, stop=True), inc=(s_pe, 1))
    pe_n = s_pe.n
    mul_done = []
    for c in range(8):
        xa = xf_ap(c)
        dd = d_t[:, 0:n]
        w = [(s_pe, pe_n)]
        if c >= 1:
            w.append((s_dv, mul_done[c - 1]))
        P.emit("vector", lambda e, xa=xa, dd=dd, n=n: e.tensor_sub(out=dd, in0=xa,
                                                                   in1=bc0[:, 0:n]),
               waits=w, inc=(s_dv, 1))
        dvn = s_dv.n
        mn = P.emit("vector", lambda e, dd=dd, c=c, n=n: e.tensor_mul(
            out=out_ap(c), in0=dd, in1=bc1[:, 0:n]),
            waits=([(s_dv, dvn)] if SAFE_CHAIN else ()), inc=(s_dv, 1))
        mul_done.append(mn)
    st["bc_free_dv"] = s_dv.n
    return s_dv.n


def emit_ln_single(P, sem, pb, st, xf_ap, out_ap, scr, rows, consts, pre_dv=0):
    s_dv, s_ac, s_pe = sem["dv"], sem["ac"], sem["pe"]
    onesb, onesf, epst = consts["ones_bf"], consts["ones_f32"], consts["epst"]
    st0, st1, bc0, bc1 = pb["st0"], pb["st1"], pb["bc0"], pb["bc1"]
    xb_t, sq_t = scr["xb"], scr["sq"]
    mrow, vrow = rows["m"], rows["v"]
    if pre_dv:
        P.wait("vector", s_dv, pre_dv)
        P.wait("scalar", s_dv, pre_dv)
    for c in range(8):
        xa = xf_ap(c)
        P.emit("vector", lambda e, c=c, xa=xa: e.tensor_copy(out=xb_t[:, c:c + 1], in_=xa),
               inc=(s_dv, 1))
        P.emit("scalar", lambda e, c=c, xa=xa: e.activation(out=sq_t[:, c:c + 1], in_=xa,
                                                            func=AF.Square), inc=(s_ac, 1))
    dvn, acn = s_dv.n, s_ac.n
    for c in range(8):
        w0 = [(s_dv, dvn)]
        if c == 0:
            w0.append((s_ac, st["stats_free_ac"]))
            w0.append((s_dv, st["stats_free_dv"]))
        P.emit("tensor", lambda e, c=c: e.matmul(st0[:, 0:1], onesb[:], xb_t[:, c:c + 1],
                                                 start=(c == 0), stop=(c == 7)), waits=w0)
        P.emit("tensor", lambda e, c=c: e.matmul(st1[:, 0:1], onesb[:], sq_t[:, c:c + 1],
                                                 start=(c == 0), stop=(c == 7)),
               waits=[(s_ac, acn)], inc=((s_pe, 1) if c == 7 else None))
    pe_n = s_pe.n
    P.emit("scalar", lambda e: e.activation(out=mrow[:, 0:1], in_=st0[:, 0:1], func=AF.Copy,
                                            scale=1.0 / D), waits=[(s_pe, pe_n)], inc=(s_ac, 1))
    P.emit("scalar", lambda e: e.activation(out=vrow[:, 0:1], in_=st1[:, 0:1], func=AF.Copy,
                                            scale=1.0 / D), inc=(s_ac, 1))
    st["stats_free_ac"] = s_ac.n
    acn = s_ac.n
    P.emit("scalar", lambda e: e.activation(out=st0[:, 0:1], in_=mrow[:, 0:1], func=AF.Square),
           waits=([(s_ac, acn)] if SAFE_CHAIN else ()), inc=(s_ac, 1))
    acn = s_ac.n
    P.emit("vector", lambda e: e.tensor_sub(out=vrow[:, 0:1], in0=vrow[:, 0:1],
                                            in1=st0[:, 0:1]),
           waits=[(s_ac, acn)], inc=(s_dv, 1))
    dvn = s_dv.n
    P.emit("scalar", lambda e: e.activation(out=vrow[:, 0:1], in_=vrow[:, 0:1], func=AF.Ln,
                                            bias=epst[0:1, :]),
           waits=[(s_dv, dvn)], inc=(s_ac, 1))
    acn = s_ac.n
    P.emit("scalar", lambda e: e.activation(out=vrow[:, 0:1], in_=vrow[:, 0:1], func=AF.Exp,
                                            scale=-0.5),
           waits=([(s_ac, acn)] if SAFE_CHAIN else ()), inc=(s_ac, 1))
    acn = s_ac.n
    P.emit("tensor", lambda e: e.matmul(bc0[:, 0:1], onesf[:], mrow[:, 0:1],
                                        start=True, stop=True),
           waits=[(s_ac, acn), (s_dv, st["bc_free_dv"])])
    P.emit("tensor", lambda e: e.matmul(bc1[:, 0:1], onesf[:], vrow[:, 0:1],
                                        start=True, stop=True), inc=(s_pe, 1))
    pe_n = s_pe.n
    dd = scr["d"][:, 0:1]
    for c in range(8):
        xa = xf_ap(c)
        P.emit("vector", lambda e, xa=xa, dd=dd: e.tensor_sub(out=dd, in0=xa, in1=bc0[:, 0:1]),
               waits=[(s_pe, pe_n), (s_dv, s_dv.n)], inc=(s_dv, 1))
        dvn = s_dv.n
        P.emit("vector", lambda e, c=c, dd=dd: e.tensor_mul(out=out_ap(c), in0=dd,
                                                            in1=bc1[:, 0:1]),
               waits=[(s_dv, dvn)], inc=(s_dv, 1))
    st["bc_free_dv"] = s_dv.n
    return s_dv.n


def emit_ffn(P, sem, pb, st, bufs, tg_list, ap_x_load, post_tg, consts):
    s_ld, s_cp, s_pe = sem["ld"], sem["cp"], sem["pe"]
    s_dv, s_ac, s_kp = sem["dv"], sem["ac"], sem["kp"]
    s_xf = sem["xfree"]
    xf, xn, hg, cpw, acc, C8 = (bufs["xf"], bufs["xn"], bufs["hg"], bufs["cpw"],
                                bufs["acc"], bufs["C8"])
    cpst = bufs["cpstream"]
    pbr = [pb["rail0"], pb["rail1"]]
    pbc = [pb["cp0"], pb["cp1"]]

    kp_base = s_kp.n
    ntg = len(tg_list)

    x_loaded = {}

    def emit_x_load(t_idx):
        if t_idx >= ntg:
            return
        tg_i, col0, n = tg_list[t_idx]
        b = t_idx % 2
        for c in range(8):
            dst = xf[:, (b * 8 + c) * TG:(b * 8 + c) * TG + n]
            w = ([(s_xf, st["xf_base"])] if (t_idx < 2 and st["xf_base"]) else [])
            P.emit("sync", lambda e, dst=dst, c=c, col0=col0, n=n: e.dma_start(
                out=dst, in_=ap_x_load(c, col0, n)), waits=w, inc=(s_ld, 16))
        x_loaded[t_idx] = s_ld.n

    emit_x_load(0)
    emit_x_load(1)

    slab_loaded = []
    for gstep in range(32 * ntg):
        g, o = (gstep % 32) // 8, gstep % 8
        dst = cpw[:, (gstep % 2) * 4096:(gstep % 2) * 4096 + 4096]
        src = cpst[:, (g * 8 + o) * 4096:(g * 8 + o) * 4096 + 4096]
        w = [(s_kp, kp_base + gstep - 1)] if gstep >= 2 else []
        P.emit("gpsimd", lambda e, dst=dst, src=src: e.dma_start(out=dst, in_=src),
               waits=w, inc=(s_cp, 16))
        slab_loaded.append(s_cp.n)

    for t_idx, (tg_i, col0, n) in enumerate(tg_list):
        b = t_idx % 2

        def xap(c, b=b, n=n):
            return xf[:, (b * 8 + c) * TG:(b * 8 + c) * TG + n]

        ldw = x_loaded[t_idx]
        P.wait("vector", s_ld, ldw)
        P.wait("scalar", s_ld, ldw)
        ln_done = emit_ln(P, sem, pb, st, xap,
                          lambda c, n=n: xn[:, c * TG:c * TG + n],
                          n, bufs["scr"], bufs["rows"], consts)
        if t_idx == ntg - 1 and "dbg_ln" in bufs:
            dbg = bufs["dbg_ln"]
            rows_ = bufs["rows"]
            P.emit("sync", lambda e: e.dma_start(out=dbg[0:1, 0:TG], in_=rows_["m"][:, 0:TG]),
                   waits=[(s_dv, ln_done)], inc=(sem["st"], 16))
            P.emit("sync", lambda e: e.dma_start(out=dbg[1:2, 0:TG], in_=rows_["v"][:, 0:TG]),
                   inc=(sem["st"], 16))
            P.emit("sync", lambda e, b=b: e.dma_start(out=dbg[2:130, 0:TG][0:128, :],
                                                      in_=xf[:, b * 8 * TG:b * 8 * TG + TG]),
                   inc=(sem["st"], 16))
            P.emit("gpsimd", lambda e: e.dma_start(out=dbg[130:258, 0:TG][0:128, :],
                                                   in_=xn[:, 0:TG]),
                   waits=[(s_dv, ln_done)], inc=(sem["st"], 16))
            P.emit("sync", lambda e: e.dma_start(out=dbg[0:1, 0:128],
                                                 in_=consts["ones_f32"][:]),
                   inc=(sem["st"], 16))

        rail_j = [0]
        tg_slot0 = st["slot_ctr"]

        def emit_rails(kcnt, t_idx=t_idx, tg_slot0=tg_slot0, n=n, ln_done=ln_done):
            for _ in range(kcnt):
                j = rail_j[0]
                if j >= 128:
                    return
                rail_j[0] += 1
                h, rp = j // 32, j % 32
                c, s = rp // 4, rp % 4
                slot = (tg_slot0 + j) % HG_SLOTS
                bank = pbr[j % 2]
                lhs = C8[:, (h * 8 + c) * TG:(h * 8 + c) * TG + TG]
                lhsT = lhs[32 * s:32 * s + 32, 128 * s:128 * s + 128]
                rhs = xn[32 * s:32 * s + 32, c * TG:c * TG + n]
                w = [(s_dv, ln_done), (s_ac, st["rail_gelu"][j % 2])]
                mmn = P.emit("tensor", lambda e, bank=bank, lhsT=lhsT, rhs=rhs, s=s, n=n:
                             e.matmul(bank[:, 0:n], lhsT, rhs, start=True, stop=True,
                                      tile_position=(32 * s, 0)), waits=w, inc=(s_pe, 1))
                glob_slot_idx = tg_slot0 + j
                wg = [(s_pe, mmn)]
                prev_reader = st["slot_reader_step"][slot]
                if prev_reader > 0:
                    wg.append((s_kp, prev_reader))
                dst = hg[:, slot * TG:slot * TG + n]
                gn = P.emit("scalar", lambda e, dst=dst, bank=bank, n=n: e.activation(
                    out=dst, in_=bank[:, 0:n], func=AF.Gelu), waits=wg, inc=(s_ac, 1))
                st["gelu_done"][slot] = gn
                st["rail_gelu"][j % 2] = gn
            return

        emit_rails(32)
        st["slot_ctr"] += 128

        for g in range(4):
            for o in range(8):
                gstep = t_idx * 32 + g * 8 + o
                bank = pbc[gstep % 2]
                slab = cpw[:, (gstep % 2) * 4096:(gstep % 2) * 4096 + 4096]
                w = [(s_cp, slab_loaded[gstep]), (s_dv, st["cp_bank_dv"][gstep % 2])]
                gmax = 0
                for jj in range(32):
                    slot = (tg_slot0 + g * 32 + jj) % HG_SLOTS
                    gmax = max(gmax, st["gelu_done"][slot])
                w.append((s_ac, gmax))
                for jj in range(32):
                    slot = (tg_slot0 + g * 32 + jj) % HG_SLOTS
                    lhsT = slab[:, jj * 128:jj * 128 + 128]
                    rhs = hg[:, slot * TG:slot * TG + n]
                    P.emit("tensor", lambda e, bank=bank, lhsT=lhsT, rhs=rhs, jj=jj, n=n:
                           e.matmul(bank[:, 0:n], lhsT, rhs, start=(jj == 0),
                                    stop=(jj == 31)),
                           waits=(w if jj == 0 else ()),
                           inc=((s_kp, 1) if jj == 31 else None))
                for jj in range(32):
                    slot = (tg_slot0 + g * 32 + jj) % HG_SLOTS
                    st["slot_reader_step"][slot] = s_kp.n
                emit_rails(4)
                accsl = acc[:, o * TG:o * TG + n]
                if g == 0:
                    fn = lambda e, accsl=accsl, bank=bank, n=n: e.tensor_copy(
                        out=accsl, in_=bank[:, 0:n])
                else:
                    fn = lambda e, accsl=accsl, bank=bank, n=n: e.tensor_add(
                        out=accsl, in0=accsl, in1=bank[:, 0:n])
                wdv = [(s_kp, s_kp.n)]
                if SAFE_CHAIN:
                    wdv.append((s_dv, s_dv.n))
                dn = P.emit("vector", fn, waits=wdv, inc=(s_dv, 1))
                st["cp_bank_dv"][gstep % 2] = dn

        res_done = []
        for c in range(8):
            xa = xap(c)
            accsl = acc[:, c * TG:c * TG + n]
            dn = P.emit("vector", lambda e, xa=xa, accsl=accsl: e.scalar_tensor_tensor(
                out=xa, in0=accsl, scalar=0.5, in1=xa, op0=ALU.mult, op1=ALU.add),
                waits=[(s_dv, s_dv.n)], inc=(s_dv, 1))
            res_done.append(dn)

        post_tg(t_idx, b, n, res_done)
        P.emit("vector", lambda e: e.engine_nop(),
               waits=[(s_dv, st["post_dv"]), (sem["st"], st["post_st"])],
               inc=(s_xf, 1))
        xfn = s_xf.n
        if t_idx + 2 < ntg:
            P.wait("sync", s_xf, xfn)
            emit_x_load(t_idx + 2)

    st["xf_base"] += ntg


def build_l1():
    nc = bass.Bass("TRN2", target_bir_lowering=False)

    x0T = nc.dram_tensor("x0T", [D, T_IN], F32, kind="ExternalInput").ap()
    cpstream_d = nc.dram_tensor("cpstream", [128, 131072], F32, kind="ExternalInput").ap()
    rmbd = nc.dram_tensor("rmbd", [32, 128, 128], F32, kind="ExternalInput").ap()
    ubd = nc.dram_tensor("ubd", [32, 128, 512], F32, kind="ExternalInput").ap()
    wrt = nc.dram_tensor("wrt", [D, D], F32, kind="ExternalInput").ap()
    wkt = nc.dram_tensor("wkt", [D, D], F32, kind="ExternalInput").ap()
    wvt = nc.dram_tensor("wvt", [D, D], F32, kind="ExternalInput").ap()
    wgt = nc.dram_tensor("wgt", [D, D], F32, kind="ExternalInput").ap()
    tmix = nc.dram_tensor("tmix", [128, 32], F32, kind="ExternalInput").ap()
    td = nc.dram_tensor("td", [1, 16], F32, kind="ExternalInput").ap()

    x1o = nc.dram_tensor("x1o", [D, T_OWN], F32, kind="ExternalOutput").ap()
    Ro = nc.dram_tensor("Ro", [D, T_OWN], BF16, kind="ExternalOutput").ap()
    Ko = nc.dram_tensor("Ko", [D, T_OWN], BF16, kind="ExternalOutput").ap()
    Go = nc.dram_tensor("Go", [T_OWN, D], BF16, kind="ExternalOutput").ap()
    Vo = nc.dram_tensor("Vo", [T_OWN, D], BF16, kind="ExternalOutput").ap()
    Kwo = nc.dram_tensor("Kwo", [T_OWN, D], BF16, kind="ExternalOutput").ap()
    Bo = nc.dram_tensor("Bo", [8, 128, 64], F32, kind="ExternalOutput").ap()
    dbgo = nc.dram_tensor("dbgo", [128, 4 * TG], F32, kind="ExternalOutput").ap()
    c8o = nc.dram_tensor("c8o", [128, 32 * TG], BF16, kind="ExternalOutput").ap()
    dbgln = nc.dram_tensor("dbgln", [258, TG], F32, kind="ExternalOutput").ap()

    ones_bf_h = nc.inline_tensor(np.ones((128, 1), dtype=BF), "c_ones_bf")
    ones_f32_h = nc.inline_tensor(np.ones((1, 128), dtype=np.float32), "c_ones_f32")
    epst_h = nc.inline_tensor(np.full((128, 1), EPS, dtype=np.float32), "c_eps")
    riota_h = nc.inline_tensor((127.0 - np.arange(128, dtype=np.float32)).reshape(128, 1),
                               "c_riota")

    with ExitStack() as ctx:
        ec = ctx.enter_context
        ones_bf = ec(nc.sbuf_tensor("ones_bf", [128, 1], BF16))
        ones_f32 = ec(nc.sbuf_tensor("ones_f32", [1, 128], F32))
        epst = ec(nc.sbuf_tensor("epst", [128, 1], F32))
        riota = ec(nc.sbuf_tensor("riota", [128, 1], F32))
        H2 = ec(nc.sbuf_tensor("H2", [128, 8 * 2050], BF16))
        tmixs = ec(nc.sbuf_tensor("tmixs", [128, 32], F32))
        xb_s = ec(nc.sbuf_tensor("xb_s", [128, TG], BF16))
        sq_s = ec(nc.sbuf_tensor("sq_s", [128, TG], BF16))
        d_s = ec(nc.sbuf_tensor("d_s", [128, TG], F32))
        mrow = ec(nc.sbuf_tensor("mrow", [1, TG], F32))
        vrow = ec(nc.sbuf_tensor("vrow", [1, TG], F32))
        st0 = ec(nc.psum_tensor("st0", [1, TG], F32))
        st1 = ec(nc.psum_tensor("st1", [1, TG], F32))
        bc0 = ec(nc.psum_tensor("bc0", [128, TG], F32))
        bc1 = ec(nc.psum_tensor("bc1", [128, TG], F32))
        h_ld = ec(nc.semaphore("s_ld"))
        h_cp = ec(nc.semaphore("s_cp"))
        h_pe = ec(nc.semaphore("s_pe"))
        h_dv = ec(nc.semaphore("s_dv"))
        h_ac = ec(nc.semaphore("s_ac"))
        h_st = ec(nc.semaphore("s_st"))
        h_tr = ec(nc.semaphore("s_tr"))
        h_xf = ec(nc.semaphore("s_xf"))
        h_kp = ec(nc.semaphore("s_kp"))
        P = Plan(nc)
        sem = {"ld": Sem(h_ld), "cp": Sem(h_cp), "pe": Sem(h_pe), "dv": Sem(h_dv),
               "ac": Sem(h_ac), "st": Sem(h_st), "tr": Sem(h_tr), "xfree": Sem(h_xf),
               "kp": Sem(h_kp)}
        s_ld, s_cp, s_pe = sem["ld"], sem["cp"], sem["pe"]
        s_dv, s_ac, s_st, s_tr, s_kp = (sem["dv"], sem["ac"], sem["st"], sem["tr"],
                                        sem["kp"])

        consts = {"ones_bf": ones_bf, "ones_f32": ones_f32, "epst": epst}
        pb = {"st0": st0, "st1": st1, "bc0": bc0, "bc1": bc1}
        st = {"stats_free_ac": 0, "stats_free_dv": 0, "bc_free_dv": 0,
              "xf_base": 0, "slot_ctr": 0,
              "slot_reader_step": [0] * HG_SLOTS, "gelu_done": [0] * HG_SLOTS,
              "rail_gelu": [0, 0], "cp_bank_dv": [0, 0], "post_dv": 0,
              "post_st": 0}
        rows = {"m": mrow, "v": vrow}

        for dst, src in [(ones_bf, ones_bf_h.ap()), (ones_f32, ones_f32_h.ap()),
                         (epst, epst_h.ap()), (riota, riota_h.ap()),
                         (tmixs, tmix)]:
            P.emit("sync", lambda e, dst=dst, src=src: e.dma_start(out=dst[:], in_=src),
                   inc=(s_ld, 16))
        const_ld = s_ld.n

        with ExitStack() as ctx1:
            ec1 = ctx1.enter_context
            C8 = ec1(nc.sbuf_tensor("C8", [128, 32 * TG], BF16))
            acc = ec1(nc.sbuf_tensor("acc", [128, 8 * TG], F32))
            hg = ec1(nc.sbuf_tensor("hg", [128, HG_SLOTS * TG], BF16))
            cpw = ec1(nc.sbuf_tensor("cpw", [128, 2 * 4096], BF16))
            xf = ec1(nc.sbuf_tensor("xf", [128, 2 * 8 * TG], F32))
            xnb = ec1(nc.sbuf_tensor("xnb", [128, 8 * TG], BF16))
            rail0 = ec1(nc.psum_tensor("rail0", [128, TG], F32))
            rail1 = ec1(nc.psum_tensor("rail1", [128, TG], F32))
            cp0 = ec1(nc.psum_tensor("cp0", [128, TG], F32))
            cp1 = ec1(nc.psum_tensor("cp1", [128, TG], F32))
            pb.update({"rail0": rail0, "rail1": rail1, "cp0": cp0, "cp1": cp1})

            rmw = hg[:, 0:128]
            ubw = hg[:, 4096:4096 + 512]
            for m in range(32):
                w = [(s_pe, s_pe.n)] if m >= 1 else []
                P.emit("gpsimd", lambda e, m=m: e.dma_start(out=rmw, in_=rmbd[m]),
                       waits=w, inc=(s_cp, 16))
                P.emit("gpsimd", lambda e, m=m: e.dma_start(out=ubw, in_=ubd[m]),
                       inc=(s_cp, 16))
                cpn = s_cp.n
                bank = [rail0, rail1][m % 2]
                mmn = P.emit("tensor", lambda e, bank=bank: e.matmul(
                    bank[:], rmw, ubw, start=True, stop=True),
                    waits=[(s_cp, cpn), (s_dv, s_dv.n)], inc=(s_pe, 1))
                P.emit("vector", lambda e, m=m, bank=bank: e.tensor_copy(
                    out=C8[:, m * TG:m * TG + TG], in_=bank[:]),
                    waits=[(s_pe, mmn)], inc=(s_dv, 1))
            c8_dv = s_dv.n

            bufs = {"xf": xf, "xn": xnb, "hg": hg, "cpw": cpw, "acc": acc, "C8": C8,
                    "cpstream": cpstream_d,
                    "scr": {"xb": xb_s, "sq": sq_s, "d": d_s}, "rows": rows}

            def dump_debug():
                P.emit("gpsimd", lambda e: e.dma_start(out=dbgo[:, 0:TG],
                                                       in_=C8[:, 0:TG]), inc=(s_st, 16))
                P.emit("gpsimd", lambda e: e.dma_start(out=dbgo[:, TG:2*TG],
                                                       in_=xnb[:, 0:TG]), inc=(s_st, 16))
                P.emit("gpsimd", lambda e: e.dma_start(out=dbgo[:, 2*TG:3*TG],
                                                       in_=hg[:, 0:TG]), inc=(s_st, 16))
                P.emit("sync", lambda e: e.dma_start(out=dbgo[:, 3*TG:4*TG],
                                                     in_=acc[:, 0:TG]), inc=(s_st, 16))
                P.emit("sync", lambda e: e.dma_start(out=dbgo[0:1, 2*TG:3*TG],
                                                     in_=mrow[:, 0:TG]), inc=(s_st, 16))
                P.emit("sync", lambda e: e.dma_start(out=dbgo[1:2, 2*TG:3*TG],
                                                     in_=vrow[:, 0:TG]), inc=(s_st, 16))

            tg_list = [(0, 0, OVL)] + [(1 + k, OVL + k * TG, TG) for k in range(4)]

            def ap_x_load(c, col0, n):
                return x0T[128 * c:128 * c + 128, col0:col0 + n]

            def post_tg(t_idx, b, n, res_done):
                def xap1(c, b=b, n=n):
                    return xf[:, (b * 8 + c) * TG:(b * 8 + c) * TG + n]

                st["post_st"] = sem["st"].n
                if t_idx == 0:
                    dn = emit_ln_single(
                        P, sem, pb, st,
                        lambda c, b=b: xf[:, (b * 8 + c) * TG + OVL - 1:
                                          (b * 8 + c) * TG + OVL],
                        lambda c: H2[:, c * 2050:c * 2050 + 1],
                        bufs["scr"], rows, consts, pre_dv=max(res_done))
                    st["post_dv"] = dn
                else:
                    k = t_idx - 1
                    for c in range(8):
                        P.emit("sync", lambda e, c=c, k=k, b=b: e.dma_start(
                            out=x1o[128 * c:128 * c + 128, k * TG:k * TG + TG],
                            in_=xf[:, (b * 8 + c) * TG:(b * 8 + c) * TG + TG]),
                            waits=[(s_dv, res_done[c])], inc=(s_st, 16))
                    st["post_st"] = sem["st"].n
                    dn = emit_ln(
                        P, sem, pb, st, xap1,
                        lambda c, k=k: H2[:, c * 2050 + 1 + k * TG:
                                          c * 2050 + 1 + k * TG + TG],
                        TG, bufs["scr"], rows, consts,
                        pre_dv=max(res_done))
                    st["post_dv"] = dn

            emit_ffn(P, sem, pb, st, bufs, tg_list, ap_x_load, post_tg, consts)
            dump_debug()
            P.emit("sync", lambda e: e.dma_start(out=c8o, in_=C8[:]),
                   waits=[(s_dv, c8_dv)], inc=(s_st, 16))

            ph_dv, ph_ac, ph_kp, ph_st, ph_pe = s_dv.n, s_ac.n, s_kp.n, s_st.n, s_pe.n

        with ExitStack() as ctx2:
            ec2 = ctx2.enter_context
            Kt = ec2(nc.sbuf_tensor("Kt", [128, 8 * T_OWN], BF16))
            Vt = ec2(nc.sbuf_tensor("Vt", [128, 16 * D], BF16))
            wb8 = ec2(nc.sbuf_tensor("wb8", [128, 2 * 8 * D], BF16))
            dmix = ec2(nc.sbuf_tensor("dmix", [128, 8 * T_OWN], BF16))
            xm = ec2(nc.sbuf_tensor("xm", [128, 2 * 8 * TG], BF16))
            stg = ec2(nc.sbuf_tensor("stg", [128, 4 * TG], BF16))
            ktr = ec2(nc.sbuf_tensor("ktr", [128, 2 * D], BF16))
            kwb = ec2(nc.sbuf_tensor("kwb", [128, 2 * D], BF16))
            pp0 = ec2(nc.psum_tensor("pp0", [128, TG], F32))
            pp1 = ec2(nc.psum_tensor("pp1", [128, TG], F32))
            pc0 = ec2(nc.psum_tensor("pc0", [64, 64], F32))
            pc1 = ec2(nc.psum_tensor("pc1", [64, 64], F32))
            ldrow = ec2(nc.sbuf_tensor("ldrow", [1, 16], F32))
            ldb = ec2(nc.sbuf_tensor("ldb", [128, 16], F32))
            wkp = ec2(nc.sbuf_tensor("wkp", [128, 16], F32))
            wst = ec2(nc.sbuf_tensor("wst", [128, 8], F32))
            ldk = ec2(nc.sbuf_tensor("ldk", [128, 8], F32))
            Sst = ec2(nc.sbuf_tensor("Sst", [128, 8 * 64], F32))

            for eng in ("sync", "gpsimd", "tensor", "vector", "scalar"):
                P.wait(eng, s_dv, ph_dv)
                P.wait(eng, s_ac, ph_ac)
                P.wait(eng, s_kp, ph_kp)
                P.wait(eng, s_st, ph_st)

            P.emit("sync", lambda e: e.dma_start(out=ldrow[:], in_=td), inc=(s_ld, 16))
            ldn = s_ld.n
            P.emit("scalar", lambda e: e.activation(out=ldrow[:], in_=ldrow[:], func=AF.Exp),
                   waits=[(s_ld, ldn)], inc=(s_ac, 1))
            P.emit("vector", lambda e: e.tensor_scalar_mul(out=ldrow[:], in0=ldrow[:],
                                                           scalar1=-1.0),
                   waits=[(s_ac, s_ac.n)], inc=(s_dv, 1))
            P.emit("tensor", lambda e: e.matmul(bc0[:, 0:16], ones_f32[:], ldrow[:],
                                                start=True, stop=True),
                   waits=[(s_dv, s_dv.n), (s_dv, st["bc_free_dv"])], inc=(s_pe, 1))
            P.emit("vector", lambda e: e.tensor_copy(out=ldb[:], in_=bc0[:, 0:16]),
                   waits=[(s_pe, s_pe.n)], inc=(s_dv, 1))
            st["bc_free_dv"] = s_dv.n
            P.emit("vector", lambda e: e.tensor_scalar_mul(out=wkp[:], in0=ldb[:],
                                                           scalar1=riota[:]),
                   waits=([(s_dv, s_dv.n)] if SAFE_CHAIN else ()), inc=(s_dv, 1))
            P.emit("scalar", lambda e: e.activation(out=wkp[:], in_=wkp[:], func=AF.Exp),
                   waits=[(s_dv, s_dv.n)], inc=(s_ac, 1))
            for c in range(8):
                for hh in range(2):
                    sl = slice(64 * hh, 64 * hh + 64)
                    P.emit("vector", lambda e, c=c, hh=hh, sl=sl: e.tensor_copy(
                        out=ldk[sl, c:c + 1], in_=ldb[sl, 2 * c + hh:2 * c + hh + 1]),
                        waits=([(s_dv, s_dv.n)] if SAFE_CHAIN else ()), inc=(s_dv, 1))
            P.emit("scalar", lambda e: e.activation(out=wst[:], in_=ldk[:], func=AF.Exp,
                                                    scale=128.0),
                   waits=[(s_dv, s_dv.n), (s_ac, s_ac.n)], inc=(s_ac, 1))
            P.emit("vector", lambda e: e.memset(Sst[:], 0.0), inc=(s_dv, 1))

            for k in range(4):
                for c in range(8):
                    x1ap = H2[:, c * 2050 + 1 + k * TG: c * 2050 + 1 + k * TG + TG]
                    xxap = H2[:, c * 2050 + 0 + k * TG: c * 2050 + 0 + k * TG + TG]
                    dap = dmix[:, (c * 4 + k) * TG:(c * 4 + k) * TG + TG]
                    P.emit("vector", lambda e, dap=dap, x1ap=x1ap, xxap=xxap:
                           e.tensor_sub(out=dap, in0=x1ap, in1=xxap), inc=(s_dv, 1))
            dmix_done = s_dv.n

            def emit_mix(tgt_i, k, c, guard_pe):
                xxap = H2[:, c * 2050 + 0 + k * TG: c * 2050 + 0 + k * TG + TG]
                dap = dmix[:, (c * 4 + k) * TG:(c * 4 + k) * TG + TG]
                xb0 = (k % 2) * 8 * TG
                out = xm[:, xb0 + c * TG:xb0 + c * TG + TG]
                tm = tmixs[:, tgt_i * 8 + c:tgt_i * 8 + c + 1]
                return P.emit("vector", lambda e, out=out, dap=dap, tm=tm, xxap=xxap:
                              e.scalar_tensor_tensor(out=out, in0=dap, scalar=tm, in1=xxap,
                                                     op0=ALU.mult, op1=ALU.add),
                              waits=[(s_dv, dmix_done), (s_pe, guard_pe)], inc=(s_dv, 1))

            wld_map = {}

            def load_weights(tgt_i, wsrc, guard):
                wb0 = (tgt_i % 2) * 8 * D
                for ci in range(8):
                    P.emit("gpsimd", lambda e, ci=ci, wsrc=wsrc, wb0=wb0: e.dma_start(
                        out=wb8[:, wb0 + ci * D:wb0 + ci * D + D],
                        in_=wsrc[128 * ci:128 * ci + 128, :]),
                        waits=([(s_pe, guard)] if ci == 0 else ()), inc=(s_cp, 16))
                wld_map[tgt_i] = s_cp.n

            pp = [pp0, pp1]
            pp_free = [[], []]
            stg_free_st = [0, 0, 0, 0]

            load_weights(0, wrt, 0)
            load_weights(1, wkt, 0)
            xm_free = [0, 0]
            for tgt_i, (wsrc, resident, dram_out) in enumerate(
                    [(wrt, None, Ro), (wkt, Kt, Ko)]):
                wld = wld_map[tgt_i]
                wb0 = (tgt_i % 2) * 8 * D
                for k in range(4):
                    for c in range(8):
                        emit_mix(tgt_i, k, c, xm_free[k % 2])
                    mixn = s_dv.n
                    for o in range(8):
                        stp = tgt_i * 32 + k * 8 + o
                        bank = pp[stp % 2]
                        for ci in range(8):
                            lhsT = wb8[:, wb0 + ci * D + o * 128: wb0 + ci * D + o * 128 + 128]
                            rhs = xm[:, (k % 2) * 8 * TG + ci * TG:(k % 2) * 8 * TG + ci * TG + TG]
                            w0 = ([(s_cp, wld), (s_dv, mixn)] + pp_free[stp % 2]
                                  if ci == 0 else ())
                            P.emit("tensor", lambda e, bank=bank, lhsT=lhsT, rhs=rhs,
                                   ci=ci: e.matmul(bank[:], lhsT, rhs, start=(ci == 0),
                                                   stop=(ci == 7)),
                                   waits=w0, inc=((s_pe, 1) if ci == 7 else None))
                        pen = s_pe.n
                        if o == 7:
                            xm_free[k % 2] = pen
                        if resident is not None:
                            dst = resident[:, o * T_OWN + k * TG: o * T_OWN + k * TG + TG]
                            dn = P.emit("vector", lambda e, dst=dst, bank=bank:
                                        e.tensor_copy(out=dst, in_=bank[:]),
                                        waits=[(s_pe, pen)], inc=(s_dv, 1))
                            P.emit("sync", lambda e, dst=dst, o=o, k=k,
                                   dram_out=dram_out: e.dma_start(
                                       out=dram_out[128 * o:128 * o + 128,
                                                    k * TG:k * TG + TG], in_=dst),
                                   waits=[(s_dv, dn)], inc=(s_st, 16))
                            pp_free[stp % 2] = [(s_dv, dn)]
                        else:
                            sslot = stp % 4
                            dst = stg[:, sslot * TG:sslot * TG + TG]
                            dn = P.emit("vector", lambda e, dst=dst, bank=bank:
                                        e.tensor_copy(out=dst, in_=bank[:]),
                                        waits=[(s_pe, pen),
                                               (s_st, stg_free_st[sslot])],
                                        inc=(s_dv, 1))
                            stn = P.emit("sync", lambda e, dst=dst, o=o, k=k,
                                         dram_out=dram_out: e.dma_start(
                                             out=dram_out[128 * o:128 * o + 128,
                                                          k * TG:k * TG + TG], in_=dst),
                                         waits=[(s_dv, dn)], inc=(s_st, 16))
                            stg_free_st[sslot] = stn
                            pp_free[stp % 2] = [(s_dv, dn)]

            load_weights(2, wvt, wld_map[0] * 0 + s_pe.n)
            for tgt_i, (wsrc, resident, dram_out, act) in enumerate(
                    [(wvt, Vt, Vo, None), (wgt, None, Go, AF.Silu)], start=2):
                if tgt_i == 3:
                    load_weights(3, wgt, s_pe.n)
                wld = wld_map[tgt_i]
                wb0 = (tgt_i % 2) * 8 * D
                for k in range(4):
                    for c in range(8):
                        emit_mix(tgt_i, k, c, xm_free[k % 2])
                    mixn = s_dv.n
                    for tt in range(4):
                        t_glob = k * 4 + tt
                        for oh in range(2):
                            stp = tt * 2 + oh
                            bank = pp[stp % 2]
                            for ci in range(8):
                                lhsT = xm[:, (k % 2) * 8 * TG + ci * TG + tt * 128:
                                          (k % 2) * 8 * TG + ci * TG + tt * 128 + 128]
                                rhs = wb8[:, wb0 + ci * D + oh * 512: wb0 + ci * D + oh * 512 + 512]
                                w0 = ([(s_cp, wld), (s_dv, mixn)] + pp_free[stp % 2]
                                      if ci == 0 else ())
                                P.emit("tensor", lambda e, bank=bank, lhsT=lhsT, rhs=rhs,
                                       ci=ci: e.matmul(bank[:], lhsT, rhs,
                                                       start=(ci == 0), stop=(ci == 7)),
                                       waits=w0, inc=((s_pe, 1) if ci == 7 else None))
                            pen = s_pe.n
                            if tt == 3 and oh == 1:
                                xm_free[k % 2] = pen
                            if resident is not None:
                                dst = resident[:, t_glob * D + oh * 512:
                                               t_glob * D + oh * 512 + 512]
                                dn = P.emit("vector", lambda e, dst=dst, bank=bank:
                                            e.tensor_copy(out=dst, in_=bank[:]),
                                            waits=[(s_pe, pen)], inc=(s_dv, 1))
                                P.emit("sync", lambda e, dst=dst, t_glob=t_glob, oh=oh,
                                       dram_out=dram_out: e.dma_start(
                                           out=dram_out[t_glob * 128:t_glob * 128 + 128,
                                                        oh * 512:oh * 512 + 512],
                                           in_=dst),
                                       waits=[(s_dv, dn)], inc=(s_st, 16))
                                pp_free[stp % 2] = [(s_dv, dn)]
                            else:
                                sslot = stp % 4
                                dst = stg[:, sslot * TG:sslot * TG + TG]
                                an = P.emit("scalar", lambda e, dst=dst, bank=bank:
                                            e.activation(out=dst, in_=bank[:],
                                                         func=AF.Silu),
                                            waits=[(s_pe, pen),
                                                   (s_st, stg_free_st[sslot])],
                                            inc=(s_ac, 1))
                                stn = P.emit("sync", lambda e, dst=dst, t_glob=t_glob,
                                             oh=oh, dram_out=dram_out: e.dma_start(
                                                 out=dram_out[
                                                     t_glob * 128:t_glob * 128 + 128,
                                                     oh * 512:oh * 512 + 512], in_=dst),
                                             waits=[(s_ac, an)], inc=(s_st, 16))
                                stg_free_st[sslot] = stn
                                pp_free[stp % 2] = [(s_ac, an)]

            kv_done_dv = s_dv.n

            kw_ready = []
            chunk_pe = []
            chunk_st = []
            pc_free = [[], []]
            for n_ch in range(NCHUNK):
                nb = n_ch % 2
                for c in range(8):
                    src = Kt[:, c * T_OWN + n_ch * 128: c * T_OWN + n_ch * 128 + 128]
                    dst = ktr[:, nb * D + c * 128: nb * D + c * 128 + 128]
                    w = ([(s_dv, kw_ready[n_ch - 2])] if n_ch >= 2
                         else [(s_dv, kv_done_dv)])
                    P.emit("sync", lambda e, dst=dst, src=src: e.dma_start(
                        out=dst, in_=src, transpose=True), waits=w, inc=(s_tr, 16))
                trn = s_tr.n
                for c in range(8):
                    for hh in range(2):
                        src = ktr[:, nb * D + c * 128 + 64 * hh:
                                  nb * D + c * 128 + 64 * hh + 64]
                        dst = kwb[:, nb * D + c * 128 + 64 * hh:
                                  nb * D + c * 128 + 64 * hh + 64]
                        sc = wkp[:, 2 * c + hh:2 * c + hh + 1]
                        w2 = [(s_tr, trn)]
                        if n_ch >= 2:
                            w2 += [(s_pe, chunk_pe[n_ch - 2]), (s_st, chunk_st[n_ch - 2])]
                        P.emit("vector", lambda e, dst=dst, src=src, sc=sc:
                               e.tensor_scalar_mul(out=dst, in0=src, scalar1=sc),
                               waits=w2, inc=(s_dv, 1))
                kw_ready.append(s_dv.n)
                P.emit("sync", lambda e, nb=nb, n_ch=n_ch: e.dma_start(
                    out=Kwo[n_ch * 128:n_ch * 128 + 128, :],
                    in_=kwb[:, nb * D:nb * D + D]),
                    waits=[(s_dv, kw_ready[-1])], inc=(s_st, 16))
                chunk_st.append(s_st.n)
                for c in range(8):
                    pens = []
                    for hh, bank in ((0, pc0), (1, pc1)):
                        lhsT = kwb[:, nb * D + c * 128 + 64 * hh:
                                   nb * D + c * 128 + 64 * hh + 64]
                        rhs = Vt[:, n_ch * D + (2 * c + hh) * 64:
                                 n_ch * D + (2 * c + hh) * 64 + 64]
                        pn = P.emit("tensor", lambda e, bank=bank, lhsT=lhsT, rhs=rhs:
                                    e.matmul(bank[:], lhsT, rhs, start=True, stop=True),
                                    waits=[(s_dv, kw_ready[-1])] + pc_free[hh],
                                    inc=(s_pe, 1))
                        pens.append(pn)
                    for hh, bank in ((0, pc0), (1, pc1)):
                        ssl = Sst[64 * hh:64 * hh + 64, c * 64:c * 64 + 64]
                        wsl = wst[64 * hh:64 * hh + 64, c:c + 1]
                        dn = P.emit("vector", lambda e, ssl=ssl, wsl=wsl, bank=bank:
                                    e.scalar_tensor_tensor(out=ssl, in0=ssl, scalar=wsl,
                                                           in1=bank[:], op0=ALU.mult,
                                                           op1=ALU.add),
                                    waits=[(s_pe, pens[hh]), (s_dv, s_dv.n)],
                                    inc=(s_dv, 1))
                        pc_free[hh] = [(s_dv, dn)]
                chunk_pe.append(s_pe.n)

            bdone = s_dv.n
            for c in range(8):
                P.emit("sync", lambda e, c=c: e.dma_start(
                    out=Bo[c], in_=Sst[:, c * 64:c * 64 + 64]),
                    waits=[(s_dv, bdone)], inc=(s_st, 16))

            P.wait("gpsimd", s_st, s_st.n)
            P.wait("gpsimd", s_tr, s_tr.n)

        with nc.Block() as block:
            P.build(block)

    return nc



def build_l2():
    nc = bass.Bass("TRN2", target_bir_lowering=False)

    x1i = nc.dram_tensor("x1i", [D, T_OWN], F32, kind="ExternalInput").ap()
    Ri = nc.dram_tensor("Ri", [D, T_OWN], BF16, kind="ExternalInput").ap()
    Ki = nc.dram_tensor("Ki", [D, T_OWN], BF16, kind="ExternalInput").ap()
    Gi = nc.dram_tensor("Gi", [T_OWN, D], BF16, kind="ExternalInput").ap()
    Vi = nc.dram_tensor("Vi", [T_OWN, D], BF16, kind="ExternalInput").ap()
    Kwi = nc.dram_tensor("Kwi", [T_OWN, D], BF16, kind="ExternalInput").ap()
    Ball = nc.dram_tensor("Ball", [128, 4096], F32, kind="ExternalInput").ap()
    cpstream_d = nc.dram_tensor("cpstream", [128, 131072], F32, kind="ExternalInput").ap()
    c8i = nc.dram_tensor("c8i", [128, 32 * TG], BF16, kind="ExternalInput").ap()
    wot = nc.dram_tensor("wot", [D, D], F32, kind="ExternalInput").ap()
    td = nc.dram_tensor("td", [1, 16], F32, kind="ExternalInput").ap()
    tf = nc.dram_tensor("tf", [1, 16], F32, kind="ExternalInput").ap()
    lnw_d = nc.dram_tensor("lnw", [1, D], F32, kind="ExternalInput").ap()
    lnb_d = nc.dram_tensor("lnb", [1, D], F32, kind="ExternalInput").ap()
    expo_d = nc.dram_tensor("expo", [1, 8], F32, kind="ExternalInput").ap()
    mask8_d = nc.dram_tensor("mask8", [1, 8], F32, kind="ExternalInput").ap()

    x3o = nc.dram_tensor("x3o", [D, T_OWN], F32, kind="ExternalOutput").ap()
    x2o = nc.dram_tensor("x2o", [D, T_OWN], F32, kind="ExternalOutput").ap()
    dbg2 = nc.dram_tensor("dbg2", [128, 4 * TG], F32, kind="ExternalOutput").ap()

    x2d = nc.dram_tensor("x2d", [D, T_OWN], F32, kind="Internal")

    ones_bf_h = nc.inline_tensor(np.ones((128, 1), dtype=BF), "c_ones_bf")
    ones_f32_h = nc.inline_tensor(np.ones((1, 128), dtype=np.float32), "c_ones_f32")
    epst_h = nc.inline_tensor(np.full((128, 1), EPS, dtype=np.float32), "c_eps")
    ii = np.arange(128, dtype=np.float32)
    d1t_h = nc.inline_tensor(np.maximum(ii[None, :] - ii[:, None] - 1.0, 0.0), "c_d1t")
    maskt_h = nc.inline_tensor((ii[None, :] > ii[:, None]).astype(np.float32), "c_maskt")
    iotaf_h = nc.inline_tensor(np.broadcast_to(ii[None, :], (128, 128)).copy(), "c_iotaf")
    eye_h = nc.inline_tensor(np.eye(128, dtype=np.float32), "c_eye")

    with ExitStack() as ctx:
        ec = ctx.enter_context
        ones_bf = ec(nc.sbuf_tensor("ones_bf", [128, 1], BF16))
        ones_f32 = ec(nc.sbuf_tensor("ones_f32", [1, 128], F32))
        epst = ec(nc.sbuf_tensor("epst", [128, 1], F32))
        xb_s = ec(nc.sbuf_tensor("xb_s", [128, TG], BF16))
        sq_s = ec(nc.sbuf_tensor("sq_s", [128, TG], BF16))
        d_s = ec(nc.sbuf_tensor("d_s", [128, TG], F32))
        mrow = ec(nc.sbuf_tensor("mrow", [1, TG], F32))
        vrow = ec(nc.sbuf_tensor("vrow", [1, TG], F32))
        h_ld = ec(nc.semaphore("s_ld"))
        h_cp = ec(nc.semaphore("s_cp"))
        h_pe = ec(nc.semaphore("s_pe"))
        h_dv = ec(nc.semaphore("s_dv"))
        h_ac = ec(nc.semaphore("s_ac"))
        h_st = ec(nc.semaphore("s_st"))
        h_tr = ec(nc.semaphore("s_tr"))
        h_xf = ec(nc.semaphore("s_xf"))
        h_kp = ec(nc.semaphore("s_kp"))

        P = Plan(nc)
        sem = {"ld": Sem(h_ld), "cp": Sem(h_cp), "pe": Sem(h_pe), "dv": Sem(h_dv),
               "ac": Sem(h_ac), "st": Sem(h_st), "tr": Sem(h_tr), "xfree": Sem(h_xf),
               "kp": Sem(h_kp)}
        s_ld, s_cp, s_pe = sem["ld"], sem["cp"], sem["pe"]
        s_dv, s_ac, s_st, s_tr, s_kp = (sem["dv"], sem["ac"], sem["st"], sem["tr"],
                                        sem["kp"])
        consts = {"ones_bf": ones_bf, "ones_f32": ones_f32, "epst": epst}
        pb = {}
        st = {"stats_free_ac": 0, "stats_free_dv": 0, "bc_free_dv": 0,
              "xf_base": 0, "slot_ctr": 0,
              "slot_reader_step": [0] * HG_SLOTS, "gelu_done": [0] * HG_SLOTS,
              "rail_gelu": [0, 0], "cp_bank_dv": [0, 0], "post_dv": 0,
              "post_st": 0, "prev_g1": 0, "ygtm_tr": 0}
        rows = {"m": mrow, "v": vrow}

        for dst, src in [(ones_bf, ones_bf_h.ap()), (ones_f32, ones_f32_h.ap()),
                         (epst, epst_h.ap())]:
            P.emit("sync", lambda e, dst=dst, src=src: e.dma_start(out=dst[:], in_=src),
                   inc=(s_ld, 16))

        with ExitStack() as ctxa:
            eca = ctxa.enter_context
            Rt = eca(nc.sbuf_tensor("Rt", [128, 8 * T_OWN], BF16))
            Kt = eca(nc.sbuf_tensor("Kt", [128, 8 * T_OWN], BF16))
            Vt = eca(nc.sbuf_tensor("Vt", [128, 16 * D], BF16))
            gt2 = eca(nc.sbuf_tensor("gt2", [128, 2 * D], BF16))
            kwt = eca(nc.sbuf_tensor("kwt", [128, 2 * D], BF16))
            wmt = eca(nc.sbuf_tensor("wmt", [128, 16 * 128], BF16))
            wbt = eca(nc.sbuf_tensor("wbt", [128, 8 * 128], BF16))
            lnwb = eca(nc.sbuf_tensor("lnwb", [128, D], BF16))
            lnbb = eca(nc.sbuf_tensor("lnbb", [128, D], BF16))
            lnwrow = eca(nc.sbuf_tensor("lnwrow", [1, D], F32))
            lnbrow = eca(nc.sbuf_tensor("lnbrow", [1, D], F32))
            d1t = eca(nc.sbuf_tensor("d1t", [128, 128], F32))
            maskt = eca(nc.sbuf_tensor("maskt", [128, 128], F32))
            iotaf = eca(nc.sbuf_tensor("iotaf", [128, 128], F32))
            eyet = eca(nc.sbuf_tensor("eyet", [128, 128], F32))
            ldrow = eca(nc.sbuf_tensor("ldrow", [1, 16], F32))
            ufrow = eca(nc.sbuf_tensor("ufrow", [1, 16], F32))
            ldb = eca(nc.sbuf_tensor("ldb", [128, 16], F32))
            ufb = eca(nc.sbuf_tensor("ufb", [128, 16], F32))
            ldk = eca(nc.sbuf_tensor("ldk", [128, 8], F32))
            wst = eca(nc.sbuf_tensor("wst", [128, 8], F32))
            Sst = eca(nc.sbuf_tensor("Sst", [128, 8 * 64], F32))
            Sbf = eca(nc.sbuf_tensor("Sbf", [128, 8 * 64], BF16))
            exb = eca(nc.sbuf_tensor("exb", [128, 8], F32))
            mkb = eca(nc.sbuf_tensor("mkb", [128, 8], F32))
            cofb = eca(nc.sbuf_tensor("cofb", [128, 8 * 8], F32))
            wo8 = eca(nc.sbuf_tensor("wo8", [128, 8 * D], BF16))
            attwb = eca(nc.sbuf_tensor("attwb", [128, 2 * 2048], BF16))
            rwbb = eca(nc.sbuf_tensor("rwbb", [128, 2 * D], BF16))
            xof = eca(nc.sbuf_tensor("xof", [128, 2 * D], F32))
            gnscr = eca(nc.sbuf_tensor("gnscr", [128, D], F32))
            ygtm2 = eca(nc.sbuf_tensor("ygtm2", [128, D], BF16))
            srow = eca(nc.sbuf_tensor("srow", [128, 96], F32))
            ygtm = eca(nc.sbuf_tensor("ygtm", [128, D], BF16))
            ygfm = eca(nc.sbuf_tensor("ygfm", [128, 8 * 512], BF16))
            x1t = eca(nc.sbuf_tensor("x1t", [128, 8 * 512], F32))
            x2t = eca(nc.sbuf_tensor("x2t", [128, 512], F32))
            setup_ctx = ExitStack()
            bc0 = setup_ctx.enter_context(nc.psum_tensor("bcA0", [128, TG], F32))
            bc1 = setup_ctx.enter_context(nc.psum_tensor("bcA1", [128, TG], F32))

            P.emit("sync", lambda e: e.dma_start(out=d1t[:], in_=d1t_h.ap()), inc=(s_ld, 16))
            P.emit("sync", lambda e: e.dma_start(out=maskt[:], in_=maskt_h.ap()), inc=(s_ld, 16))
            P.emit("sync", lambda e: e.dma_start(out=iotaf[:], in_=iotaf_h.ap()), inc=(s_ld, 16))
            P.emit("sync", lambda e: e.dma_start(out=eyet[:], in_=eye_h.ap()), inc=(s_ld, 16))
            P.emit("sync", lambda e: e.dma_start(out=ldrow[:], in_=td), inc=(s_ld, 16))
            P.emit("sync", lambda e: e.dma_start(out=ufrow[:], in_=tf), inc=(s_ld, 16))
            P.emit("sync", lambda e: e.dma_start(out=lnwrow[:], in_=lnw_d), inc=(s_ld, 16))
            P.emit("sync", lambda e: e.dma_start(out=lnbrow[:], in_=lnb_d), inc=(s_ld, 16))
            P.emit("sync", lambda e: e.dma_start(out=exb[0:1, :], in_=expo_d), inc=(s_ld, 16))
            P.emit("sync", lambda e: e.dma_start(out=mkb[0:1, :], in_=mask8_d), inc=(s_ld, 16))
            for c in range(8):
                P.emit("sync", lambda e, c=c: e.dma_start(
                    out=Rt[:, c * T_OWN:(c + 1) * T_OWN], in_=Ri[128 * c:128 * c + 128, :]),
                    inc=(s_ld, 16))
                P.emit("sync", lambda e, c=c: e.dma_start(
                    out=Kt[:, c * T_OWN:(c + 1) * T_OWN], in_=Ki[128 * c:128 * c + 128, :]),
                    inc=(s_ld, 16))
            for tt in range(16):
                P.emit("sync", lambda e, tt=tt: e.dma_start(
                    out=Vt[:, tt * D:(tt + 1) * D], in_=Vi[tt * 128:tt * 128 + 128, :]),
                    inc=(s_ld, 16))
            for c in range(8):
                P.emit("gpsimd", lambda e, c=c: e.dma_start(
                    out=wo8[:, c * D:(c + 1) * D], in_=wot[128 * c:128 * c + 128, :]),
                    inc=(s_cp, 16))
            const_ld = s_ld.n

            P.emit("scalar", lambda e: e.activation(out=ldrow[:], in_=ldrow[:], func=AF.Exp),
                   waits=[(s_ld, const_ld)], inc=(s_ac, 1))
            P.emit("vector", lambda e: e.tensor_scalar_mul(out=ldrow[:], in0=ldrow[:],
                                                           scalar1=-1.0),
                   waits=[(s_ac, s_ac.n)], inc=(s_dv, 1))
            P.emit("tensor", lambda e: e.matmul(bc0[:, 0:16], ones_f32[:], ldrow[:],
                                                start=True, stop=True),
                   waits=[(s_dv, s_dv.n)], inc=(s_pe, 1))
            P.emit("tensor", lambda e: e.matmul(bc1[:, 0:16], ones_f32[:], ufrow[:],
                                                start=True, stop=True), inc=(s_pe, 1))
            P.emit("vector", lambda e: e.tensor_copy(out=ldb[:], in_=bc0[:, 0:16]),
                   waits=[(s_pe, s_pe.n)], inc=(s_dv, 1))
            P.emit("vector", lambda e: e.tensor_copy(out=ufb[:], in_=bc1[:, 0:16]),
                   waits=([(s_dv, s_dv.n)] if SAFE_CHAIN else ()), inc=(s_dv, 1))
            st["bc_free_dv"] = s_dv.n
            for c in range(8):
                for hh in range(2):
                    sl = slice(64 * hh, 64 * hh + 64)
                    P.emit("vector", lambda e, c=c, hh=hh, sl=sl: e.tensor_copy(
                        out=ldk[sl, c:c + 1], in_=ldb[sl, 2 * c + hh:2 * c + hh + 1]),
                        waits=([(s_dv, s_dv.n)] if SAFE_CHAIN else ()), inc=(s_dv, 1))
            P.emit("scalar", lambda e: e.activation(out=wst[:], in_=ldk[:], func=AF.Exp,
                                                    scale=128.0),
                   waits=[(s_dv, s_dv.n), (s_ac, s_ac.n)], inc=(s_ac, 1))
            for h in range(16):
                dst = wmt[:, h * 128:(h + 1) * 128]
                P.emit("vector", lambda e, dst=dst, h=h: e.tensor_scalar_mul(
                    out=dst, in0=d1t[:], scalar1=ldb[:, h:h + 1]),
                    waits=[(s_dv, s_dv.n)], inc=(s_dv, 1))
                P.emit("scalar", lambda e, dst=dst: e.activation(out=dst, in_=dst,
                                                                 func=AF.Exp),
                       waits=[(s_dv, s_dv.n), (s_ac, s_ac.n)], inc=(s_ac, 1))
                P.emit("vector", lambda e, dst=dst: e.tensor_mul(out=dst, in0=dst,
                                                                 in1=maskt[:]),
                       waits=[(s_ac, s_ac.n)], inc=(s_dv, 1))
                P.emit("vector", lambda e, h=h: e.tensor_scalar_mul(
                    out=gnscr[:, 0:128], in0=eyet[:], scalar1=ufb[:, h:h + 1]),
                    waits=([(s_dv, s_dv.n)] if SAFE_CHAIN else ()), inc=(s_dv, 1))
                P.emit("vector", lambda e, dst=dst: e.tensor_add(out=dst, in0=dst,
                                                                 in1=gnscr[:, 0:128]),
                       waits=[(s_dv, s_dv.n)], inc=(s_dv, 1))
            for c in range(8):
                dst = wbt[:, c * 128:(c + 1) * 128]
                P.emit("vector", lambda e, dst=dst, c=c: e.tensor_scalar_mul(
                    out=dst, in0=iotaf[:], scalar1=ldk[:, c:c + 1]),
                    waits=[(s_dv, s_dv.n)], inc=(s_dv, 1))
                P.emit("scalar", lambda e, dst=dst: e.activation(out=dst, in_=dst,
                                                                 func=AF.Exp),
                       waits=[(s_dv, s_dv.n), (s_ac, s_ac.n)], inc=(s_ac, 1))
            wmt_done_ac = s_ac.n
            for half in range(2):
                P.emit("tensor", lambda e, half=half: e.matmul(
                    bc0[:, 0:512], ones_f32[:], lnwrow[0:1, half * 512:half * 512 + 512],
                    start=True, stop=True),
                    waits=[(s_ld, const_ld), (s_dv, st["bc_free_dv"])], inc=(s_pe, 1))
                P.emit("tensor", lambda e, half=half: e.matmul(
                    bc1[:, 0:512], ones_f32[:], lnbrow[0:1, half * 512:half * 512 + 512],
                    start=True, stop=True), inc=(s_pe, 1))
                pen = s_pe.n
                P.emit("vector", lambda e, half=half: e.tensor_copy(
                    out=lnwb[:, half * 512:half * 512 + 512], in_=bc0[:, 0:512]),
                    waits=[(s_pe, pen)], inc=(s_dv, 1))
                P.emit("vector", lambda e, half=half: e.tensor_copy(
                    out=lnbb[:, half * 512:half * 512 + 512], in_=bc1[:, 0:512]),
                    waits=([(s_dv, s_dv.n)] if SAFE_CHAIN else ()), inc=(s_dv, 1))
                st["bc_free_dv"] = s_dv.n
            P.emit("tensor", lambda e: e.matmul(bc0[:, 0:8], ones_f32[:], exb[0:1, :],
                                                start=True, stop=True),
                   waits=[(s_dv, st["bc_free_dv"])], inc=(s_pe, 1))
            P.emit("tensor", lambda e: e.matmul(bc1[:, 0:8], ones_f32[:], mkb[0:1, :],
                                                start=True, stop=True), inc=(s_pe, 1))
            pen = s_pe.n
            P.emit("vector", lambda e: e.tensor_copy(out=exb[:], in_=bc0[:, 0:8]),
                   waits=[(s_pe, pen)], inc=(s_dv, 1))
            P.emit("vector", lambda e: e.tensor_copy(out=mkb[:], in_=bc1[:, 0:8]),
                   waits=([(s_dv, s_dv.n)] if SAFE_CHAIN else ()), inc=(s_dv, 1))
            st["bc_free_dv"] = s_dv.n
            for c in range(8):
                dst = cofb[:, c * 8:(c + 1) * 8]
                P.emit("vector", lambda e, dst=dst, c=c: e.tensor_scalar_mul(
                    out=dst, in0=exb[:], scalar1=ldk[:, c:c + 1]),
                    waits=[(s_dv, s_dv.n)], inc=(s_dv, 1))
                P.emit("scalar", lambda e, dst=dst: e.activation(out=dst, in_=dst,
                                                                 func=AF.Exp),
                       waits=[(s_dv, s_dv.n), (s_ac, s_ac.n)], inc=(s_ac, 1))
                P.emit("vector", lambda e, dst=dst: e.tensor_mul(out=dst, in0=dst,
                                                                 in1=mkb[:]),
                       waits=[(s_ac, s_ac.n)], inc=(s_dv, 1))
            bts = x1t
            P.emit("sync", lambda e: e.dma_start(out=bts[:], in_=Ball), inc=(s_ld, 16))
            ldn = s_ld.n
            for c in range(8):
                for r in range(8):
                    bsl = bts[:, (r * 8 + c) * 64:(r * 8 + c) * 64 + 64]
                    ssl = Sst[:, c * 64:c * 64 + 64]
                    cf = cofb[:, c * 8 + r:c * 8 + r + 1]
                    if r == 0:
                        P.emit("vector", lambda e, ssl=ssl, cf=cf, bsl=bsl:
                               e.tensor_scalar_mul(out=ssl, in0=bsl, scalar1=cf),
                               waits=[(s_ld, ldn), (s_dv, s_dv.n)], inc=(s_dv, 1))
                    else:
                        P.emit("vector", lambda e, ssl=ssl, cf=cf, bsl=bsl:
                               e.scalar_tensor_tensor(out=ssl, in0=bsl, scalar=cf,
                                                      in1=ssl, op0=ALU.mult, op1=ALU.add),
                               waits=[(s_ld, ldn), (s_dv, s_dv.n)], inc=(s_dv, 1))
            for c in range(8):
                P.emit("vector", lambda e, c=c: e.tensor_copy(
                    out=Sbf[:, c * 64:c * 64 + 64], in_=Sst[:, c * 64:c * 64 + 64]),
                    waits=[(s_dv, s_dv.n)], inc=(s_dv, 1))
            sin_done = s_dv.n
            setup_ctx.close()
            psA = eca(nc.psum_tensor("psA", [128, 128], F32))
            psB = eca(nc.psum_tensor("psB", [128, 128], F32))
            psoA = eca(nc.psum_tensor("psoA", [128, 64], F32))
            psoB = eca(nc.psum_tensor("psoB", [128, 64], F32))
            pc0 = eca(nc.psum_tensor("pc0", [64, 64], F32))
            pc1 = eca(nc.psum_tensor("pc1", [64, 64], F32))
            ppo0 = eca(nc.psum_tensor("ppo0", [128, TG], F32))
            ppo1 = eca(nc.psum_tensor("ppo1", [128, TG], F32))

            x1_loaded = {}

            def load_x1(kk):
                dvn = s_dv.n
                for c in range(8):
                    P.emit("sync", lambda e, c=c, kk=kk: e.dma_start(
                        out=x1t[:, c * 512:c * 512 + 512],
                        in_=x1i[128 * c:128 * c + 128, kk * 512:kk * 512 + 512]),
                        waits=[(s_dv, dvn if kk > 0 else sin_done)], inc=(s_ld, 16))
                x1_loaded[kk] = s_ld.n

            res_ld = s_ld.n
            kw_loaded = {}

            def load_kw(n_ch):
                P.emit("sync", lambda e, n_ch=n_ch: e.dma_start(
                    out=kwt[:, (n_ch % 2) * D:(n_ch % 2) * D + D],
                    in_=Kwi[n_ch * 128:n_ch * 128 + 128, :]),
                    waits=[(s_pe, st.get("kw_pe_%d" % ((n_ch - 2) % 2), 0))],
                    inc=(s_ld, 16))
                kw_loaded[n_ch] = s_ld.n

            g_loaded = {}

            def load_g(n_ch):
                P.emit("sync", lambda e, n_ch=n_ch: e.dma_start(
                    out=gt2[:, (n_ch % 2) * D:(n_ch % 2) * D + D],
                    in_=Gi[n_ch * 128:n_ch * 128 + 128, :]),
                    waits=[(s_dv, st.get("g_dv_%d" % ((n_ch - 2) % 2), 0))],
                    inc=(s_ld, 16))
                g_loaded[n_ch] = s_ld.n

            load_kw(0)
            load_kw(1)
            load_g(0)
            load_g(1)
            load_x1(0)

            pc_free = [[], []]
            pso_free = [[], []]
            psAB_free = [[], []]
            attw_done = {}
            yg_done = []

            def emit_scores(n_ch):
                ab0 = (n_ch % 2) * 2048
                rb0 = (n_ch % 2) * D
                for c in range(8):
                    kcol = c * T_OWN + n_ch * 128
                    w0 = [(s_ld, res_ld), (s_ac, wmt_done_ac), (s_dv, sin_done)]
                    mmA = P.emit("tensor", lambda e, kcol=kcol: e.matmul(
                        psA[:], Kt[0:64, kcol:kcol + 128], Rt[0:64, kcol:kcol + 128],
                        start=True, stop=True, tile_position=(0, 0)),
                        waits=w0 + psAB_free[0], inc=(s_pe, 1))
                    mmB = P.emit("tensor", lambda e, kcol=kcol: e.matmul(
                        psB[:], Kt[64:128, kcol:kcol + 128], Rt[64:128, kcol:kcol + 128],
                        start=True, stop=True, tile_position=(64, 0)),
                        waits=psAB_free[1], inc=(s_pe, 1))
                    a0 = P.emit("vector", lambda e, c=c, ab0=ab0: e.tensor_mul(
                        out=attwb[:, ab0 + c * 256:ab0 + c * 256 + 128], in0=psA[:],
                        in1=wmt[:, (2 * c) * 128:(2 * c) * 128 + 128]),
                        waits=[(s_pe, mmA), (s_dv, s_dv.n)], inc=(s_dv, 1))
                    a1 = P.emit("vector", lambda e, c=c, ab0=ab0: e.tensor_mul(
                        out=attwb[:, ab0 + c * 256 + 128:ab0 + c * 256 + 256], in0=psB[:],
                        in1=wmt[:, (2 * c + 1) * 128:(2 * c + 1) * 128 + 128]),
                        waits=[(s_pe, mmB)], inc=(s_dv, 1))
                    psAB_free[0] = [(s_dv, a0)]
                    psAB_free[1] = [(s_dv, a1)]
                    P.emit("vector", lambda e, kcol=kcol, c=c, rb0=rb0: e.tensor_mul(
                        out=rwbb[:, rb0 + c * 128:rb0 + c * 128 + 128],
                        in0=Rt[:, kcol:kcol + 128], in1=wbt[:, c * 128:c * 128 + 128]),
                        waits=([(s_dv, s_dv.n)] if SAFE_CHAIN else ()), inc=(s_dv, 1))
                attw_done[n_ch] = s_dv.n

            emit_scores(0)
            for n_ch in range(L2_CHUNKS):
                kk = n_ch // 4
                ygfm_col = (n_ch % 4) * 128
                if n_ch + 1 < L2_CHUNKS and L2_STAGE >= 2:
                    emit_scores(n_ch + 1)
                if L2_STAGE < 2:
                    continue
                ab0 = (n_ch % 2) * 2048
                rb0 = (n_ch % 2) * D
                adn = attw_done[n_ch]
                for c in range(8):
                    vcol = n_ch * D + (2 * c) * 64
                    aw_a = attwb[:, ab0 + c * 256:ab0 + c * 256 + 128]
                    aw_b = attwb[:, ab0 + c * 256 + 128:ab0 + c * 256 + 256]
                    rw = rwbb[:, rb0 + c * 128:rb0 + c * 128 + 128]
                    P.emit("tensor", lambda e, vcol=vcol, aw_a=aw_a: e.matmul(
                        psoA[:], aw_a, Vt[:, vcol:vcol + 64],
                        start=True, stop=False),
                        waits=[(s_dv, adn)] + pso_free[0], inc=None)
                    P.emit("tensor", lambda e, c=c, rw=rw: e.matmul(
                        psoA[:], rw[0:64, :], Sbf[0:64, c * 64:c * 64 + 64],
                        start=False, stop=True, tile_position=(0, 0)), inc=None)
                    P.emit("tensor", lambda e, vcol=vcol, aw_b=aw_b: e.matmul(
                        psoB[:], aw_b, Vt[:, vcol + 64:vcol + 128],
                        start=True, stop=False),
                        waits=pso_free[1], inc=None)
                    P.emit("tensor", lambda e, c=c, rw=rw: e.matmul(
                        psoB[:], rw[64:128, :], Sbf[64:128, c * 64:c * 64 + 64],
                        start=False, stop=True, tile_position=(64, 0)), inc=(s_pe, 1))
                    opn = s_pe.n
                    if L2_STAGE >= 3:
                        kwcol = (n_ch % 2) * D + c * 128
                        pn0 = P.emit("tensor", lambda e, kwcol=kwcol, vcol=vcol: e.matmul(
                            pc0[:], kwt[:, kwcol:kwcol + 64], Vt[:, vcol:vcol + 64],
                            start=True, stop=True),
                            waits=[(s_ld, kw_loaded[n_ch])] + pc_free[0],
                            inc=(s_pe, 1))
                        pn1 = P.emit("tensor", lambda e, kwcol=kwcol, vcol=vcol: e.matmul(
                            pc1[:], kwt[:, kwcol + 64:kwcol + 128], Vt[:, vcol + 64:vcol + 128],
                            start=True, stop=True),
                            waits=pc_free[1], inc=(s_pe, 1))
                        for hh, bank, pn in ((0, pc0, pn0), (1, pc1, pn1)):
                            ssl = Sst[64 * hh:64 * hh + 64, c * 64:c * 64 + 64]
                            wsl = wst[64 * hh:64 * hh + 64, c:c + 1]
                            dn = P.emit("vector", lambda e, ssl=ssl, wsl=wsl, bank=bank:
                                        e.scalar_tensor_tensor(out=ssl, in0=ssl, scalar=wsl,
                                                               in1=bank[:], op0=ALU.mult,
                                                               op1=ALU.add),
                                        waits=[(s_pe, pn), (s_dv, s_dv.n)], inc=(s_dv, 1))
                            pc_free[hh] = [(s_dv, dn)]
                        P.emit("vector", lambda e, c=c: e.tensor_copy(
                            out=Sbf[:, c * 64:c * 64 + 64], in_=Sst[:, c * 64:c * 64 + 64]),
                            waits=([(s_dv, s_dv.n)] if SAFE_CHAIN else ()), inc=(s_dv, 1))
                    if L2_STAGE < 4:
                        pso_free[0] = []
                        pso_free[1] = []
                        continue
                    xo0 = (n_ch % 2) * D
                    e0 = P.emit("scalar", lambda e, c=c, xo0=xo0: e.activation(
                        out=xof[:, xo0 + c * 128:xo0 + c * 128 + 64], in_=psoA[:],
                        func=AF.Copy, scale=0.125),
                        waits=[(s_pe, opn),
                               (s_dv, st.get("gn_dv_%d" % (n_ch % 2), 0))],
                        inc=(s_ac, 1))
                    ev = P.emit("scalar", lambda e, c=c, xo0=xo0: e.activation(
                        out=xof[:, xo0 + c * 128 + 64:xo0 + c * 128 + 128], in_=psoB[:],
                        func=AF.Copy, scale=0.125),
                        inc=(s_ac, 1))
                    pso_free[0] = [(s_ac, e0)]
                    pso_free[1] = [(s_ac, ev)]
                if L2_STAGE < 4:
                    st["kw_pe_%d" % (n_ch % 2)] = s_pe.n
                    st["g_dv_%d" % (n_ch % 2)] = s_dv.n
                    continue
                st["kw_pe_%d" % (n_ch % 2)] = s_pe.n
                evn = s_ac.n
                xo0 = (n_ch % 2) * D
                x3d = xof[:, xo0:xo0 + D].rearrange("p (h k) -> p h k", h=16)
                s3d = gnscr[:, 0:D].rearrange("p (h k) -> p h k", h=16)
                ygt = [ygtm, ygtm2][n_ch % 2]
                sr0 = (n_ch % 2) * 16
                P.emit("vector", lambda e, x3d=x3d, sr0=sr0: e.reduce_sum(
                    out=srow[:, sr0:sr0 + 16], in_=x3d, axis=mybir.AxisListType.X),
                    waits=[(s_ac, evn), (s_dv, s_dv.n)], inc=(s_dv, 1))
                dvn = s_dv.n
                P.emit("scalar", lambda e, sr0=sr0: e.activation(
                    out=srow[:, 32 + sr0:48 + sr0], in_=srow[:, sr0:sr0 + 16],
                    func=AF.Copy, scale=1.0 / 64),
                    waits=[(s_dv, dvn)], inc=(s_ac, 1))
                acn = s_ac.n
                P.emit("vector", lambda e, x3d=x3d, sr0=sr0: e.tensor_tensor(
                    out=x3d, in0=x3d,
                    in1=srow[:, 32 + sr0:48 + sr0].unsqueeze(2).broadcast_to([128, 16, 64]),
                    op=ALU.subtract),
                    waits=[(s_ac, acn)], inc=(s_dv, 1))
                dvn = s_dv.n
                P.emit("vector", lambda e, xo0=xo0: e.tensor_mul(
                    out=gnscr[:, 0:D], in0=xof[:, xo0:xo0 + D],
                    in1=xof[:, xo0:xo0 + D]),
                    waits=([(s_dv, dvn)] if SAFE_CHAIN else ()), inc=(s_dv, 1))
                dvn = s_dv.n
                P.emit("vector", lambda e, s3d=s3d, sr0=sr0: e.reduce_sum(
                    out=srow[:, 64 + sr0:80 + sr0], in_=s3d, axis=mybir.AxisListType.X),
                    waits=([(s_dv, dvn)] if SAFE_CHAIN else ()), inc=(s_dv, 1))
                dvn = s_dv.n
                P.emit("scalar", lambda e, sr0=sr0: e.activation(
                    out=srow[:, 64 + sr0:80 + sr0], in_=srow[:, 64 + sr0:80 + sr0],
                    func=AF.Ln, scale=1.0 / 64, bias=epst[:]),
                    waits=[(s_dv, dvn)], inc=(s_ac, 1))
                acn = s_ac.n
                P.emit("scalar", lambda e, sr0=sr0: e.activation(
                    out=srow[:, 64 + sr0:80 + sr0], in_=srow[:, 64 + sr0:80 + sr0],
                    func=AF.Exp, scale=-0.5),
                    waits=([(s_ac, acn)] if SAFE_CHAIN else ()), inc=(s_ac, 1))
                acn = s_ac.n
                P.emit("vector", lambda e, x3d=x3d, sr0=sr0: e.tensor_tensor(
                    out=x3d, in0=x3d,
                    in1=srow[:, 64 + sr0:80 + sr0].unsqueeze(2).broadcast_to([128, 16, 64]),
                    op=ALU.mult),
                    waits=[(s_ac, acn), (s_dv, s_dv.n)], inc=(s_dv, 1))
                dvn = s_dv.n
                P.emit("vector", lambda e, xo0=xo0: e.tensor_mul(
                    out=xof[:, xo0:xo0 + D], in0=xof[:, xo0:xo0 + D], in1=lnwb[:]),
                       waits=([(s_dv, dvn)] if SAFE_CHAIN else ()), inc=(s_dv, 1))
                dvn = s_dv.n
                P.emit("vector", lambda e, xo0=xo0: e.tensor_add(
                    out=xof[:, xo0:xo0 + D], in0=xof[:, xo0:xo0 + D], in1=lnbb[:]),
                       waits=([(s_dv, dvn)] if SAFE_CHAIN else ()), inc=(s_dv, 1))
                dvn = s_dv.n
                g1 = P.emit("vector", lambda e, n_ch=n_ch, xo0=xo0, ygt=ygt: e.tensor_mul(
                    out=ygt[:], in0=xof[:, xo0:xo0 + D],
                    in1=gt2[:, (n_ch % 2) * D:(n_ch % 2) * D + D]),
                    waits=[(s_dv, dvn), (s_tr, st.get("ygtm_tr_%d" % (n_ch % 2), 0)),
                           (s_ld, g_loaded[n_ch])], inc=(s_dv, 1))
                st["gn_dv_%d" % (n_ch % 2)] = s_dv.n
                st["g_dv_%d" % (n_ch % 2)] = s_dv.n
                if L2_STAGE < 5:
                    continue
                if n_ch + 2 < NCHUNK:
                    load_kw(n_ch + 2)
                    load_g(n_ch + 2)
                ygn = g1
                for c in range(8):
                    P.emit("sync", lambda e, c=c, ygfm_col=ygfm_col, ygt=ygt: e.dma_start(
                        out=ygfm[:, c * 512 + ygfm_col:c * 512 + ygfm_col + 128],
                        in_=ygt[:, c * 128:c * 128 + 128], transpose=True),
                        waits=[(s_dv, ygn), (s_pe, st.get("wo_pe", 0))], inc=(s_tr, 16))
                yg_done.append(s_tr.n)
                st["ygtm_tr_%d" % (n_ch % 2)] = s_tr.n
                if n_ch % 4 == 3:
                    trn = s_tr.n
                    for o in range(8):
                        bank = [ppo0, ppo1][o % 2]
                        for ci in range(8):
                            w0 = ([(s_tr, trn), (s_cp, s_cp.n), (s_ld, x1_loaded[kk])]
                                  + (st.get("ppo_free_%d" % (o % 2)) or [])
                                  if ci == 0 else ())
                            P.emit("tensor", lambda e, bank=bank, ci=ci, o=o: e.matmul(
                                bank[:], wo8[:, ci * D + o * 128:ci * D + o * 128 + 128],
                                ygfm[:, ci * 512:ci * 512 + 512],
                                start=(ci == 0), stop=(ci == 7)),
                                waits=w0, inc=((s_pe, 1) if ci == 7 else None))
                        pen = s_pe.n
                        dn = P.emit("vector", lambda e, bank=bank, o=o: e.tensor_add(
                            out=x2t[:], in0=x1t[:, o * 512:o * 512 + 512], in1=bank[:]),
                            waits=[(s_pe, pen), (s_st, st.get("x2t_st", 0)),
                                   (s_dv, s_dv.n)],
                            inc=(s_dv, 1))
                        st["ppo_free_%d" % (o % 2)] = [(s_dv, dn)]
                        stn = P.emit("sync", lambda e, o=o, kk=kk: e.dma_start(
                            out=x2d.ap()[128 * o:128 * o + 128, kk * 512:kk * 512 + 512],
                            in_=x2t[:]),
                            waits=[(s_dv, dn)], inc=(s_st, 16))
                        st["x2t_st"] = stn
                    st["wo_pe"] = s_pe.n
                    if kk + 1 < 4:
                        load_x1(kk + 1)
            att_dv, att_ac, att_pe, att_st, att_tr = (s_dv.n, s_ac.n, s_pe.n, s_st.n,
                                                      s_tr.n)

        with ExitStack() as ctxb:
          if L2_FFN:
            ecb = ctxb.enter_context
            C8 = ecb(nc.sbuf_tensor("C8", [128, 32 * TG], BF16))
            acc = ecb(nc.sbuf_tensor("acc", [128, 8 * TG], F32))
            hg = ecb(nc.sbuf_tensor("hg", [128, HG_SLOTS * TG], BF16))
            cpw = ecb(nc.sbuf_tensor("cpw", [128, 2 * 4096], BF16))
            xf = ecb(nc.sbuf_tensor("xf", [128, 2 * 8 * TG], F32))
            xnb = ecb(nc.sbuf_tensor("xnb", [128, 8 * TG], BF16))
            st0 = ecb(nc.psum_tensor("st0", [1, TG], F32))
            st1 = ecb(nc.psum_tensor("st1", [1, TG], F32))
            bc0 = ecb(nc.psum_tensor("bc0", [128, TG], F32))
            bc1 = ecb(nc.psum_tensor("bc1", [128, TG], F32))
            rail0 = ecb(nc.psum_tensor("rail0", [128, TG], F32))
            rail1 = ecb(nc.psum_tensor("rail1", [128, TG], F32))
            cp0 = ecb(nc.psum_tensor("cp0", [128, TG], F32))
            cp1 = ecb(nc.psum_tensor("cp1", [128, TG], F32))
            pb.update({"st0": st0, "st1": st1, "bc0": bc0, "bc1": bc1,
                       "rail0": rail0, "rail1": rail1, "cp0": cp0, "cp1": cp1})

            for eng in ("sync", "gpsimd", "tensor", "vector", "scalar"):
                P.wait(eng, s_dv, att_dv)
                P.wait(eng, s_ac, att_ac)
                P.wait(eng, s_pe, att_pe)
                P.wait(eng, s_st, att_st)
                P.wait(eng, s_tr, att_tr)

            cpn0 = P.emit("sync", lambda e: e.dma_start(out=C8[:], in_=c8i),
                          inc=(s_cp, 16))
            P.wait("tensor", s_cp, cpn0)

            bufs = {"xf": xf, "xn": xnb, "hg": hg, "cpw": cpw, "acc": acc, "C8": C8,
                    "cpstream": cpstream_d,
                    "scr": {"xb": xb_s, "sq": sq_s, "d": d_s}, "rows": rows}

            tg_list2 = [(k, k * TG, TG) for k in range(4)]

            def ap_x_load2(c, col0, n):
                if L2_CHUNKS == 0:
                    return x1i[128 * c:128 * c + 128, col0:col0 + n]
                return x2d.ap()[128 * c:128 * c + 128, col0:col0 + n]

            def post_tg2(t_idx, b, n, res_done):
                k = t_idx
                for c in range(8):
                    P.emit("sync", lambda e, c=c, k=k, b=b: e.dma_start(
                        out=x3o[128 * c:128 * c + 128, k * TG:k * TG + TG],
                        in_=xf[:, (b * 8 + c) * TG:(b * 8 + c) * TG + TG]),
                        waits=[(s_dv, res_done[c])], inc=(s_st, 16))
                st["post_st"] = s_st.n
                st["post_dv"] = max(res_done)

            emit_ffn(P, sem, pb, st, bufs, tg_list2, ap_x_load2, post_tg2, consts)

            P.wait("gpsimd", s_dv, s_dv.n)
            P.wait("gpsimd", s_ac, s_ac.n)
            P.wait("gpsimd", s_kp, s_kp.n)

            P.wait("gpsimd", s_st, s_st.n)
            P.wait("gpsimd", s_tr, s_tr.n)

        with nc.Block() as block:
            P.build(block)

    return nc


def run_l2(inputs, l1_results):
    shared = _prep_shared(inputs)
    Ball = np.stack([np.asarray(l1_results[r]["Bo"], np.float32) for r in range(NCORES)])
    Ball = np.ascontiguousarray(Ball.transpose(2, 0, 1, 3)).reshape(128, 4096)

    in_maps = []
    for i in range(NCORES):
        r = l1_results[i]
        expo = np.zeros((1, 8), np.float32)
        mask8 = np.zeros((1, 8), np.float32)
        for j in range(i):
            expo[0, j] = 2048.0 * (i - 1 - j)
            mask8[0, j] = 1.0
        m = {
            "x1i": np.ascontiguousarray(r["x1o"]),
            "Ri": np.ascontiguousarray(r["Ro"]),
            "Ki": np.ascontiguousarray(r["Ko"]),
            "Gi": np.ascontiguousarray(r["Go"]),
            "Vi": np.ascontiguousarray(r["Vo"]),
            "Kwi": np.ascontiguousarray(r["Kwo"]),
            "Ball": Ball,
            "cpstream": shared["cpstream"],
            "c8i": np.ascontiguousarray(r["c8o"]),
            "wot": np.ascontiguousarray(np.asarray(inputs["L0_attn_W_o"], np.float32).T),
            "td": shared["td"],
            "tf": np.asarray(inputs["L0_attn_time_faaaa"], np.float32).reshape(1, 16),
            "lnw": np.asarray(inputs["L0_attn_ln_w"], np.float32).reshape(1, D),
            "lnb": np.asarray(inputs["L0_attn_ln_b"], np.float32).reshape(1, D),
            "expo": expo, "mask8": mask8,
        }
        in_maps.append(m)

    nc = build_l2()
    res = run_bass_kernel_spmd(nc, in_maps, core_ids=list(range(NCORES)))
    return res.results


def _prep_shared(inputs):
    W = np.asarray(inputs["L0_ffn_w_cproj"], dtype=np.float32)
    A = W.T.reshape(4, 32, 128, 8, 128).transpose(2, 0, 3, 1, 4)
    cpstream = np.ascontiguousarray(A).reshape(128, 131072)

    rm = np.asarray(inputs["L0_ffn_rail_maker"], dtype=np.float32)
    u = np.asarray(inputs["L0_ffn_rail_up_proj"], dtype=np.float32)
    rmbd = np.zeros((32, 128, 128), dtype=np.float32)
    ubd = np.zeros((32, 128, 512), dtype=np.float32)
    for h in range(4):
        for c in range(8):
            m = h * 8 + c
            for p in range(8):
                r = 8 * c + p
                rmbd[m, 16 * p:16 * p + 16, 16 * p:16 * p + 16] = rm[h, r].T
                ubd[m, 16 * p:16 * p + 16, 64 * p:64 * p + 64] = u[h, r].T

    tm = np.stack([np.asarray(inputs[f"L0_attn_time_mix_{t}"], np.float32).reshape(-1)
                   for t in ("r", "k", "v", "g")])
    tmix = np.ascontiguousarray(tm.reshape(4, 8, 128).transpose(2, 0, 1)).reshape(128, 32)

    return {
        "cpstream": cpstream, "rmbd": rmbd, "ubd": ubd,
        "wrt": np.ascontiguousarray(np.asarray(inputs["L0_attn_W_r"], np.float32).T),
        "wkt": np.ascontiguousarray(np.asarray(inputs["L0_attn_W_k"], np.float32).T),
        "wvt": np.ascontiguousarray(np.asarray(inputs["L0_attn_W_v"], np.float32).T),
        "wgt": np.ascontiguousarray(np.asarray(inputs["L0_attn_W_g"], np.float32).T),
        "tmix": tmix,
        "td": np.asarray(inputs["L0_attn_time_decay"], np.float32).reshape(1, 16),
    }


def run_l1(inputs):
    shared = _prep_shared(inputs)
    x = np.asarray(inputs["L0_attn_xq"], np.float32)[0]
    xT = np.ascontiguousarray(x.T)
    xTp = np.concatenate([np.zeros((D, OVL), np.float32), xT], axis=1)

    in_maps = []
    for i in range(NCORES):
        m = dict(shared)
        m["x0T"] = np.ascontiguousarray(xTp[:, i * T_OWN: i * T_OWN + T_IN])
        in_maps.append(m)

    nc = build_l1()
    res = run_bass_kernel_spmd(nc, in_maps, core_ids=list(range(NCORES)))
    return res.results


def kernel(**inputs):
    l1 = run_l1(inputs)
    l2 = run_l2(inputs, l1)
    x3 = np.concatenate([r["x3o"].T for r in l2], axis=0)
    return x3.reshape(1, 16384, 1024).astype(np.float32)
